# version 02
# baseline (speedup 1.0000x reference)
import numpy as np
import ml_dtypes
from contextlib import ExitStack
from dataclasses import replace as ap_replace
from types import SimpleNamespace as NS

import concourse.bass as bass
import concourse.mybir as mybir
from concourse.bass_utils import run_bass_kernel_spmd

F32 = mybir.dt.float32
BF16 = mybir.dt.bfloat16
AF = mybir.ActivationFunctionType
ALU = mybir.AluOpType
AX = mybir.AxisListType
BF = ml_dtypes.bfloat16

R = 8
B = 2
T = 8192
TC = T // R
C = 1024
H = 16
HD = 64
NT = TC // 128
EPS = 1e-5

ENGINES = ("sync", "tensor", "vector", "scalar", "gpsimd")


def blocks_for_core(c):
    return [8 * o + c for o in range(8)]


class _Sem:
    __slots__ = ("h", "n")

    def __init__(self, h):
        self.h = h
        self.n = 0


class Ctx:

    def __init__(self, prog, eng, q):
        self.prog = prog
        self.eng = eng
        self.q = q
        self.sems = {}

    def sem(self, name):
        name = self.prog.prefix + name
        if name not in self.prog.handles:
            self.prog.handles[name] = self.prog.es.enter_context(self.prog.nc.semaphore(name))
        s = _Sem(self.prog.handles[name])
        self.sems[name] = s
        return s

    def do(self, eng, fn, wait=(), inc=()):
        if self.eng == eng:
            for s, v in wait:
                self.q.wait_ge(s.h, v)
            ins = fn(self.q)
            for s, k in inc:
                ins.then_inc(s.h, k)
        for s, k in inc:
            s.n += k

    def wait(self, eng, sem, value):
        if self.eng == eng:
            self.q.wait_ge(sem.h, value)

    def wait_h(self, eng, handle, value):
        if self.eng == eng:
            self.q.wait_ge(handle, value)

    def on(self, eng, fn, inc=()):
        if self.eng == eng:
            ins = fn(self.q)
            for h, k in inc:
                ins.then_inc(h, k)


class Prog:
    def __init__(self, nc, es):
        self.nc = nc
        self.es = es
        self.handles = {}
        self.prefix = ""

    def build(self, phase_fn):
        with self.nc.Block() as block:
            for name in ENGINES:
                def body(q, name=name):
                    phase_fn(Ctx(self, name, q))
                getattr(block, name)(body)



def emit_L1(nc, prog, *, x_sl, x_halo, w_qkv, wk_map, bk_map, id_bf,
            xnT_sp, kf_sp, v_sp, states, hx_scr, hooks=None, stage="full"):
    hooks = hooks or {}
    es = ExitStack()
    with es:
        T_ = lambda nm, shp, dt: es.enter_context(nc.sbuf_tensor(prog.prefix + nm, shp, dt))
        P_ = lambda nm, shp, dt: es.enter_context(nc.psum_tensor(prog.prefix + nm, shp, dt))
        sid = T_("sid", [128, 128], BF16)
        wkr = T_("wkr", [128, 8 * 1024], BF16)
        wkm = T_("wkm", [64, 64], BF16)
        wkmT2 = T_("wkmT2", [128, 128], BF16)
        bkb = T_("bkb", [1, 64], BF16)
        bk8 = T_("bk8", [1, 512], BF16)
        ones1 = T_("ones1", [1, 128], BF16)
        onescol = T_("onescol", [128, 1], BF16)
        wcomb = T_("wcomb", [128, 8 * 1024], BF16)
        wvT = T_("wvT", [128, 8 * 1024], BF16)
        sx = [T_(f"sx{i}", [128, 1024], F32) for i in range(2)]
        sxc = T_("sxc", [128, 1024], F32)
        srow = T_("srow", [128, 1], F32)
        svar = T_("svar", [128, 1], F32)
        sln = T_("sln", [128, 1], F32)
        srstd = T_("srstd", [128, 1], F32)
        sxn = T_("sxn", [128, 1024], BF16)
        xnT = T_("xnT", [128, 16 * 1025], BF16)
        sep = T_("sep", [128, 512], F32)
        sen = T_("sen", [128, 512], F32)
        ssum = T_("ssum", [128, 8], F32)
        srec = T_("srec", [128, 8], F32)
        ssum2 = T_("ssum2", [128, 8], F32)
        srec2 = T_("srec2", [128, 8], F32)
        kf = T_("kf", [128, 16 * 2048], BF16)
        vsb = T_("vsb", [128, 16 * 1024], BF16)
        sst = T_("sst", [128, 32 * 65], F32)
        shr = T_("shr", [128, 1], F32)
        shv = T_("shv", [128, 1], F32)
        shl = T_("shl", [128, 1], F32)
        shs = T_("shs", [128, 1], F32)
        shn = T_("shn", [128, 1024], BF16)
        seps = T_("seps", [128, 1], F32)

        ptrT = P_("ptrT", [128, 1024], BF16)
        pwb = P_("pwb", [128, 1024], BF16)
        pp = [P_(f"pp{i}", [128, 512], F32) for i in range(2)]
        pv = [P_(f"pv{i}", [128, 512], F32) for i in range(2)]
        pwk = pwb[0:64, 0:64]
        pw = pp[0]
        pst = pv[0][:, 0:65]

        xnT3 = xnT[:].rearrange("p (g c) -> p g c", c=1025)
        kf3 = kf[:].rearrange("p (g c) -> p g c", c=2048)
        v3 = vsb[:].rearrange("p (g c) -> p g c", c=1024)
        wkr3 = wkr[:].rearrange("p (r c) -> p r c", c=1024)
        wvT3 = wvT[:].rearrange("p (r c) -> p r c", c=1024)

        def phase(ctx: Ctx):
            hooks.get("head", lambda c: None)(ctx)
            ld = ctx.sem("ld")
            tp = ctx.sem("tp")
            vp = ctx.sem("vp")
            sp = ctx.sem("sp")
            stc = ctx.sem("stc")

            ctx.do("sync", lambda q: q.dma_start(out=sid[:], in_=id_bf), inc=[(ld, 16)])
            ctx.do("gpsimd", lambda q: q.dma_start(out=wkr[:].rearrange("p (r c) -> p r c", c=1024),
                                                   in_=w_qkv[1024:2048, :].rearrange("(r p) c -> p r c", p=128)),
                   inc=[(ld, 16)])
            ctx.do("gpsimd", lambda q: q.dma_start(out=wkm[:], in_=wk_map), inc=[(ld, 16)])
            ctx.do("gpsimd", lambda q: q.dma_start(out=bkb[:], in_=bk_map), inc=[(ld, 16)])
            ld_w = ld.n
            ctx.do("vector", lambda q: q.memset(ones1[:], 1.0))
            ctx.do("vector", lambda q: q.memset(onescol[:], 1.0))
            ctx.do("vector", lambda q: q.memset(seps[:], EPS), inc=[(vp, 1)])
            ctx.do("vector", lambda q: q.tensor_copy(bk8[:].rearrange("p (s e) -> p s e", e=64),
                                                     bkb[:].rearrange("p (s e) -> p s e", e=64).broadcast_to([1, 8, 64])),
                   wait=[(ld, ld_w)], inc=[(vp, 1)])
            v_bk8 = vp.n
            ctx.do("vector", lambda q: q.memset(wkmT2[:], 0.0), inc=[(vp, 1)])
            v_z = vp.n
            ctx.do("tensor", lambda q: q.transpose(pwk, wkm[:], sid[0:64, 0:64]),
                   wait=[(ld, ld_w)], inc=[(tp, 1)])
            ctx.do("vector", lambda q: q.tensor_copy(wkmT2[0:64, 0:64], pwk), wait=[(tp, tp.n)], inc=[(vp, 1)])
            ctx.do("gpsimd", lambda q: q.dma_start(out=wkmT2[64:128, 64:128], in_=wkmT2[0:64, 0:64]),
                   wait=[(vp, vp.n)], inc=[(ld, 16)])
            v_wkmT = vp.n
            ld_wkmT = ld.n
            if stage == "prep1":
                vfin = vp.n
                ctx.do("sync", lambda q: q.dma_start(out=xnT_sp, in_=xnT[:]), wait=[(vp, vfin)], inc=[(stc, 16)])
                ctx.do("sync", lambda q: q.dma_start(out=kf_sp, in_=kf[:]), inc=[(stc, 16)])
                ctx.do("sync", lambda q: q.dma_start(out=v_sp, in_=vsb[:]), inc=[(stc, 16)])
                ctx.do("sync", lambda q: q.dma_start(out=states, in_=sst[:]), inc=[(stc, 16)])
                ctx.wait("gpsimd", stc, 16 * 4)
                return
            for ct in range(8):
                for hg in range(2):
                    def mmgrp(q, ct=ct, hg=hg):
                        for j in range(4):
                            hp = hg * 4 + j
                            lt = wkr3[:, hp, ct * 128:(ct + 1) * 128]
                            ins = q.matmul(pw[:, j * 128:(j + 1) * 128], lt, wkmT2[:],
                                           start=True, stop=True)
                        return ins
                    ctx.do("tensor", mmgrp,
                           wait=[(vp, v_wkmT), (ld, ld_wkmT)] if (ct == 0 and hg == 0) else [(vp, vp.n)],
                           inc=[(tp, 1)])
                    ctx.do("vector", lambda q, ct=ct, hg=hg: q.tensor_copy(
                        wcomb[:, ct * 1024 + hg * 512: ct * 1024 + (hg + 1) * 512], pw[:]),
                        wait=[(tp, tp.n)], inc=[(vp, 1)])
            if stage == "prep2":
                vfin = vp.n
                ctx.do("sync", lambda q: q.dma_start(out=xnT_sp, in_=xnT[:]), wait=[(vp, vfin)], inc=[(stc, 16)])
                ctx.do("sync", lambda q: q.dma_start(out=kf_sp, in_=kf[:]), inc=[(stc, 16)])
                ctx.do("sync", lambda q: q.dma_start(out=v_sp, in_=vsb[:]), inc=[(stc, 16)])
                ctx.do("sync", lambda q: q.dma_start(out=states, in_=sst[:]), inc=[(stc, 16)])
                ctx.wait("gpsimd", stc, 16 * 4)
                return
            ctx.do("gpsimd", lambda q: q.dma_start(out=wkr[:].rearrange("p (r c) -> p r c", c=1024),
                                                   in_=w_qkv[2048:3072, :].rearrange("(r p) c -> p r c", p=128)),
                   wait=[(tp, tp.n)], inc=[(ld, 16)])
            ld_vr = ld.n
            for dvt in range(8):
                def trgrp(q, dvt=dvt):
                    for ct in range(8):
                        ins = q.transpose(pwb[:, ct * 128:(ct + 1) * 128], wkr3[:, dvt, ct * 128:(ct + 1) * 128], sid[:])
                    return ins
                ctx.do("tensor", trgrp, wait=[(vp, vp.n), (ld, ld_vr)], inc=[(tp, 1)])
                ctx.do("vector", lambda q, dvt=dvt: q.tensor_copy(
                    wvT3[:, :, dvt * 128:(dvt + 1) * 128],
                    pwb[:].rearrange("p (a c) -> p a c", c=128)),
                    wait=[(tp, tp.n)], inc=[(vp, 1)])

            if stage == "prep":
                vfin = vp.n
                ctx.do("sync", lambda q: q.dma_start(out=xnT_sp, in_=xnT[:]), wait=[(vp, vfin)], inc=[(stc, 16)])
                ctx.do("sync", lambda q: q.dma_start(out=kf_sp, in_=kf[:]), inc=[(stc, 16)])
                ctx.do("sync", lambda q: q.dma_start(out=v_sp, in_=vsb[:]), inc=[(stc, 16)])
                ctx.do("sync", lambda q: q.dma_start(out=states, in_=sst[:]), inc=[(stc, 16)])
                ctx.wait("gpsimd", stc, 16 * 4)
                return
            ctx.do("vector", lambda q: q.memset(sx[0][:], 0.0), inc=[(vp, 1)])
            v_hz = vp.n
            for b in range(B):
                ctx.do("sync", lambda q, b=b: q.dma_start(out=sx[0][64 * b:64 * b + 1, :], in_=x_halo[b:b + 1, :]),
                       wait=[(vp, v_hz)], inc=[(ld, 16)])
            ld_halo = ld.n
            ctx.do("vector", lambda q: q.reduce_sum(shr[:], sx[0][:], axis=AX.X), wait=[(ld, ld_halo)], inc=[(vp, 1)])
            ctx.do("vector", lambda q: q.tensor_scalar(shr[:], shr[:], -1.0 / C, None, op0=ALU.mult),
                   wait=[(vp, vp.n)], inc=[(vp, 1)])
            ctx.do("scalar", lambda q: q.activation(sxc[:], sx[0][:], AF.Identity, bias=shr[:, 0:1]),
                   wait=[(vp, vp.n)], inc=[(sp, 1)])
            ctx.do("scalar", lambda q: q.activation(sx[1][:], sxc[:], AF.Square, accum_out=shv[:, 0:1]),
                   wait=[(sp, sp.n)], inc=[(sp, 1)])
            ctx.do("scalar", lambda q: q.activation(shl[:], shv[:], AF.Ln, scale=1.0 / C, bias=seps[:, 0:1]),
                   wait=[(sp, sp.n)], inc=[(sp, 1)])
            ctx.do("scalar", lambda q: q.activation(shs[:], shl[:], AF.Exp, scale=-0.5),
                   wait=[(sp, sp.n)], inc=[(sp, 1)])
            ctx.do("vector", lambda q: q.tensor_scalar(shn[:], sxc[:], shs[:, 0:1], None, op0=ALU.mult),
                   wait=[(sp, sp.n)], inc=[(vp, 1)])
            v_halo = vp.n
            s_halo = sp.n
            for b in range(B):
                ctx.do("gpsimd", lambda q, b=b: q.dma_start(out=hx_scr[b:b + 1, :], in_=shn[64 * b:64 * b + 1, :]),
                       wait=[(vp, v_halo)], inc=[(ld, 16)])
            ld_hxw = ld.n

            if stage == "halo":
                vfin = vp.n
                ctx.do("sync", lambda q: q.dma_start(out=xnT_sp, in_=xnT[:]), wait=[(vp, vfin)], inc=[(stc, 16)])
                ctx.do("sync", lambda q: q.dma_start(out=kf_sp, in_=kf[:]), inc=[(stc, 16)])
                ctx.do("sync", lambda q: q.dma_start(out=v_sp, in_=vsb[:]), inc=[(stc, 16)])
                ctx.do("sync", lambda q: q.dma_start(out=states, in_=sst[:]), inc=[(stc, 16)])
                ctx.wait("gpsimd", stc, 16 * 4)
                return
            xc_done = {}
            trx_done = {}
            scale_done = [0]
            tr_copy_done = {}
            pp_free = {}
            pv_free = {}
            for i in range(16):
                b, tt = divmod(i, NT)
                g = b * 8 + tt
                w = [(sp, xc_done[i - 2])] if i - 2 in xc_done else [(sp, s_halo)]
                ctx.do("sync", lambda q, b=b, tt=tt, i=i: q.dma_start(
                    out=sx[i % 2][:], in_=x_sl[b, tt * 128:(tt + 1) * 128, :]), wait=w, inc=[(ld, 16)])
                ld_x = ld.n
                ctx.do("vector", lambda q, i=i: q.reduce_sum(srow[:], sx[i % 2][:], axis=AX.X),
                       wait=[(ld, ld_x)] + ([(sp, xc_done[i - 1])] if i - 1 in xc_done else [(sp, s_halo)]),
                       inc=[(vp, 1)])
                ctx.do("vector", lambda q: q.tensor_scalar(srow[:], srow[:], -1.0 / C, None, op0=ALU.mult),
                       wait=[(vp, vp.n)], inc=[(vp, 1)])
                ctx.do("scalar", lambda q, i=i: q.activation(sxc[:], sx[i % 2][:], AF.Identity, bias=srow[:, 0:1]),
                       wait=[(vp, vp.n), (sp, sp.n)], inc=[(sp, 1)])
                ctx.do("scalar", lambda q, i=i: q.activation(sx[i % 2][:], sxc[:], AF.Square, accum_out=svar[:, 0:1]),
                       wait=[(sp, sp.n)] + ([(tp, trx_done[i - 1])] if i - 1 in trx_done else []),
                       inc=[(sp, 1)])
                xc_done[i] = sp.n
                ctx.do("scalar", lambda q: q.activation(sln[:], svar[:], AF.Ln, scale=1.0 / C, bias=seps[:, 0:1]),
                       wait=[(sp, sp.n)], inc=[(sp, 1)])
                ctx.do("scalar", lambda q: q.activation(srstd[:], sln[:], AF.Exp, scale=-0.5),
                       wait=[(sp, sp.n)], inc=[(sp, 1)])
                ctx.do("vector", lambda q: q.tensor_scalar(sxn[:], sxc[:], srstd[:, 0:1], None, op0=ALU.mult),
                       wait=[(sp, sp.n), (vp, vp.n)], inc=[(vp, 1)])
                v_xn = vp.n
                def trx(q):
                    for ct in range(8):
                        ins = q.transpose(ptrT[:, ct * 128:(ct + 1) * 128], sxn[:, ct * 128:(ct + 1) * 128], sid[:])
                    return ins
                wtr = [(vp, v_xn)] + ([(vp, tr_copy_done[i - 1])] if i - 1 in tr_copy_done else [])
                ctx.do("tensor", trx, wait=wtr, inc=[(tp, 1)])
                trx_done[i] = tp.n
                ctx.do("vector", lambda q, b=b, tt=tt: q.tensor_copy(
                    xnT3[:, b * 8:(b + 1) * 8, 1 + tt * 128:1 + (tt + 1) * 128],
                    ptrT[:].rearrange("p (a c) -> p a c", c=128)),
                    wait=[(tp, tp.n)], inc=[(vp, 1)])
                tr_copy_done[i] = vp.n
                for hg in range(2):
                    def pmm2(q, g=g, hg=hg):
                        q.matmul(pp[hg][:], ones1[:], bk8[:], start=True, stop=False)
                        for ct in range(8):
                            lhs = xnT3[:, (g // 8) * 8 + ct, 1 + (g % 8) * 128:1 + (g % 8 + 1) * 128]
                            rhs = wcomb[:, ct * 1024 + hg * 512: ct * 1024 + (hg + 1) * 512]
                            ins = q.matmul(pp[hg][:], lhs, rhs, start=False, stop=(ct == 7))
                        return ins
                    wpp = [(vp, tr_copy_done[i]), (vp, v_bk8)]
                    if (i - 1, hg) in pp_free:
                        wpp.append((sp, pp_free[(i - 1, hg)]))
                    ctx.do("tensor", pmm2, wait=wpp, inc=[(tp, 1)])
                    t_pp = tp.n
                    ctx.do("scalar", lambda q, hg=hg: q.activation(sep[:], pp[hg][:], AF.Exp),
                           wait=[(sp, sp.n), (tp, t_pp), (vp, scale_done[0])], inc=[(sp, 1)])
                    ctx.do("scalar", lambda q, hg=hg: q.activation(sen[:], pp[hg][:], AF.Exp, scale=-1.0),
                           wait=[(sp, sp.n)], inc=[(sp, 1)])
                    pp_free[(i, hg)] = sp.n
                    s_exp = sp.n
                    sep3 = sep[:].rearrange("p (s e) -> p s e", e=64)
                    sen3 = sen[:].rearrange("p (s e) -> p s e", e=64)
                    ssum3 = ssum[:].rearrange("p (s e) -> p s e", e=1)
                    srec3 = srec[:].rearrange("p (s e) -> p s e", e=1)
                    ssum23 = ssum2[:].rearrange("p (s e) -> p s e", e=1)
                    srec23 = srec2[:].rearrange("p (s e) -> p s e", e=1)
                    kfg = kf3[:, g, 0:2048].rearrange("p (h f) -> p h f", f=128)
                    dst_p = kfg[:, hg * 8:(hg + 1) * 8, 0:64]
                    dst_n = kfg[:, hg * 8:(hg + 1) * 8, 64:128]
                    ctx.do("vector", lambda q: q.tensor_reduce(ssum3, sep3, axis=AX.X, op=ALU.add),
                           wait=[(sp, s_exp), (vp, vp.n)], inc=[(vp, 1)])
                    ctx.do("vector", lambda q: q.reciprocal(srec[:], ssum[:]), wait=[(vp, vp.n)], inc=[(vp, 1)])
                    ctx.do("vector", lambda q, dst_p=dst_p: q.tensor_tensor(
                        dst_p, sep3, srec3.broadcast_to([128, 8, 64]), op=ALU.mult),
                        wait=[(vp, vp.n)], inc=[(vp, 1)])
                    ctx.do("vector", lambda q: q.tensor_reduce(ssum23, sen3, axis=AX.X, op=ALU.add),
                           wait=[(vp, vp.n)], inc=[(vp, 1)])
                    ctx.do("vector", lambda q: q.reciprocal(srec2[:], ssum2[:]), wait=[(vp, vp.n)], inc=[(vp, 1)])
                    ctx.do("vector", lambda q, dst_n=dst_n: q.tensor_tensor(
                        dst_n, sen3, srec23.broadcast_to([128, 8, 64]), op=ALU.mult),
                        wait=[(vp, vp.n)], inc=[(vp, 1)])
                    scale_done[0] = vp.n
                for ch in range(2):
                    def vmm(q, g=g, ch=ch):
                        for ct in range(8):
                            lhs = xnT3[:, (g // 8) * 8 + ct, 1 + (g % 8) * 128:1 + (g % 8 + 1) * 128]
                            rhs = wvT[:, ct * 1024 + ch * 512: ct * 1024 + (ch + 1) * 512]
                            ins = q.matmul(pv[ch][:], lhs, rhs, start=(ct == 0), stop=(ct == 7))
                        return ins
                    wv = [(vp, tr_copy_done[i])]
                    if (i - 1, ch) in pv_free:
                        wv.append((vp, pv_free[(i - 1, ch)]))
                    ctx.do("tensor", vmm, wait=wv, inc=[(tp, 1)])
                    ctx.do("vector", lambda q, g=g, ch=ch: q.tensor_copy(
                        v3[:, g, ch * 512:(ch + 1) * 512], pv[ch][:]),
                        wait=[(tp, tp.n), (vp, vp.n)], inc=[(vp, 1)])
                    pv_free[(i, ch)] = vp.n
            v_main = vp.n
            if stage == "main":
                vfin = vp.n
                ctx.do("sync", lambda q: q.dma_start(out=xnT_sp, in_=xnT[:]), wait=[(vp, vfin)], inc=[(stc, 16)])
                ctx.do("sync", lambda q: q.dma_start(out=kf_sp, in_=kf[:]), inc=[(stc, 16)])
                ctx.do("sync", lambda q: q.dma_start(out=v_sp, in_=vsb[:]), inc=[(stc, 16)])
                ctx.do("sync", lambda q: q.dma_start(out=states, in_=sst[:]), inc=[(stc, 16)])
                ctx.wait("gpsimd", stc, 16 * 4)
                return
            pst_free = v_main
            for b in range(B):
                for h in range(H):
                    def stmm(q, b=b, h=h):
                        for ch in range(8):
                            lhs = kf3[:, b * 8 + ch, h * 128:(h + 1) * 128]
                            rhsv = v3[:, b * 8 + ch, h * 64:(h + 1) * 64]
                            q.matmul(pst[:, 0:64], lhs, rhsv, start=(ch == 0), stop=False)
                            ins = q.matmul(pst[:, 64:65], lhs, onescol[:], start=False, stop=(ch == 7))
                        return ins
                    ctx.do("tensor", stmm, wait=[(vp, pst_free)], inc=[(tp, 1)])
                    ctx.do("vector", lambda q, b=b, h=h: q.tensor_copy(
                        sst[:, (b * 16 + h) * 65:(b * 16 + h + 1) * 65], pst),
                        wait=[(tp, tp.n)], inc=[(vp, 1)])
                    pst_free = vp.n
            for b in range(B):
                for ct in range(8):
                    g = b * 8 + ct
                    ctx.do("gpsimd", lambda q, b=b, ct=ct, g=g: q.dma_start(
                        out=xnT3[:, g, 0:1],
                        in_=hx_scr[b, ct * 128:(ct + 1) * 128].rearrange("(a o) -> a o", o=1),
                        allow_slow_non_contiguous=True),
                        wait=[(ld, ld_hxw), (vp, vp.n)], inc=[(ld, 16)])
            ld_halo2 = ld.n
            ctx.wait("sync", ld, ld_halo2)
            vfin = vp.n
            ctx.do("sync", lambda q: q.dma_start(out=xnT_sp, in_=xnT[:]), wait=[(vp, vfin)], inc=[(stc, 16)])
            ctx.do("sync", lambda q: q.dma_start(out=kf_sp, in_=kf[:]), inc=[(stc, 16)])
            ctx.do("sync", lambda q: q.dma_start(out=v_sp, in_=vsb[:]), inc=[(stc, 16)])
            ctx.do("sync", lambda q: q.dma_start(out=states, in_=sst[:]), inc=[(stc, 16)])
            ctx.wait("gpsimd", stc, 16 * 4)
            hooks.get("tail", lambda c: None)(ctx)

        prog.build(phase)


def _l1_io(nc, kind_in="ExternalInput", kind_out="ExternalOutput"):
    return dict(
        x_sl=nc.dram_tensor("x_sl", [B, TC, C], F32, kind=kind_in).ap(),
        x_halo=nc.dram_tensor("x_halo", [B, C], F32, kind=kind_in).ap(),
        w_qkv=nc.dram_tensor("w_qkv", [3 * C, C], F32, kind=kind_in).ap(),
        wk_map=nc.dram_tensor("wk_map", [64, 64], F32, kind=kind_in).ap(),
        bk_map=nc.dram_tensor("bk_map", [1, 64], F32, kind=kind_in).ap(),
        id_bf=nc.dram_tensor("id_bf", [128, 128], BF16, kind=kind_in).ap(),
        xnT_sp=nc.dram_tensor("xnT_sp", [128, 16 * 1025], BF16, kind=kind_out).ap(),
        kf_sp=nc.dram_tensor("kf_sp", [128, 16 * 2048], BF16, kind=kind_out).ap(),
        v_sp=nc.dram_tensor("v_sp", [128, 16 * 1024], BF16, kind=kind_out).ap(),
        states=nc.dram_tensor("states", [128, 32 * 65], F32, kind=kind_out).ap(),
        hx_scr=nc.dram_tensor("hx_scr", [B, 1024], BF16).ap(),
    )


def build_L1(stage="full"):
    nc = bass.Bass("TRN2", target_bir_lowering=False)
    io = _l1_io(nc)
    es = ExitStack()
    with es:
        prog = Prog(nc, es)
        prog.prefix = "L1_"
        emit_L1(nc, prog, stage=stage, **io)
    return nc



def _split_inputs(L0_attn_x):
    x = np.ascontiguousarray(L0_attn_x)
    xs, halos = [], []
    for c in range(R):
        xs.append(np.ascontiguousarray(x[:, c * TC:(c + 1) * TC, :]))
        if c == 0:
            h = np.zeros((B, C), np.float32)
        else:
            h = np.ascontiguousarray(x[:, c * TC - 1, :])
        halos.append(h)
    return xs, halos


def run_L1(inp, stage="full"):
    nc = build_L1(stage)
    xs, halos = _split_inputs(inp["L0_attn_x"])
    id_bf = np.eye(128, dtype=BF)
    in_maps = []
    for c in range(R):
        in_maps.append(dict(
            x_sl=xs[c], x_halo=halos[c],
            w_qkv=inp["L0_attn_w_qkv"],
            wk_map=inp["L0_attn_wk_map"],
            bk_map=np.ascontiguousarray(inp["L0_attn_bk_map"].reshape(1, 64)),
            id_bf=id_bf,
        ))
    return run_bass_kernel_spmd(nc, in_maps, core_ids=list(range(R)))





def emit_L2a(nc, prog, *, xnT_in, kf_in, v_in, states_r, prefw, w_qkv, wq_map, bq_map, id_bf, triu_f, y_sp, hooks=None):
    hooks = hooks or {}
    es = ExitStack()
    with es:
        T_ = lambda nm, shp, dt: es.enter_context(nc.sbuf_tensor(prog.prefix + nm, shp, dt))
        P_ = lambda nm, shp, dt: es.enter_context(nc.psum_tensor(prog.prefix + nm, shp, dt))
        sid = T_("sid", [128, 128], BF16)
        striu = T_("striu", [128, 128], F32)
        wkr = T_("wkr", [128, 8 * 1024], BF16)
        wqm = T_("wqm", [64, 64], BF16)
        wqmT2 = T_("wqmT2", [128, 128], BF16)
        bqb = T_("bqb", [1, 64], BF16)
        bq8 = T_("bq8", [1, 512], BF16)
        ones1 = T_("ones1", [1, 128], BF16)
        onescol = T_("onescol", [128, 1], BF16)
        wcomb = T_("wcomb", [128, 8 * 1024], BF16)
        xnT = T_("xnT", [128, 16 * 1025], BF16)
        kfb = T_("kfb", [128, 8 * 2048], BF16)
        vb = T_("vb", [128, 8 * 1024], BF16)
        qfb = T_("qfb", [128, 8 * 2048], BF16)
        yb = T_("yb", [128, 8 * 1024], BF16)
        acc = T_("acc", [128, 32 * 65], F32)
        stt_ = T_("stt_", [128, 32 * 65], F32)
        spw = T_("spw", [128, 8], F32)
        sep = T_("sep", [128, 512], F32)
        sen = T_("sen", [128, 512], F32)
        ssum = T_("ssum", [128, 8], F32)
        srec = T_("srec", [128, 8], F32)
        ssum2 = T_("ssum2", [128, 8], F32)
        srec2 = T_("srec2", [128, 8], F32)
        kfT = T_("kfT", [128, 128], BF16)
        qfT = T_("qfT", [128, 128], BF16)
        sbf = T_("sbf", [128, 65], BF16)
        pA_sb = T_("pA_sb", [128, 128], BF16)
        sden = T_("sden", [128, 1], F32)
        sdr = T_("sdr", [128, 1], F32)

        pq = [P_(f"pq{i}", [128, 512], F32) for i in range(2)]
        ptrb = P_("ptrb", [128, 256], BF16)
        pat = P_("pat", [128, 128], F32)
        py = P_("py", [128, 65], F32)
        pu = P_("pu", [128, 65], F32)

        xnT3 = xnT[:].rearrange("p (g c) -> p g c", c=1025)
        kfb3 = kfb[:].rearrange("p (g c) -> p g c", c=2048)
        vb3 = vb[:].rearrange("p (g c) -> p g c", c=1024)
        qfb3 = qfb[:].rearrange("p (g c) -> p g c", c=2048)
        yb3 = yb[:].rearrange("p (g c) -> p g c", c=1024)
        wkr3 = wkr[:].rearrange("p (r c) -> p r c", c=1024)

        def phase(ctx: Ctx):
            hooks.get("head", lambda c: None)(ctx)
            ld = ctx.sem("ld")
            tp = ctx.sem("tp")
            vp = ctx.sem("vp")
            sp = ctx.sem("sp")
            stc = ctx.sem("stc")

            ctx.do("sync", lambda q: q.dma_start(out=sid[:], in_=id_bf), inc=[(ld, 16)])
            ctx.do("sync", lambda q: q.dma_start(out=striu[:], in_=triu_f), inc=[(ld, 16)])
            ctx.do("sync", lambda q: q.dma_start(out=xnT[:], in_=xnT_in), inc=[(ld, 16)])
            ctx.do("sync", lambda q: q.dma_start(out=spw[:], in_=prefw), inc=[(ld, 16)])
            ctx.do("gpsimd", lambda q: q.dma_start(out=wkr3, in_=w_qkv[0:1024, :].rearrange("(r p) c -> p r c", p=128)),
                   inc=[(ld, 16)])
            ctx.do("gpsimd", lambda q: q.dma_start(out=wqm[:], in_=wq_map), inc=[(ld, 16)])
            ctx.do("gpsimd", lambda q: q.dma_start(out=bqb[:], in_=bq_map), inc=[(ld, 16)])
            ld_w = ld.n
            ctx.do("vector", lambda q: q.memset(ones1[:], 1.0))
            ctx.do("vector", lambda q: q.memset(onescol[:], 1.0))
            ctx.do("vector", lambda q: q.tensor_copy(bq8[:].rearrange("p (s e) -> p s e", e=64),
                                                     bqb[:].rearrange("p (s e) -> p s e", e=64).broadcast_to([1, 8, 64])),
                   wait=[(ld, ld_w)], inc=[(vp, 1)])
            v_bq8 = vp.n
            ctx.do("vector", lambda q: q.memset(wqmT2[:], 0.0), inc=[(vp, 1)])
            ctx.do("tensor", lambda q: q.transpose(ptrb[0:64, 0:64], wqm[:], sid[0:64, 0:64]),
                   wait=[(ld, ld_w)], inc=[(tp, 1)])
            ctx.do("vector", lambda q: q.tensor_copy(wqmT2[0:64, 0:64], ptrb[0:64, 0:64]),
                   wait=[(tp, tp.n), (vp, vp.n)], inc=[(vp, 1)])
            ctx.do("gpsimd", lambda q: q.dma_start(out=wqmT2[64:128, 64:128], in_=wqmT2[0:64, 0:64]),
                   wait=[(vp, vp.n)], inc=[(ld, 16)])
            ld_wqmT = ld.n
            v_wqmT = vp.n
            for ct in range(8):
                for hg in range(2):
                    def mmgrp(q, ct=ct, hg=hg):
                        for j in range(4):
                            hp = hg * 4 + j
                            lt = wkr3[:, hp, ct * 128:(ct + 1) * 128]
                            ins = q.matmul(pq[0][:, j * 128:(j + 1) * 128], lt, wqmT2[:], start=True, stop=True)
                        return ins
                    ctx.do("tensor", mmgrp,
                           wait=[(vp, v_wqmT), (ld, ld_wqmT)] if (ct == 0 and hg == 0) else [(vp, vp.n)],
                           inc=[(tp, 1)])
                    ctx.do("vector", lambda q, ct=ct, hg=hg: q.tensor_copy(
                        wcomb[:, ct * 1024 + hg * 512: ct * 1024 + (hg + 1) * 512], pq[0][:]),
                        wait=[(tp, tp.n)], inc=[(vp, 1)])
            ctx.do("vector", lambda q: q.memset(acc[:], 0.0), inc=[(vp, 1)])
            for r_ in range(R):
                ctx.do("sync", lambda q, r_=r_: q.dma_start(out=stt_[:], in_=states_r(r_)),
                       wait=[(vp, vp.n)], inc=[(ld, 16)])
                ctx.do("vector", lambda q, r_=r_: q.scalar_tensor_tensor(
                    acc[:], stt_[:], spw[:, r_:r_ + 1], acc[:], op0=ALU.mult, op1=ALU.add),
                    wait=[(ld, ld.n), (vp, vp.n)], inc=[(vp, 1)])
            for b in range(B):
                ctx.do("sync", lambda q, b=b: q.dma_start(out=kfb[:], in_=kf_in[:, b * 8 * 2048:(b + 1) * 8 * 2048]),
                       wait=[(vp, vp.n), (stc, stc.n)], inc=[(ld, 16)])
                ctx.do("sync", lambda q, b=b: q.dma_start(out=vb[:], in_=v_in[:, b * 8 * 1024:(b + 1) * 8 * 1024]),
                       inc=[(ld, 16)])
                ld_b = ld.n
                pp_free = {}
                scale_done = [0]
                for tt in range(NT):
                    for hg in range(2):
                        def pmm2(q, b=b, tt=tt, hg=hg):
                            q.matmul(pq[hg][:], ones1[:], bq8[:], start=True, stop=False)
                            for ct in range(8):
                                lhs = xnT3[:, b * 8 + ct, 1 + tt * 128:1 + (tt + 1) * 128]
                                rhs = wcomb[:, ct * 1024 + hg * 512: ct * 1024 + (hg + 1) * 512]
                                ins = q.matmul(pq[hg][:], lhs, rhs, start=False, stop=(ct == 7))
                            return ins
                        wpp = [(vp, v_bq8), (ld, ld_b)]
                        if (tt - 1, hg) in pp_free:
                            wpp.append((sp, pp_free[(tt - 1, hg)]))
                        elif b == 1:
                            wpp.append((sp, sp.n))
                        ctx.do("tensor", pmm2, wait=wpp, inc=[(tp, 1)])
                        t_pp = tp.n
                        ctx.do("scalar", lambda q, hg=hg: q.activation(sep[:], pq[hg][:], AF.Exp),
                               wait=[(sp, sp.n), (tp, t_pp), (vp, scale_done[0])], inc=[(sp, 1)])
                        ctx.do("scalar", lambda q, hg=hg: q.activation(sen[:], pq[hg][:], AF.Exp, scale=-1.0),
                               wait=[(sp, sp.n)], inc=[(sp, 1)])
                        pp_free[(tt, hg)] = sp.n
                        s_exp = sp.n
                        sep3 = sep[:].rearrange("p (s e) -> p s e", e=64)
                        sen3 = sen[:].rearrange("p (s e) -> p s e", e=64)
                        ssum3 = ssum[:].rearrange("p (s e) -> p s e", e=1)
                        srec3 = srec[:].rearrange("p (s e) -> p s e", e=1)
                        ssum23 = ssum2[:].rearrange("p (s e) -> p s e", e=1)
                        srec23 = srec2[:].rearrange("p (s e) -> p s e", e=1)
                        qfg = qfb3[:, tt, 0:2048].rearrange("p (h f) -> p h f", f=128)
                        dst_p = qfg[:, hg * 8:(hg + 1) * 8, 0:64]
                        dst_n = qfg[:, hg * 8:(hg + 1) * 8, 64:128]
                        ctx.do("vector", lambda q: q.tensor_reduce(ssum3, sep3, axis=AX.X, op=ALU.add),
                               wait=[(sp, s_exp), (vp, vp.n)], inc=[(vp, 1)])
                        ctx.do("vector", lambda q: q.reciprocal(srec[:], ssum[:]), wait=[(vp, vp.n)], inc=[(vp, 1)])
                        ctx.do("vector", lambda q, dst_p=dst_p: q.tensor_tensor(
                            dst_p, sep3, srec3.broadcast_to([128, 8, 64]), op=ALU.mult),
                            wait=[(vp, vp.n)], inc=[(vp, 1)])
                        ctx.do("vector", lambda q: q.tensor_reduce(ssum23, sen3, axis=AX.X, op=ALU.add),
                               wait=[(vp, vp.n)], inc=[(vp, 1)])
                        ctx.do("vector", lambda q: q.reciprocal(srec2[:], ssum2[:]), wait=[(vp, vp.n)], inc=[(vp, 1)])
                        ctx.do("vector", lambda q, dst_n=dst_n: q.tensor_tensor(
                            dst_n, sen3, srec23.broadcast_to([128, 8, 64]), op=ALU.mult),
                            wait=[(vp, vp.n)], inc=[(vp, 1)])
                        scale_done[0] = vp.n
                v_qf = vp.n
                for h in range(H):
                    for chk in range(8):
                        i65 = (b * 16 + h) * 65
                        def trq(q, h=h, chk=chk):
                            q.transpose(ptrb[:, 0:128], kfb3[:, chk, h * 128:(h + 1) * 128], sid[:])
                            return q.transpose(ptrb[:, 128:256], qfb3[:, chk, h * 128:(h + 1) * 128], sid[:])
                        ctx.do("tensor", trq, wait=[(vp, max(v_qf, vp.n))], inc=[(tp, 1)])
                        ctx.do("vector", lambda q: q.tensor_copy(kfT[:], ptrb[:, 0:128]),
                               wait=[(tp, tp.n)], inc=[(vp, 1)])
                        ctx.do("vector", lambda q: q.tensor_copy(qfT[:], ptrb[:, 128:256]),
                               wait=[(vp, vp.n)], inc=[(vp, 1)])
                        ctx.do("vector", lambda q, i65=i65: q.tensor_copy(sbf[:], acc[:, i65:i65 + 65]),
                               wait=[(vp, vp.n)], inc=[(vp, 1)])
                        v_pre = vp.n
                        ctx.do("tensor", lambda q: q.matmul(pat[:], kfT[:], qfT[:], start=True, stop=True),
                               wait=[(vp, v_pre)], inc=[(tp, 1)])
                        ctx.do("vector", lambda q: q.tensor_tensor(pA_sb[:], pat[:], striu[:], op=ALU.mult),
                               wait=[(tp, tp.n)], inc=[(vp, 1)])
                        v_pa = vp.n
                        def ymm(q, b=b, h=h, chk=chk):
                            q.matmul(py[:, 0:64], pA_sb[:], vb3[:, chk, h * 64:(h + 1) * 64], start=True, stop=False)
                            q.matmul(py[:, 64:65], pA_sb[:], onescol[:], start=False, stop=False)
                            q.matmul(py[:, 0:64], qfT[:], sbf[:, 0:64], start=False, stop=False)
                            return q.matmul(py[:, 64:65], qfT[:], sbf[:, 64:65], start=False, stop=True)
                        ctx.do("tensor", ymm, wait=[(vp, v_pa)], inc=[(tp, 1)])
                        t_y = tp.n
                        def umm(q, b=b, h=h, chk=chk):
                            q.matmul(pu[:, 0:64], kfb3[:, chk, h * 128:(h + 1) * 128],
                                     vb3[:, chk, h * 64:(h + 1) * 64], start=True, stop=False)
                            return q.matmul(pu[:, 64:65], kfb3[:, chk, h * 128:(h + 1) * 128], onescol[:],
                                            start=False, stop=True)
                        ctx.do("tensor", umm, inc=[(tp, 1)])
                        t_u = tp.n
                        ctx.do("vector", lambda q, i65=i65: q.tensor_tensor(
                            acc[:, i65:i65 + 65], pu[:], acc[:, i65:i65 + 65], op=ALU.add),
                            wait=[(tp, t_u)], inc=[(vp, 1)])
                        ctx.do("vector", lambda q: q.tensor_scalar(sden[:], py[:, 64:65], 1e-6, None, op0=ALU.max),
                               wait=[(tp, t_y), (vp, vp.n)], inc=[(vp, 1)])
                        ctx.do("vector", lambda q: q.reciprocal(sdr[:], sden[:]), wait=[(vp, vp.n)], inc=[(vp, 1)])
                        ctx.do("vector", lambda q, h=h, chk=chk: q.tensor_scalar(
                            yb3[:, chk, h * 64:(h + 1) * 64], py[:, 0:64], sdr[:, 0:1], None, op0=ALU.mult),
                            wait=[(vp, vp.n)], inc=[(vp, 1)])
                ctx.do("sync", lambda q, b=b: q.dma_start(out=y_sp[b], in_=yb[:]),
                       wait=[(vp, vp.n)], inc=[(stc, 16)])
            ctx.wait("gpsimd", stc, 16 * 2)
            hooks.get("tail", lambda c: None)(ctx)

        prog.build(phase)


def _l2a_io(nc, ki="ExternalInput", ko="ExternalOutput"):
    return dict(
        xnT_in=nc.dram_tensor("xnT_in", [128, 16 * 1025], BF16, kind=ki).ap(),
        kf_in=nc.dram_tensor("kf_in", [128, 16 * 2048], BF16, kind=ki).ap(),
        v_in=nc.dram_tensor("v_in", [128, 16 * 1024], BF16, kind=ki).ap(),
        prefw=nc.dram_tensor("prefw", [128, 8], F32, kind=ki).ap(),
        w_qkv=nc.dram_tensor("w_qkv", [3 * C, C], F32, kind=ki).ap(),
        wq_map=nc.dram_tensor("wq_map", [64, 64], F32, kind=ki).ap(),
        bq_map=nc.dram_tensor("bq_map", [1, 64], F32, kind=ki).ap(),
        id_bf=nc.dram_tensor("id_bf", [128, 128], BF16, kind=ki).ap(),
        triu_f=nc.dram_tensor("triu_f", [128, 128], F32, kind=ki).ap(),
        y_sp=nc.dram_tensor("y_sp", [B, 128, 8 * 1024], BF16, kind=ko).ap(),
    )


def build_L2a():
    nc = bass.Bass("TRN2", target_bir_lowering=False)
    io = _l2a_io(nc)
    st_all = nc.dram_tensor("states_all", [8, 128, 32 * 65], F32, kind="ExternalInput").ap()
    io["states_r"] = lambda r_: st_all[r_]
    es = ExitStack()
    with es:
        prog = Prog(nc, es)
        prog.prefix = "L2a_"
        emit_L2a(nc, prog, **io)
    return nc


def run_L2a(inp, r1):
    nc = build_L2a()
    id_bf = np.eye(128, dtype=BF)
    triu_f = np.triu(np.ones((128, 128), np.float32))
    states_all = np.stack([r1.results[c]["states"] for c in range(R)])
    in_maps = []
    for c in range(R):
        pw_ = np.zeros((128, 8), np.float32)
        pw_[:, :c] = 1.0
        in_maps.append(dict(
            xnT_in=r1.results[c]["xnT_sp"],
            kf_in=r1.results[c]["kf_sp"],
            v_in=r1.results[c]["v_sp"],
            states_all=states_all,
            prefw=pw_,
            w_qkv=inp["L0_attn_w_qkv"],
            wq_map=inp["L0_attn_wq_map"],
            bq_map=np.ascontiguousarray(inp["L0_attn_bq_map"].reshape(1, 64)),
            id_bf=id_bf, triu_f=triu_f,
        ))
    return run_bass_kernel_spmd(nc, in_maps, core_ids=list(range(R)))





def emit_L2b(nc, prog, *, xnT_in, y_in, x_sl, W_r, W_k, W_v, w_out, mu_r, mu_k,
             id_bf, x1_out, xn1T_sp, hooks=None):
    hooks = hooks or {}
    es = ExitStack()
    with es:
        T_ = lambda nm, shp, dt: es.enter_context(nc.sbuf_tensor(prog.prefix + nm, shp, dt))
        P_ = lambda nm, shp, dt: es.enter_context(nc.psum_tensor(prog.prefix + nm, shp, dt))
        sid = T_("sid", [128, 128], BF16)
        stg = T_("stg", [128, 8 * 1024], BF16)
        WrT = T_("WrT", [128, 8 * 1024], BF16)
        WkT = T_("WkT", [128, 8 * 1024], BF16)
        WvT = T_("WvT", [128, 8 * 1024], BF16)
        woT = T_("woT", [128, 8 * 1024], BF16)
        smur = T_("smur", [128, 8], F32)
        smuk = T_("smuk", [128, 8], F32)
        somur = T_("somur", [128, 8], F32)
        somuk = T_("somuk", [128, 8], F32)
        xnTb = T_("xnTb", [128, 8 * 1025], BF16)
        ybuf = T_("ybuf", [128, 8 * 1024], BF16)
        yT = T_("yT", [128, 8 * 1024], BF16)
        xrT = T_("xrT", [128, 8 * 1024], BF16)
        xkT = T_("xkT", [128, 8 * 1024], BF16)
        kact = T_("kact", [128, 8 * 1024], BF16)
        stmp = T_("stmp", [128, 1024], F32)
        sx = T_("sx", [128, 1024], F32)
        sx1 = T_("sx1", [128, 1024], F32)
        sr = T_("sr", [128, 1024], F32)
        srel = T_("srel", [128, 512], F32)
        srow = T_("srow", [128, 1], F32)
        svar = T_("svar", [128, 1], F32)
        sln = T_("sln", [128, 1], F32)
        srstd = T_("srstd", [128, 1], F32)
        seps = T_("seps", [128, 1], F32)
        sxn1 = T_("sxn1", [128, 1024], BF16)

        pA = [P_(f"pA{i}", [128, 512], F32) for i in range(2)]
        pB = [P_(f"pB{i}", [128, 512], F32) for i in range(2)]
        ptrb = P_("ptrb", [128, 1024], BF16)

        xn1Tb = stg
        xnTb3 = xnTb[:].rearrange("p (g c) -> p g c", c=1025)
        stg3 = stg[:].rearrange("p (r c) -> p r c", c=1024)
        yb3 = ybuf[:].rearrange("p (g c) -> p g c", c=1024)
        yT3 = yT[:].rearrange("p (g c) -> p g c", c=1024)
        xrT3 = xrT[:].rearrange("p (g c) -> p g c", c=1024)
        xkT3 = xkT[:].rearrange("p (g c) -> p g c", c=1024)
        kact3 = kact[:].rearrange("p (g c) -> p g c", c=1024)
        WkT3 = WkT[:].rearrange("p (g c) -> p g c", c=1024)
        xn1Tb3 = xn1Tb[:].rearrange("p (g c) -> p g c", c=1024)

        def phase(ctx: Ctx):
            hooks.get("head", lambda c: None)(ctx)
            ld = ctx.sem("ld")
            tp = ctx.sem("tp")
            vp = ctx.sem("vp")
            sp = ctx.sem("sp")
            stc = ctx.sem("stc")

            ctx.do("sync", lambda q: q.dma_start(out=sid[:], in_=id_bf), inc=[(ld, 16)])
            ctx.do("sync", lambda q: q.dma_start(out=smur[:], in_=mu_r), inc=[(ld, 16)])
            ctx.do("sync", lambda q: q.dma_start(out=smuk[:], in_=mu_k), inc=[(ld, 16)])
            ld_mu = ld.n
            ctx.do("vector", lambda q: q.memset(seps[:], EPS))
            ctx.do("vector", lambda q: q.tensor_scalar(somur[:], smur[:], -1.0, 1.0, op0=ALU.mult, op1=ALU.add),
                   wait=[(ld, ld_mu)], inc=[(vp, 1)])
            ctx.do("vector", lambda q: q.tensor_scalar(somuk[:], smuk[:], -1.0, 1.0, op0=ALU.mult, op1=ALU.add),
                   wait=[(vp, vp.n)], inc=[(vp, 1)])
            for (wsrc, wdst) in ((W_r, WrT), (W_k, WkT), (W_v, WvT), (w_out, woT)):
                ctx.do("gpsimd", lambda q, wsrc=wsrc: q.dma_start(
                    out=stg3, in_=wsrc.rearrange("(r p) c -> p r c", p=128)),
                    wait=[(tp, tp.n)], inc=[(ld, 16)])
                ld_st = ld.n
                for dt in range(8):
                    def trg(q, dt=dt):
                        for ct in range(8):
                            ins = q.transpose(ptrb[:, ct * 128:(ct + 1) * 128],
                                              stg3[:, dt, ct * 128:(ct + 1) * 128], sid[:])
                        return ins
                    ctx.do("tensor", trg, wait=[(ld, ld_st), (vp, vp.n)], inc=[(tp, 1)])
                    ctx.do("vector", lambda q, wdst=wdst, dt=dt: q.tensor_copy(
                        wdst[:].rearrange("p (g c) -> p g c", c=1024)[:, :, dt * 128:(dt + 1) * 128],
                        ptrb[:].rearrange("p (a c) -> p a c", c=128)),
                        wait=[(tp, tp.n)], inc=[(vp, 1)])
            for b in range(B):
                v_start = vp.n
                ctx.do("sync", lambda q, b=b: q.dma_start(
                    out=xnTb[:], in_=xnT_in[:, b * 8 * 1025:(b + 1) * 8 * 1025]),
                    wait=[(vp, v_start), (stc, stc.n)], inc=[(ld, 16)])
                ctx.do("sync", lambda q, b=b: q.dma_start(out=ybuf[:], in_=y_in[b]), inc=[(ld, 16)])
                ld_b = ld.n
                for ct in range(8):
                    for (dst3, mu_s, omu_s, tag) in ((xrT3, smur, somur, 0), (xkT3, smuk, somuk, 1)):
                        ctx.do("vector", lambda q, ct=ct, omu_s=omu_s: q.tensor_scalar(
                            stmp[:], xnTb3[:, ct, 0:1024], omu_s[:, ct:ct + 1], None, op0=ALU.mult),
                            wait=[(ld, ld_b), (vp, vp.n)], inc=[(vp, 1)])
                        ctx.do("vector", lambda q, ct=ct, dst3=dst3, mu_s=mu_s: q.scalar_tensor_tensor(
                            dst3[:, ct, :], xnTb3[:, ct, 1:1025], mu_s[:, ct:ct + 1], stmp[:],
                            op0=ALU.mult, op1=ALU.add),
                            wait=[(vp, vp.n)], inc=[(vp, 1)])
                v_lerp = vp.n
                for tt in range(8):
                    def tryy(q, tt=tt):
                        for dt in range(8):
                            ins = q.transpose(ptrb[:, dt * 128:(dt + 1) * 128],
                                              yb3[:, tt, dt * 128:(dt + 1) * 128], sid[:])
                        return ins
                    ctx.do("tensor", tryy, wait=[(ld, ld_b), (vp, vp.n)], inc=[(tp, 1)])
                    ctx.do("vector", lambda q, tt=tt: q.tensor_copy(
                        yT3[:, :, tt * 128:(tt + 1) * 128],
                        ptrb[:].rearrange("p (a c) -> p a c", c=128)),
                        wait=[(tp, tp.n)], inc=[(vp, 1)])
                for dt in range(8):
                    for ch in range(2):
                        def kmm(q, dt=dt, ch=ch):
                            for ct in range(8):
                                ins = q.matmul(pB[ch][:], WkT3[:, ct, dt * 128:(dt + 1) * 128],
                                               xkT3[:, ct, ch * 512:(ch + 1) * 512],
                                               start=(ct == 0), stop=(ct == 7))
                            return ins
                        ctx.do("tensor", kmm, wait=[(vp, v_lerp), (sp, sp.n)], inc=[(tp, 1)])
                        t_k = tp.n
                        ctx.do("scalar", lambda q, ch=ch: q.activation(srel[:], pB[ch][:], AF.Relu),
                               wait=[(tp, t_k), (sp, sp.n)], inc=[(sp, 1)])
                        ctx.do("scalar", lambda q, dt=dt, ch=ch: q.activation(
                            kact3[:, dt, ch * 512:(ch + 1) * 512], srel[:], AF.Square),
                            wait=[(sp, sp.n)], inc=[(sp, 1)])
                s_kact = sp.n
                pa_free = sp.n
                for tt in range(8):
                    for ch in range(2):
                        def amm(q, tt=tt, ch=ch):
                            for dt in range(8):
                                ins = q.matmul(pA[ch][:], yT3[:, dt, tt * 128:(tt + 1) * 128],
                                               woT[:, dt * 1024 + ch * 512: dt * 1024 + (ch + 1) * 512],
                                               start=(dt == 0), stop=(dt == 7))
                            return ins
                        ctx.do("tensor", amm, wait=[(vp, vp.n)], inc=[(tp, 1)])
                    t_a = tp.n
                    for ch in range(2):
                        def rmm(q, tt=tt, ch=ch):
                            for ct in range(8):
                                ins = q.matmul(pB[ch][:], xrT3[:, ct, tt * 128:(tt + 1) * 128],
                                               WrT[:, ct * 1024 + ch * 512: ct * 1024 + (ch + 1) * 512],
                                               start=(ct == 0), stop=(ct == 7))
                            return ins
                        ctx.do("tensor", rmm, wait=[(sp, max(s_kact, sp.n)), (vp, vp.n)], inc=[(tp, 1)])
                        t_r = tp.n
                        ctx.do("scalar", lambda q, tt=tt, ch=ch: q.activation(
                            sr[:, ch * 512:(ch + 1) * 512], pB[ch][:], AF.Sigmoid),
                            wait=[(tp, t_r), (sp, sp.n)], inc=[(sp, 1)])
                    s_r = sp.n
                    ctx.do("sync", lambda q, b=b, tt=tt: q.dma_start(
                        out=sx[:], in_=x_sl[b, tt * 128:(tt + 1) * 128, :]),
                        wait=[(vp, vp.n)], inc=[(ld, 16)])
                    ld_x = ld.n
                    for ch in range(2):
                        def fmm(q, tt=tt, ch=ch):
                            for dt in range(8):
                                ins = q.matmul(pB[ch][:], kact3[:, dt, tt * 128:(tt + 1) * 128],
                                               WvT[:, dt * 1024 + ch * 512: dt * 1024 + (ch + 1) * 512],
                                               start=(dt == 0), stop=(dt == 7))
                            return ins
                        ctx.do("tensor", fmm, wait=[(sp, s_r)], inc=[(tp, 1)])
                    t_f = tp.n
                    for ch in range(2):
                        ctx.do("vector", lambda q, ch=ch: q.tensor_tensor(
                            sx1[:, ch * 512:(ch + 1) * 512], pA[ch][:], sx[:, ch * 512:(ch + 1) * 512],
                            op=ALU.add),
                            wait=[(tp, t_f), (ld, ld_x), (vp, vp.n)], inc=[(vp, 1)])
                        ctx.do("vector", lambda q, ch=ch: q.tensor_tensor(
                            stmp[:, 0:512], pB[ch][:], sr[:, ch * 512:(ch + 1) * 512], op=ALU.mult),
                            wait=[(vp, vp.n)], inc=[(vp, 1)])
                        ctx.do("vector", lambda q, ch=ch: q.tensor_tensor(
                            sx1[:, ch * 512:(ch + 1) * 512], sx1[:, ch * 512:(ch + 1) * 512],
                            stmp[:, 0:512], op=ALU.add),
                            wait=[(vp, vp.n)], inc=[(vp, 1)])
                    v_x1 = vp.n
                    ctx.do("sync", lambda q, b=b, tt=tt: q.dma_start(
                        out=x1_out[b, tt * 128:(tt + 1) * 128, :], in_=sx1[:]),
                        wait=[(vp, v_x1)], inc=[(stc, 16)])
                    ctx.do("vector", lambda q: q.reduce_sum(srow[:], sx1[:], axis=AX.X),
                           wait=[(vp, v_x1)], inc=[(vp, 1)])
                    ctx.do("vector", lambda q: q.tensor_scalar(srow[:], srow[:], -1.0 / C, None, op0=ALU.mult),
                           wait=[(vp, vp.n)], inc=[(vp, 1)])
                    ctx.do("scalar", lambda q: q.activation(sx[:], sx1[:], AF.Identity, bias=srow[:, 0:1]),
                           wait=[(vp, vp.n), (sp, sp.n), (stc, stc.n)], inc=[(sp, 1)])
                    ctx.do("scalar", lambda q: q.activation(sx1[:], sx[:], AF.Square, accum_out=svar[:, 0:1]),
                           wait=[(sp, sp.n)], inc=[(sp, 1)])
                    ctx.do("scalar", lambda q: q.activation(sln[:], svar[:], AF.Ln, scale=1.0 / C, bias=seps[:, 0:1]),
                           wait=[(sp, sp.n)], inc=[(sp, 1)])
                    ctx.do("scalar", lambda q: q.activation(srstd[:], sln[:], AF.Exp, scale=-0.5),
                           wait=[(sp, sp.n)], inc=[(sp, 1)])
                    ctx.do("vector", lambda q: q.tensor_scalar(sxn1[:], sx[:], srstd[:, 0:1], None, op0=ALU.mult),
                           wait=[(sp, sp.n), (vp, vp.n)], inc=[(vp, 1)])
                    v_xn1 = vp.n
                    def trx1(q):
                        for ct in range(8):
                            ins = q.transpose(ptrb[:, ct * 128:(ct + 1) * 128],
                                              sxn1[:, ct * 128:(ct + 1) * 128], sid[:])
                        return ins
                    ctx.do("tensor", trx1, wait=[(vp, v_xn1)], inc=[(tp, 1)])
                    ctx.do("vector", lambda q, tt=tt: q.tensor_copy(
                        xn1Tb3[:, :, tt * 128:(tt + 1) * 128],
                        ptrb[:].rearrange("p (a c) -> p a c", c=128)),
                        wait=[(tp, tp.n)], inc=[(vp, 1)])
                ctx.do("sync", lambda q, b=b: q.dma_start(
                    out=xn1T_sp[:, b * 8 * 1024:(b + 1) * 8 * 1024], in_=xn1Tb[:]),
                    wait=[(vp, vp.n)], inc=[(stc, 16)])
            ctx.wait("gpsimd", stc, stc.n)
            hooks.get("tail", lambda c: None)(ctx)

        prog.build(phase)


def _l2b_io(nc, ki="ExternalInput", ko="ExternalOutput"):
    return dict(
        xnT_in=nc.dram_tensor("xnT_in", [128, 16 * 1025], BF16, kind=ki).ap(),
        y_in=nc.dram_tensor("y_in", [B, 128, 8 * 1024], BF16, kind=ki).ap(),
        x_sl=nc.dram_tensor("x_sl", [B, TC, C], F32, kind=ki).ap(),
        W_r=nc.dram_tensor("W_r", [C, C], F32, kind=ki).ap(),
        W_k=nc.dram_tensor("W_k", [C, C], F32, kind=ki).ap(),
        W_v=nc.dram_tensor("W_v", [C, C], F32, kind=ki).ap(),
        w_out=nc.dram_tensor("w_out", [C, C], F32, kind=ki).ap(),
        mu_r=nc.dram_tensor("mu_r", [128, 8], F32, kind=ki).ap(),
        mu_k=nc.dram_tensor("mu_k", [128, 8], F32, kind=ki).ap(),
        id_bf=nc.dram_tensor("id_bf", [128, 128], BF16, kind=ki).ap(),
        x1_out=nc.dram_tensor("x1_out", [B, TC, C], F32, kind=ko).ap(),
        xn1T_sp=nc.dram_tensor("xn1T_sp", [128, 16 * 1024], BF16, kind=ko).ap(),
    )


def build_L2b():
    nc = bass.Bass("TRN2", target_bir_lowering=False)
    io = _l2b_io(nc)
    es = ExitStack()
    with es:
        prog = Prog(nc, es)
        prog.prefix = "L2b_"
        emit_L2b(nc, prog, **io)
    return nc


def run_L2b(inp, r1, r2a):
    nc = build_L2b()
    id_bf = np.eye(128, dtype=BF)
    xs, _ = _split_inputs(inp["L0_attn_x"])
    mu_r = np.ascontiguousarray(inp["L0_ffn_mu_r"].reshape(8, 128).T)
    mu_k = np.ascontiguousarray(inp["L0_ffn_mu_k"].reshape(8, 128).T)
    in_maps = []
    for c in range(R):
        in_maps.append(dict(
            xnT_in=r1.results[c]["xnT_sp"],
            y_in=r2a.results[c]["y_sp"],
            x_sl=xs[c],
            W_r=inp["L0_ffn_W_r"], W_k=inp["L0_ffn_W_k"], W_v=inp["L0_ffn_W_v"],
            w_out=inp["L0_attn_w_out"],
            mu_r=mu_r, mu_k=mu_k, id_bf=id_bf,
        ))
    return run_bass_kernel_spmd(nc, in_maps, core_ids=list(range(R)))





def emit_L3(nc, prog, *, xn1T_in, Wq, Wk, Wv, w_red, wg, id_bf, id_f,
            qT_out, k_out, v_out, bs_out, gsc_sp, hooks=None):
    hooks = hooks or {}
    es = ExitStack()
    with es:
        T_ = lambda nm, shp, dt: es.enter_context(nc.sbuf_tensor(prog.prefix + nm, shp, dt))
        P_ = lambda nm, shp, dt: es.enter_context(nc.psum_tensor(prog.prefix + nm, shp, dt))
        sid = T_("sid", [128, 128], BF16)
        sidf = T_("sidf", [128, 128], F32)
        stg = T_("stg", [128, 8 * 1024], BF16)
        WqT = T_("WqT", [128, 8 * 1024], BF16)
        WkT = T_("WkT", [128, 8 * 1024], BF16)
        WvT = T_("WvT", [128, 8 * 1024], BF16)
        xn1T = T_("xn1T", [128, 16 * 1024], BF16)
        qTb = T_("qTb", [128, 8 * 1024], BF16)
        ksb = T_("ksb", [128, 8 * 1024], BF16)
        vsb2 = T_("vsb2", [128, 8 * 1024], BF16)
        swred = T_("swred", [128, 8 * 16], BF16)
        swg = T_("swg", [8, 16], F32)
        swgb = T_("swgb", [8, 16], BF16)
        ssq8 = T_("ssq8", [8, 16], F32)
        snrm = T_("snrm", [8, 1], F32)
        srn = T_("srn", [8, 1], F32)
        sdiag = T_("sdiag", [8, 8], BF16)
        swgnT = T_("swgnT", [16, 8], BF16)
        sbsd = T_("sbsd", [128, 64], F32)
        sbsj = T_("sbsj", [8, 1024], F32)
        redT = T_("redT", [16, 1024], BF16)
        slog = T_("slog", [128, 8], F32)
        sm = T_("sm", [128, 1], F32)
        snm = T_("snm", [128, 1], F32)
        sexp = T_("sexp", [128, 8], F32)
        ssm = T_("ssm", [128, 1], F32)
        ssc = T_("ssc", [128, 1], F32)
        sind = T_("sind", [128, 8], F32)
        gsc = T_("gsc", [128, 16 * 8], F32)

        pq = [P_(f"pq{i}", [128, 512], F32) for i in range(2)]
        ptrb = P_("ptrb", [128, 1024], BF16)
        pbsj = P_("pbsj", [8, 128], F32)
        pl = P_("pl", [128, 8], F32)
        psm = P_("psm", [16, 8], F32)

        xn1T3 = xn1T[:].rearrange("p (g c) -> p g c", c=1024)
        stg3 = stg[:].rearrange("p (r c) -> p r c", c=1024)
        WqT3 = WqT[:].rearrange("p (g c) -> p g c", c=1024)
        WkT3 = WkT[:].rearrange("p (g c) -> p g c", c=1024)
        WvT3 = WvT[:].rearrange("p (g c) -> p g c", c=1024)
        qTb3 = qTb[:].rearrange("p (g c) -> p g c", c=1024)
        ksb3 = ksb[:].rearrange("p (g c) -> p g c", c=1024)
        vsb3 = vsb2[:].rearrange("p (g c) -> p g c", c=1024)
        swred3 = swred[:].rearrange("p (g c) -> p g c", c=16)

        def phase(ctx: Ctx):
            hooks.get("head", lambda c: None)(ctx)
            ld = ctx.sem("ld")
            tp = ctx.sem("tp")
            vp = ctx.sem("vp")
            sp = ctx.sem("sp")
            stc = ctx.sem("stc")

            ctx.do("sync", lambda q: q.dma_start(out=sid[:], in_=id_bf), inc=[(ld, 16)])
            ctx.do("sync", lambda q: q.dma_start(out=sidf[:], in_=id_f), inc=[(ld, 16)])
            ctx.do("sync", lambda q: q.dma_start(out=xn1T[:], in_=xn1T_in), inc=[(ld, 16)])
            ctx.do("sync", lambda q: q.dma_start(out=swg[:], in_=wg), inc=[(ld, 16)])
            ctx.do("gpsimd", lambda q: q.dma_start(out=swgb[:], in_=wg), inc=[(ld, 16)])
            ctx.do("gpsimd", lambda q: q.dma_start(out=swred3, in_=w_red.rearrange("(r p) g -> p r g", p=128)),
                   inc=[(ld, 16)])
            ld_w = ld.n
            for (wsrc, wdst) in ((Wq, WqT), (Wk, WkT), (Wv, WvT)):
                ctx.do("gpsimd", lambda q, wsrc=wsrc: q.dma_start(
                    out=stg3, in_=wsrc.rearrange("(r p) c -> p r c", p=128)),
                    wait=[(tp, tp.n)], inc=[(ld, 16)])
                ld_st = ld.n
                for dt in range(8):
                    def trg(q, dt=dt):
                        for ct in range(8):
                            ins = q.transpose(ptrb[:, ct * 128:(ct + 1) * 128],
                                              stg3[:, dt, ct * 128:(ct + 1) * 128], sid[:])
                        return ins
                    ctx.do("tensor", trg, wait=[(ld, ld_st), (vp, vp.n)], inc=[(tp, 1)])
                    ctx.do("vector", lambda q, wdst=wdst, dt=dt: q.tensor_copy(
                        wdst[:].rearrange("p (g c) -> p g c", c=1024)[:, :, dt * 128:(dt + 1) * 128],
                        ptrb[:].rearrange("p (a c) -> p a c", c=128)),
                        wait=[(tp, tp.n)], inc=[(vp, 1)])
            ctx.do("vector", lambda q: q.tensor_tensor(ssq8[:], swg[:], swg[:], op=ALU.mult),
                   wait=[(ld, ld_w)], inc=[(vp, 1)])
            ctx.do("vector", lambda q: q.reduce_sum(snrm[:], ssq8[:], axis=AX.X),
                   wait=[(vp, vp.n)], inc=[(vp, 1)])
            ctx.do("scalar", lambda q: q.activation(srn[:], snrm[:], AF.Ln), wait=[(vp, vp.n)], inc=[(sp, 1)])
            ctx.do("scalar", lambda q: q.activation(snrm[:], srn[:], AF.Exp, scale=0.5),
                   wait=[(sp, sp.n)], inc=[(sp, 1)])
            ctx.do("vector", lambda q: q.tensor_scalar(snrm[:], snrm[:], 1e-4, None, op0=ALU.max),
                   wait=[(sp, sp.n)], inc=[(vp, 1)])
            ctx.do("vector", lambda q: q.reciprocal(srn[:], snrm[:]), wait=[(vp, vp.n)], inc=[(vp, 1)])
            ctx.do("vector", lambda q: q.tensor_scalar(sdiag[:], sidf[0:8, 0:8], srn[:, 0:1], None, op0=ALU.mult),
                   wait=[(vp, vp.n)], inc=[(vp, 1)])
            v_diag = vp.n
            ctx.do("tensor", lambda q: q.matmul(psm[:], swgb[:], sdiag[:], start=True, stop=True),
                   wait=[(vp, v_diag), (ld, ld_w)], inc=[(tp, 1)])
            ctx.do("vector", lambda q: q.tensor_copy(swgnT[:], psm[:]), wait=[(tp, tp.n)], inc=[(vp, 1)])
            v_wgnT = vp.n
            for b in range(B):
                for dt in range(8):
                    for ch in range(2):
                        def qmm(q, b=b, dt=dt, ch=ch):
                            for ht in range(8):
                                ins = q.matmul(pq[ch][:], WqT3[:, ht, dt * 128:(dt + 1) * 128],
                                               xn1T3[:, b * 8 + ht, ch * 512:(ch + 1) * 512],
                                               start=(ht == 0), stop=(ht == 7))
                            return ins
                        ctx.do("tensor", qmm, wait=[(vp, vp.n), (sp, sp.n)], inc=[(tp, 1)])
                        t_q = tp.n
                        ctx.do("scalar", lambda q, b=b, dt=dt, ch=ch: q.activation(
                            qTb3[:, dt, ch * 512:(ch + 1) * 512], pq[ch][:], AF.Identity,
                            scale=(1.0 if b == 0 else -1.0)),
                            wait=[(tp, t_q), (sp, sp.n)], inc=[(sp, 1)])
                ctx.do("sync", lambda q, b=b: q.dma_start(
                    out=qT_out(b), in_=qTb[:].rearrange("p (dt t) -> p dt t", t=1024)),
                    wait=[(sp, sp.n)], inc=[(stc, 16)])
                for (wT3, dsb, dsb3, out_fn) in ((WkT3, ksb, ksb3, k_out), (WvT3, vsb2, vsb3, v_out)):
                    for tt in range(8):
                        for ch in range(2):
                            def kmm(q, b=b, tt=tt, ch=ch, wT3=wT3):
                                for ct in range(8):
                                    ins = q.matmul(pq[ch][:], xn1T3[:, b * 8 + ct, tt * 128:(tt + 1) * 128],
                                                   wT3[:, ct, ch * 512:(ch + 1) * 512],
                                                   start=(ct == 0), stop=(ct == 7))
                                return ins
                            ctx.do("tensor", kmm, wait=[(vp, vp.n), (sp, sp.n), (stc, stc.n)], inc=[(tp, 1)])
                            ctx.do("vector", lambda q, tt=tt, ch=ch, dsb3=dsb3: q.tensor_copy(
                                dsb3[:, tt, ch * 512:(ch + 1) * 512], pq[ch][:]),
                                wait=[(tp, tp.n)], inc=[(vp, 1)])
                    ctx.do("sync", lambda q, dsb=dsb, out_fn=out_fn, b=b: q.dma_start(out=out_fn(b), in_=dsb[:]),
                           wait=[(vp, vp.n)], inc=[(stc, 16)])
                for tt in range(8):
                    def trv(q, tt=tt):
                        for dt in range(8):
                            ins = q.transpose(ptrb[:, dt * 128:(dt + 1) * 128],
                                              vsb3[:, tt, dt * 128:(dt + 1) * 128], sid[:])
                        return ins
                    ctx.do("tensor", trv, wait=[(vp, vp.n)], inc=[(tp, 1)])
                    ctx.do("vector", lambda q, tt=tt: q.tensor_reduce(
                        sbsd[:].rearrange("p (dt j) -> p dt j", j=8)[:, :, tt:tt + 1],
                        ptrb[:].rearrange("p (dt c) -> p dt c", c=128), axis=AX.X, op=ALU.add),
                        wait=[(tp, tp.n)], inc=[(vp, 1)])
                v_bs = vp.n
                for dt in range(8):
                    ctx.do("tensor", lambda q, dt=dt: q.transpose(
                        pbsj[:], sbsd[:, dt * 8:(dt + 1) * 8], sidf[:]),
                        wait=[(vp, v_bs)] if dt == 0 else [(vp, vp.n)], inc=[(tp, 1)])
                    ctx.do("vector", lambda q, dt=dt: q.tensor_copy(
                        sbsj[:, dt * 128:(dt + 1) * 128], pbsj[:]),
                        wait=[(tp, tp.n)], inc=[(vp, 1)])
                ctx.do("sync", lambda q, b=b: q.dma_start(out=bs_out(b), in_=sbsj[:]),
                       wait=[(vp, vp.n)], inc=[(stc, 16)])
                for ch in range(2):
                    def rmm(q, b=b, ch=ch):
                        for ct in range(8):
                            ins = q.matmul(pq[ch][0:16, :], swred3[:, ct, :],
                                           xn1T3[:, b * 8 + ct, ch * 512:(ch + 1) * 512],
                                           start=(ct == 0), stop=(ct == 7))
                        return ins
                    ctx.do("tensor", rmm, wait=[(vp, vp.n)], inc=[(tp, 1)])
                    ctx.do("vector", lambda q, ch=ch: q.tensor_copy(
                        redT[:, ch * 512:(ch + 1) * 512], pq[ch][0:16, :]),
                        wait=[(tp, tp.n)], inc=[(vp, 1)])
                v_red = vp.n
                for tt in range(8):
                    ctx.do("tensor", lambda q, tt=tt: q.matmul(
                        pl[:], redT[:, tt * 128:(tt + 1) * 128], swgnT[:], start=True, stop=True),
                        wait=[(vp, max(v_red, v_wgnT)), (vp, vp.n)], inc=[(tp, 1)])
                    t_l = tp.n
                    ctx.do("vector", lambda q: q.tensor_copy(slog[:], pl[:]), wait=[(tp, t_l), (vp, vp.n)],
                           inc=[(vp, 1)])
                    ctx.do("vector", lambda q: q.reduce_max(sm[:], slog[:], axis=AX.X),
                           wait=[(vp, vp.n)], inc=[(vp, 1)])
                    ctx.do("vector", lambda q: q.tensor_scalar(snm[:], sm[:], -1.0, None, op0=ALU.mult),
                           wait=[(vp, vp.n)], inc=[(vp, 1)])
                    v_nm = vp.n
                    ctx.do("scalar", lambda q: q.activation(sexp[:], slog[:], AF.Exp, bias=snm[:, 0:1]),
                           wait=[(vp, v_nm), (sp, sp.n)], inc=[(sp, 1)])
                    s_e = sp.n
                    ctx.do("vector", lambda q: q.reduce_sum(ssm[:], sexp[:], axis=AX.X),
                           wait=[(sp, s_e), (vp, vp.n)], inc=[(vp, 1)])
                    ctx.do("vector", lambda q: q.reciprocal(ssc[:], ssm[:]), wait=[(vp, vp.n)], inc=[(vp, 1)])
                    ctx.do("vector", lambda q: q.tensor_scalar(sind[:], slog[:], sm[:, 0:1], None, op0=ALU.is_equal),
                           wait=[(vp, vp.n)], inc=[(vp, 1)])
                    ctx.do("vector", lambda q, b=b, tt=tt: q.tensor_scalar(
                        gsc[:, (b * 8 + tt) * 8:(b * 8 + tt + 1) * 8], sind[:], ssc[:, 0:1], None, op0=ALU.mult),
                        wait=[(vp, vp.n)], inc=[(vp, 1)])
            ctx.do("sync", lambda q: q.dma_start(out=gsc_sp, in_=gsc[:]), wait=[(vp, vp.n)], inc=[(stc, 16)])
            ctx.wait("gpsimd", stc, stc.n)
            hooks.get("tail", lambda c: None)(ctx)

        prog.build(phase)


def build_L3():
    nc = bass.Bass("TRN2", target_bir_lowering=False)
    qTn_sp = nc.dram_tensor("qTn_sp", [B, 128, 8 * 1024], BF16, kind="ExternalOutput").ap()
    k_sp = nc.dram_tensor("k_sp", [B, 128, 8 * 1024], BF16, kind="ExternalOutput").ap()
    v_sp = nc.dram_tensor("v_sp", [B, 128, 8 * 1024], BF16, kind="ExternalOutput").ap()
    bs_sp = nc.dram_tensor("bs_sp", [B, 8, 1024], F32, kind="ExternalOutput").ap()
    io = dict(
        xn1T_in=nc.dram_tensor("xn1T_in", [128, 16 * 1024], BF16, kind="ExternalInput").ap(),
        Wq=nc.dram_tensor("Wq", [C, C], F32, kind="ExternalInput").ap(),
        Wk=nc.dram_tensor("Wk", [C, C], F32, kind="ExternalInput").ap(),
        Wv=nc.dram_tensor("Wv", [C, C], F32, kind="ExternalInput").ap(),
        w_red=nc.dram_tensor("w_red", [C, 16], F32, kind="ExternalInput").ap(),
        wg=nc.dram_tensor("wg", [8, 16], F32, kind="ExternalInput").ap(),
        id_bf=nc.dram_tensor("id_bf", [128, 128], BF16, kind="ExternalInput").ap(),
        id_f=nc.dram_tensor("id_f", [128, 128], F32, kind="ExternalInput").ap(),
        gsc_sp=nc.dram_tensor("gsc_sp", [128, 16 * 8], F32, kind="ExternalOutput").ap(),
        qT_out=lambda b: qTn_sp[b].rearrange("p (dt t) -> p dt t", t=1024),
        k_out=lambda b: k_sp[b],
        v_out=lambda b: v_sp[b],
        bs_out=lambda b: bs_sp[b],
    )
    es = ExitStack()
    with es:
        prog = Prog(nc, es)
        prog.prefix = "L3_"
        emit_L3(nc, prog, **io)
    return nc


def run_L3(inp, r2b):
    nc = build_L3()
    id_bf = np.eye(128, dtype=BF)
    id_f = np.eye(128, dtype=np.float32)
    in_maps = []
    for c in range(R):
        in_maps.append(dict(
            xn1T_in=r2b.results[c]["xn1T_sp"],
            Wq=inp["L1_attn_Wq"], Wk=inp["L1_attn_Wk"], Wv=inp["L1_attn_Wv"],
            w_red=inp["L1_ffn_w_red"], wg=inp["L1_ffn_wg"],
            id_bf=id_bf, id_f=id_f,
        ))
    return run_bass_kernel_spmd(nc, in_maps, core_ids=list(range(R)))





PAIR_STEPS = [16, 32, 48, 64]
STEP_OFF = [0, 16, 48, 96]


def emit_L4(nc, prog, *, kM_src, v_src, emit_pair_loads, emit_bs_load, at_dst,
            biasT, cm0, cm1, triu_b, ntriu_b, id_bf, hooks=None):
    hooks = hooks or {}
    es = ExitStack()
    with es:
        T_ = lambda nm, shp, dt: es.enter_context(nc.sbuf_tensor(prog.prefix + nm, shp, dt))
        P_ = lambda nm, shp, dt: es.enter_context(nc.psum_tensor(prog.prefix + nm, shp, dt))
        sid = T_("sid", [128, 128], BF16)
        striu = T_("striu", [128, 128], BF16)
        sntriu = T_("sntriu", [128, 128], BF16)
        sqt = T_("sqt", [128, 8 * 2 * 2 * 128], BF16)
        svd = T_("svd", [128, 2 * 2 * 1024], BF16)
        sbsj = T_("sbsj", [64, 2 * 1024], F32)
        scm0 = T_("scm0", [64, 8 * 128], F32)
        scm1 = T_("scm1", [64, 8 * 128], F32)
        smt = [T_(f"smt{i}", [128, 8 * 2 * 128], BF16) for i in range(2)]
        sbias = [T_(f"sbias{i}", [128, 256], BF16) for i in range(2)]
        svt = [T_(f"svt{i}", [128, 2 * 512], BF16) for i in range(2)]
        pcache = T_("pcache", [128, 72 * 256], BF16)
        sx1e = T_("sx1e", [128, 512], F32)
        so = T_("so", [128, 512], F32)

        pd = [P_(f"pd{i}", [128, 256], F32) for i in range(2)]
        pav = [P_(f"pav{i}", [128, 512], F32) for i in range(4)]
        px = [P_(f"px{i}", [128, 512], F32) for i in range(2)]

        sqt5 = sqt[:].rearrange("p (dt b k i) -> p dt b k i", b=2, k=2, i=128)
        svd4 = svd[:].rearrange("p (k b d) -> p k b d", k=2, b=2)
        sbsj3 = sbsj[:].rearrange("j (b d) -> j b d", b=2)
        scm03 = scm0[:].rearrange("j (o i) -> j o i", i=128)
        scm13 = scm1[:].rearrange("j (o i) -> j o i", i=128)
        pc3 = pcache[:].rearrange("p (s i) -> p s i", i=256)

        def phase(ctx: Ctx):
            hooks.get("head", lambda c: None)(ctx)
            ld = ctx.sem("ld")
            lm = ctx.sem("lm")
            lv = ctx.sem("lv")
            tp = ctx.sem("tp")
            vp = ctx.sem("vp")
            sp = ctx.sem("sp")
            stc = ctx.sem("stc")

            ctx.do("sync", lambda q: q.dma_start(out=sid[:], in_=id_bf), inc=[(ld, 16)])
            ctx.do("sync", lambda q: q.dma_start(out=striu[:], in_=triu_b), inc=[(ld, 16)])
            ctx.do("sync", lambda q: q.dma_start(out=sntriu[:], in_=ntriu_b), inc=[(ld, 16)])
            emit_bs_load(ctx, ld, sbsj)
            ctx.do("sync", lambda q: q.dma_start(out=scm0[:], in_=cm0), inc=[(ld, 16)])
            ctx.do("sync", lambda q: q.dma_start(out=scm1[:], in_=cm1), inc=[(ld, 16)])
            ld_const = ld.n

            for p in range(4):
                nsteps = PAIR_STEPS[p]
                soff = STEP_OFF[p]
                emit_pair_loads(ctx, ld, tp, p, sqt, svd)
                ld_pair = ld.n
                mm_done = {}
                for J in range(nsteps):
                    wbuf = [(tp, mm_done[J - 2])] if J - 2 in mm_done else [(tp, tp.n)]
                    for bb_ in range(2):
                        ctx.do("sync", lambda q, J=J, bb_=bb_: q.dma_start(
                            out=smt[J % 2][:].rearrange("p (dt b j) -> p dt b j", b=2, j=128)[:, :, bb_, :],
                            in_=kM_src(bb_, J)),
                            wait=wbuf if bb_ == 0 else [], inc=[(lm, 16)])
                    ctx.do("sync", lambda q, J=J, soff=soff: q.dma_start(
                        out=sbias[J % 2][:], in_=biasT[soff + J]), inc=[(lm, 16)])
                    ld_m = lm.n
                    def dmm(q, J=J):
                        sm3 = smt[J % 2][:].rearrange("p (dt b j) -> p dt b j", b=2, j=128)
                        q.matmul(pd[J % 2][:], sid[:], sbias[J % 2][:], start=True, stop=False)
                        for b in range(2):
                            for dt in range(8):
                                ins = q.matmul(pd[J % 2][:], sm3[:, dt, b, :],
                                               sqt[:, dt * 512 + b * 256: dt * 512 + (b + 1) * 256],
                                               start=False, stop=(b == 1 and dt == 7))
                        return ins
                    wd = [(lm, ld_m), (ld, ld_pair)]
                    if J < 2:
                        wd.append((sp, sp.n))
                    ctx.do("tensor", dmm, wait=wd, inc=[(tp, 1)])
                    mm_done[J] = tp.n
                    t_d = tp.n
                    ctx.do("scalar", lambda q, J=J, soff=soff: q.activation(
                        pc3[:, J, :], pd[J % 2][:], AF.Sigmoid, scale=1.0 / 32.0),
                        wait=[(tp, t_d), (sp, sp.n)], inc=[(sp, 1)])
                s_pc = sp.n
                for dh in range(2):
                    av_done = {}
                    for J in range(nsteps):
                        wbuf = [(tp, av_done[J - 2])] if J - 2 in av_done else [(tp, tp.n)]
                        for bb_ in range(2):
                            ctx.do("sync", lambda q, J=J, dh=dh, bb_=bb_: q.dma_start(
                                out=svt[J % 2][:, bb_ * 512:(bb_ + 1) * 512],
                                in_=v_src(bb_, J, dh)),
                                wait=wbuf if bb_ == 0 else [], inc=[(lv, 16)])
                        ld_v = lv.n
                        def avmm(q, J=J, dh=dh, nsteps=nsteps):
                            sv3 = svt[J % 2][:].rearrange("p (b d) -> p b d", b=2)
                            pa_ = pc3[:, J, 0:128]
                            pb_ = pc3[:, J, 128:256]
                            st = (J == 0)
                            last = (J == nsteps - 1)
                            q.matmul(pav[0][:], pa_, sv3[:, 0, :], start=st, stop=False)
                            q.matmul(pav[1][:], pa_, sv3[:, 1, :], start=st, stop=last)
                            q.matmul(pav[2][:], pb_, sv3[:, 0, :], start=st, stop=False)
                            return q.matmul(pav[3][:], pb_, sv3[:, 1, :], start=st, stop=last)
                        ctx.do("tensor", avmm,
                               wait=[(lv, ld_v)] + ([(sp, s_pc), (vp, vp.n)] if J == 0 else []),
                               inc=[(tp, 1)])
                        av_done[J] = tp.n
                    def sufmm(q, p=p, dh=dh):
                        for ki, blk in enumerate(range(2)):
                            o = 2 * p + blk
                            q.matmul(pav[2 * blk][:], scm03[:, o, :], sbsj3[:, 0, dh * 512:(dh + 1) * 512],
                                     start=False, stop=False)
                            q.matmul(pav[2 * blk][:], sntriu[:], svd4[:, blk, 0, dh * 512:(dh + 1) * 512],
                                     start=False, stop=True)
                            q.matmul(px[blk][:], scm13[:, o, :], sbsj3[:, 1, dh * 512:(dh + 1) * 512],
                                     start=True, stop=False)
                            q.matmul(px[blk][:], striu[:], svd4[:, blk, 1, dh * 512:(dh + 1) * 512],
                                     start=False, stop=False)
                            ins = q.matmul(px[blk][:], sntriu[:], svd4[:, blk, 1, dh * 512:(dh + 1) * 512],
                                           start=False, stop=False)
                        return ins
                    ctx.do("tensor", sufmm, wait=[(ld, ld_const), (sp, sp.n), (vp, vp.n)], inc=[(tp, 1)])
                    t_suf = tp.n
                    for blk in range(2):
                        ctx.do("vector", lambda q, blk=blk: q.tensor_copy(so[:], pav[2 * blk][:]),
                               wait=[(tp, t_suf), (vp, vp.n), (stc, stc.n)], inc=[(vp, 1)])
                        v_o0 = vp.n
                        ctx.do("sync", lambda q, p=p, blk=blk, dh=dh: q.dma_start(
                            out=at_dst(2 * p + blk, 0, dh), in_=so[:]),
                            wait=[(vp, v_o0)], inc=[(stc, 16)])
                        ctx.do("scalar", lambda q, blk=blk: q.activation(sx1e[:], px[blk][:], AF.Identity),
                               wait=[(tp, t_suf), (sp, sp.n), (stc, stc.n)], inc=[(sp, 1)])
                        s_x1 = sp.n
                        ctx.do("vector", lambda q, blk=blk: q.tensor_tensor(
                            so[:], sx1e[:], pav[2 * blk + 1][:], op=ALU.subtract),
                            wait=[(sp, s_x1), (vp, vp.n), (stc, stc.n)], inc=[(vp, 1)])
                        v_o1 = vp.n
                        ctx.do("sync", lambda q, p=p, blk=blk, dh=dh: q.dma_start(
                            out=at_dst(2 * p + blk, 1, dh), in_=so[:]),
                            wait=[(vp, v_o1)], inc=[(stc, 16)])
            ctx.wait("gpsimd", stc, stc.n)
            hooks.get("tail", lambda c: None)(ctx)

        prog.build(phase)


def build_L4():
    nc = bass.Bass("TRN2", target_bir_lowering=False)
    kM = nc.dram_tensor("kM", [B, 1024, 8192], BF16, kind="ExternalInput").ap()
    vf = nc.dram_tensor("vf", [B, 8192, 1024], BF16, kind="ExternalInput").ap()
    qTb_in = nc.dram_tensor("qTb_in", [4, 128, 4096], BF16, kind="ExternalInput").ap()
    vdiag_in = nc.dram_tensor("vdiag_in", [4, 128, 4096], BF16, kind="ExternalInput").ap()
    bsJ = nc.dram_tensor("bsJ", [64, 2048], F32, kind="ExternalInput").ap()
    attn1 = nc.dram_tensor("attn1", [8, B, 128, 1024], F32, kind="ExternalOutput").ap()
    io = dict(
        biasT=nc.dram_tensor("biasT", [160, 128, 256], BF16, kind="ExternalInput").ap(),
        cm0=nc.dram_tensor("cm0", [64, 1024], F32, kind="ExternalInput").ap(),
        cm1=nc.dram_tensor("cm1", [64, 1024], F32, kind="ExternalInput").ap(),
        triu_b=nc.dram_tensor("triu_b", [128, 128], BF16, kind="ExternalInput").ap(),
        ntriu_b=nc.dram_tensor("ntriu_b", [128, 128], BF16, kind="ExternalInput").ap(),
        id_bf=nc.dram_tensor("id_bf", [128, 128], BF16, kind="ExternalInput").ap(),
        kM_src=lambda bb, J: kM[bb, :, J * 128:(J + 1) * 128].rearrange("(dt p) j -> p dt j", p=128),
        v_src=lambda bb, J, dh: vf[bb, J * 128:(J + 1) * 128, dh * 512:(dh + 1) * 512],
        at_dst=lambda o, b, dh: attn1[o, b, :, dh * 512:(dh + 1) * 512],
    )

    def emit_pair_loads(ctx, ld, tp, p, sqt, svd):
        ctx.do("sync", lambda q, p=p: q.dma_start(out=sqt[:], in_=qTb_in[p]),
               wait=[(tp, tp.n)], inc=[(ld, 16)])
        ctx.do("sync", lambda q, p=p: q.dma_start(out=svd[:], in_=vdiag_in[p]), inc=[(ld, 16)])

    def emit_bs_load(ctx, ld, sbsj):
        ctx.do("sync", lambda q: q.dma_start(out=sbsj[:], in_=bsJ), inc=[(ld, 16)])

    io["emit_pair_loads"] = emit_pair_loads
    io["emit_bs_load"] = emit_bs_load
    es = ExitStack()
    with es:
        prog = Prog(nc, es)
        prog.prefix = "L4_"
        emit_L4(nc, prog, **io)
    return nc


def run_L4(inp, r3):
    nc = build_L4()
    id_bf = np.eye(128, dtype=BF)
    triu = np.triu(np.ones((128, 128), np.float32))
    triu_b = triu.astype(BF)
    ntriu_b = (np.float32(-1e-9) * triu).astype(BF)
    kM = np.stack([np.concatenate(
        [r3.results[c]["k_sp"][b].reshape(128, 8, 1024).transpose(1, 0, 2).reshape(1024, 1024)
         for c in range(R)], axis=0).reshape(1024, 8192) for b in range(B)])
    vfull = np.stack([np.concatenate(
        [r3.results[c]["v_sp"][b].reshape(128, 8, 1024).transpose(1, 0, 2).reshape(1024, 1024)
         for c in range(R)], axis=0) for b in range(B)])
    qT = np.stack([np.concatenate(
        [r3.results[c]["qTn_sp"][b].reshape(128, 8, 1024) for c in range(R)], axis=2)
        for b in range(B)])
    bsJ_ = np.stack([np.concatenate([r3.results[c]["bs_sp"][b] for c in range(R)], axis=0)
                     for b in range(B)])
    bsJ = np.ascontiguousarray(bsJ_.transpose(1, 0, 2).reshape(64, 2048))
    in_maps = []
    for c in range(R):
        blks = blocks_for_core(c)
        qTb6 = np.zeros((4, 128, 8, 2, 2, 128), BF)
        vdg4 = np.zeros((4, 128, 2, 2, 1024), BF)
        for p in range(4):
            for blk in range(2):
                g = blks[2 * p + blk]
                for b in range(B):
                    qTb6[p, :, :, b, blk, :] = qT[b][:, :, g * 128:(g + 1) * 128]
                    vdg4[p, :, blk, b, :] = vfull[b][g * 128:(g + 1) * 128, :]
        qTb = qTb6.reshape(4, 128, 4096)
        vdg = vdg4.reshape(4, 128, 4096)
        biasT = np.zeros((160, 128, 256), np.float32)
        for p in range(4):
            for J in range(PAIR_STEPS[p]):
                for blk in range(2):
                    g = blks[2 * p + blk]
                    t = biasT[STEP_OFF[p] + J, :, blk * 128:(blk + 1) * 128]
                    if J > g:
                        t[:] = -1e9
                    elif J == g:
                        t[:] = np.where(np.arange(128)[:, None] > np.arange(128)[None, :], -1e9, 0.0)
        cm0_ = np.zeros((8, 64, 128), np.float32)
        cm1_ = np.zeros((8, 64, 128), np.float32)
        for o in range(8):
            g = blks[o]
            cm0_[o, g:, :] = np.float32(1e-9)
            cm1_[o, :g, :] = 1.0
            cm1_[o, g:, :] = np.float32(1e-9)
        cm0 = np.ascontiguousarray(cm0_.transpose(1, 0, 2).reshape(64, 1024))
        cm1 = np.ascontiguousarray(cm1_.transpose(1, 0, 2).reshape(64, 1024))
        in_maps.append(dict(
            kM=kM, vf=vfull,
            qTb_in=qTb, vdiag_in=vdg, bsJ=bsJ,
            biasT=biasT.astype(BF), cm0=cm0, cm1=cm1,
            triu_b=triu_b, ntriu_b=ntriu_b, id_bf=id_bf,
        ))
    return run_bass_kernel_spmd(nc, in_maps, core_ids=list(range(R)))





def emit_L5(nc, prog, *, xn1T_in, gsc_in, x1_in, at_batch, w1, b1h, w2, b2r,
            out_sl, acc, hooks=None):
    hooks = hooks or {}
    es = ExitStack()
    with es:
        T_ = lambda nm, shp, dt: es.enter_context(nc.sbuf_tensor(prog.prefix + nm, shp, dt))
        P_ = lambda nm, shp, dt: es.enter_context(nc.psum_tensor(prog.prefix + nm, shp, dt))
        xn1T = T_("xn1T", [128, 16 * 1024], BF16)
        gsc = T_("gsc", [128, 16 * 8], F32)
        w1sb = T_("w1sb", [128, 8 * 4096], BF16)
        w2sb = T_("w2sb", [128, 32 * 1024], BF16)
        sb1 = T_("sb1", [128, 8 * 32], F32)
        sb2 = T_("sb2", [1, 8 * 1024], BF16)
        ones1 = T_("ones1", [1, 128], BF16)
        hT = T_("hT", [128, 32 * 256], BF16)
        sa = T_("sa", [128, 1024], F32)
        sstg = T_("sstg", [128, 2048], F32)
        gy = [T_("gy0", [128, 512], F32)]

        ph = [P_(f"ph{i}", [128, 256], F32) for i in range(2)]
        py = [P_(f"py{i}", [128, 512], F32) for i in range(4)]

        xn1T3 = xn1T[:].rearrange("p (g c) -> p g c", c=1024)
        w1sb3 = w1sb[:].rearrange("p (dt f) -> p dt f", f=4096)
        w2sb3 = w2sb[:].rearrange("p (ft d) -> p ft d", d=1024)
        hT3 = hT[:].rearrange("p (ft t) -> p ft t", t=256)

        def phase(ctx: Ctx):
            hooks.get("head", lambda c: None)(ctx)
            ld = ctx.sem("ld")
            lw = ctx.sem("lw")
            tp = ctx.sem("tp")
            vp = ctx.sem("vp")
            sp = ctx.sem("sp")
            stc = ctx.sem("stc")

            ctx.do("sync", lambda q: q.dma_start(out=xn1T[:], in_=xn1T_in), inc=[(ld, 16)])
            ctx.do("sync", lambda q: q.dma_start(out=gsc[:], in_=gsc_in), inc=[(ld, 16)])
            ctx.do("sync", lambda q: q.dma_start(out=sb1[:], in_=b1h), inc=[(ld, 16)])
            ctx.do("gpsimd", lambda q: q.dma_start(out=sb2[:], in_=b2r), inc=[(ld, 16)])
            ctx.do("vector", lambda q: q.memset(ones1[:], 1.0), inc=[(vp, 1)])
            ld_c = ld.n
            v_ones = vp.n
            sstg3 = sstg[:].rearrange("p (l c) -> p l c", c=1024)
            for b in range(B):
                for half in range(4):
                    _eng = "sync" if (b * 4 + half) % 2 == 0 else "scalar"
                    ctx.do(_eng, lambda q, b=b, half=half: q.dma_start(
                        out=sstg3, in_=at_batch(q, b, half)),
                        wait=[(stc, stc.n)], inc=[(ld, 16)])
                    ld_h = ld.n
                    for li in range(2):
                        tt = half * 2 + li
                        ctx.do("sync", lambda q, b=b, tt=tt: q.dma_start(
                            out=sa[:], in_=x1_in[b, tt * 128:(tt + 1) * 128, :]),
                            wait=[(stc, stc.n)], inc=[(ld, 16)])
                        ld_i = ld.n
                        ctx.do("vector", lambda q, li=li: q.tensor_tensor(
                            sa[:], sa[:], sstg3[:, li, :], op=ALU.add),
                            wait=[(ld, max(ld_i, ld_h)), (vp, vp.n)], inc=[(vp, 1)])
                        ctx.do("sync", lambda q, b=b, tt=tt: q.dma_start(
                            out=acc[(b * 8 + tt) * 128:(b * 8 + tt + 1) * 128, :], in_=sa[:]),
                            wait=[(vp, vp.n)], inc=[(stc, 16)])
            reg_last = {i: stc.n for i in range(16)}
            gy_free = {}
            for e in range(8):
                ctx.do("gpsimd", lambda q, e=e: q.dma_start(
                    out=w1sb3, in_=w1[e].rearrange("(dt p) f -> p dt f", p=128)),
                    wait=[(tp, tp.n), (ld, ld_c)], inc=[(lw, 16)])
                ctx.do("gpsimd", lambda q, e=e: q.dma_start(
                    out=w2sb3, in_=w2[e].rearrange("(ft p) d -> p ft d", p=128)),
                    inc=[(lw, 16)])
                lw_e = lw.n
                for blk in range(8):
                    b, w_ = divmod(blk, 4)
                    t0 = w_ * 256
                    gel_done = {}
                    for ft in range(32):
                        def hmm(q, e=e, b=b, t0=t0, ft=ft):
                            for ct in range(8):
                                ins = q.matmul(ph[ft % 2][:], w1sb3[:, ct, ft * 128:(ft + 1) * 128],
                                               xn1T3[:, b * 8 + ct, t0:t0 + 256],
                                               start=(ct == 0), stop=(ct == 7))
                            return ins
                        wh = [(lw, lw_e)]
                        if ft - 2 in gel_done:
                            wh.append((sp, gel_done[ft - 2]))
                        elif ft < 2:
                            wh.append((sp, sp.n))
                        ctx.do("tensor", hmm, wait=wh, inc=[(tp, 1)])
                        t_h = tp.n
                        ctx.do("scalar", lambda q, e=e, ft=ft: q.activation(
                            hT3[:, ft, :], ph[ft % 2][:], AF.Gelu, bias=sb1[:, e * 32 + ft:e * 32 + ft + 1]),
                            wait=[(tp, t_h), (sp, sp.n)], inc=[(sp, 1)])
                        gel_done[ft] = sp.n
                    s_h = sp.n
                    for ts_ in range(2):
                        for dc in range(2):
                            pidx = ts_ * 2 + dc
                            def ymm(q, e=e, ts_=ts_, dc=dc, pidx=pidx):
                                q.matmul(py[pidx][:], ones1[:], sb2[0:1, e * 1024 + dc * 512:e * 1024 + (dc + 1) * 512],
                                         start=True, stop=False)
                                for ft in range(32):
                                    ins = q.matmul(py[pidx][:], hT3[:, ft, ts_ * 128:(ts_ + 1) * 128],
                                                   w2sb3[:, ft, dc * 512:(dc + 1) * 512],
                                                   start=False, stop=(ft == 31))
                                return ins
                            wy = [(sp, s_h), (vp, vp.n)]
                            ctx.do("tensor", ymm, wait=wy, inc=[(tp, 1)])
                            t_y = tp.n
                            tt = w_ * 2 + ts_
                            gcol = (b * 8 + tt) * 8 + e
                            gidx = 0
                            wv_ = [(tp, t_y), (vp, vp.n)]
                            if gidx in gy_free:
                                wv_.append((stc, gy_free[gidx]))
                            ctx.do("vector", lambda q, gidx=gidx, pidx=pidx, gcol=gcol: q.tensor_scalar(
                                gy[gidx][:], py[pidx][:], gsc[:, gcol:gcol + 1], None, op0=ALU.mult),
                                wait=wv_, inc=[(vp, 1)])
                            v_gy = vp.n
                            reg = b * 8 + tt
                            ctx.do("gpsimd", lambda q, gidx=gidx, reg=reg, dc=dc: q.dma_start(
                                out=acc[reg * 128:(reg + 1) * 128, dc * 512:(dc + 1) * 512],
                                in_=gy[gidx][:], accum_op=ALU.add),
                                wait=[(vp, v_gy), (stc, reg_last[reg])], inc=[(stc, 16)])
                            reg_last[reg] = stc.n
                            gy_free[gidx] = stc.n
            fin = max(reg_last.values())
            ctx.do("sync", lambda q: q.dma_start(
                out=out_sl.rearrange("b t c -> (b t) c"), in_=acc), wait=[(stc, fin)], inc=[(stc, 16)])
            ctx.wait("gpsimd", stc, stc.n)
            hooks.get("tail", lambda c: None)(ctx)

        prog.build(phase)


def build_L5():
    nc = bass.Bass("TRN2", target_bir_lowering=False)
    at_in = nc.dram_tensor("at_in", [B, TC, C], F32, kind="ExternalInput").ap()
    io = dict(
        xn1T_in=nc.dram_tensor("xn1T_in", [128, 16 * 1024], BF16, kind="ExternalInput").ap(),
        gsc_in=nc.dram_tensor("gsc_in", [128, 16 * 8], F32, kind="ExternalInput").ap(),
        x1_in=nc.dram_tensor("x1_in", [B, TC, C], F32, kind="ExternalInput").ap(),
        w1=nc.dram_tensor("w1", [8, C, 4096], F32, kind="ExternalInput").ap(),
        b1h=nc.dram_tensor("b1h", [128, 8 * 32], F32, kind="ExternalInput").ap(),
        w2=nc.dram_tensor("w2", [8, 4096, C], F32, kind="ExternalInput").ap(),
        b2r=nc.dram_tensor("b2r", [1, 8 * 1024], F32, kind="ExternalInput").ap(),
        out_sl=nc.dram_tensor("out_sl", [B, TC, C], F32, kind="ExternalOutput").ap(),
        acc=nc.dram_tensor("acc", [B * TC, C], F32).ap(),
        at_batch=lambda q, b, half: at_in[b, half * 256:(half + 1) * 256, :]
            .rearrange("(l p) c -> p l c", p=128),
    )
    es = ExitStack()
    with es:
        prog = Prog(nc, es)
        prog.prefix = "L5_"
        emit_L5(nc, prog, **io)
    return nc


def run_L5(inp, r2b, r3, r4):
    nc = build_L5()
    at_full = np.zeros((B, T, C), np.float32)
    for cp in range(R):
        blks = blocks_for_core(cp)
        for o, g in enumerate(blks):
            for b in range(B):
                at_full[b, g * 128:(g + 1) * 128] = r4.results[cp]["attn1"][o, b]
    b1h = np.ascontiguousarray(inp["L1_ffn_b1"].reshape(8, 32, 128).transpose(2, 0, 1).reshape(128, 256))
    b2r = np.ascontiguousarray(inp["L1_ffn_b2"].reshape(1, 8 * 1024))
    in_maps = []
    for c in range(R):
        in_maps.append(dict(
            xn1T_in=r2b.results[c]["xn1T_sp"],
            gsc_in=r3.results[c]["gsc_sp"],
            x1_in=r2b.results[c]["x1_out"],
            at_in=np.ascontiguousarray(at_full[:, c * TC:(c + 1) * TC, :]),
            w1=inp["L1_ffn_w1"], b1h=b1h, w2=inp["L1_ffn_w2"], b2r=b2r,
        ))
    return run_bass_kernel_spmd(nc, in_maps, core_ids=list(range(R)))


def kernel_impl(inputs):
    r1 = run_L1(inputs)
    r2a = run_L2a(inputs, r1)
    r2b = run_L2b(inputs, r1, r2a)
    r3 = run_L3(inputs, r2b)
    r4 = run_L4(inputs, r3)
    r5 = run_L5(inputs, r2b, r3, r4)
    out = np.zeros((B, T, C), np.float32)
    for c in range(R):
        out[:, c * TC:(c + 1) * TC, :] = r5.results[c]["out_sl"]
    return out





def emit_L3T(nc, prog, *, k_in, kT_out, hooks=None):
    hooks = hooks or {}

    def phase(ctx):
        hooks.get("head", lambda c: None)(ctx)
        stc = ctx.sem("stc")
        src5 = k_in.rearrange("(b2 u s) (tt d) -> b2 s tt u d", b2=2, s=8, d=1024)
        dst5 = kT_out.rearrange("(b2 tt u) (s d) -> b2 s tt u d", b2=2, u=16, d=1024)
        for b in range(2):
            for s in range(8):
                ctx.do("sync", lambda q, b=b, s=s: q.dma_start(
                    out=dst5[b, s], in_=src5[b, s]), inc=[(stc, 16)])
        ctx.wait("gpsimd", stc, stc.n)
        hooks.get("tail", lambda c: None)(ctx)

    prog.build(phase)


def build_FUSED():
    nc = bass.Bass("TRN2", target_bir_lowering=False)
    ET = mybir.EngineType

    xin = dict(
        x_sl=nc.dram_tensor("x_sl", [B, TC, C], F32, kind="ExternalInput").ap(),
        x_halo=nc.dram_tensor("x_halo", [B, C], F32, kind="ExternalInput").ap(),
        w_qkv=nc.dram_tensor("w_qkv", [3 * C, C], F32, kind="ExternalInput").ap(),
        wk_map=nc.dram_tensor("wk_map", [64, 64], F32, kind="ExternalInput").ap(),
        bk_map=nc.dram_tensor("bk_map", [1, 64], F32, kind="ExternalInput").ap(),
        wq_map=nc.dram_tensor("wq_map", [64, 64], F32, kind="ExternalInput").ap(),
        bq_map=nc.dram_tensor("bq_map", [1, 64], F32, kind="ExternalInput").ap(),
        id_bf=nc.dram_tensor("id_bf", [128, 128], BF16, kind="ExternalInput").ap(),
        id_f=nc.dram_tensor("id_f", [128, 128], F32, kind="ExternalInput").ap(),
        triu_f=nc.dram_tensor("triu_f", [128, 128], F32, kind="ExternalInput").ap(),
        triu_b=nc.dram_tensor("triu_b", [128, 128], BF16, kind="ExternalInput").ap(),
        ntriu_b=nc.dram_tensor("ntriu_b", [128, 128], BF16, kind="ExternalInput").ap(),
        prefw=nc.dram_tensor("prefw", [128, 8], F32, kind="ExternalInput").ap(),
        W_r=nc.dram_tensor("W_r", [C, C], F32, kind="ExternalInput").ap(),
        W_k=nc.dram_tensor("W_k", [C, C], F32, kind="ExternalInput").ap(),
        W_v=nc.dram_tensor("W_v", [C, C], F32, kind="ExternalInput").ap(),
        w_out=nc.dram_tensor("w_out", [C, C], F32, kind="ExternalInput").ap(),
        mu_r=nc.dram_tensor("mu_r", [128, 8], F32, kind="ExternalInput").ap(),
        mu_k=nc.dram_tensor("mu_k", [128, 8], F32, kind="ExternalInput").ap(),
        Wq=nc.dram_tensor("Wq", [C, C], F32, kind="ExternalInput").ap(),
        Wk=nc.dram_tensor("Wk", [C, C], F32, kind="ExternalInput").ap(),
        Wv=nc.dram_tensor("Wv", [C, C], F32, kind="ExternalInput").ap(),
        w_red=nc.dram_tensor("w_red", [C, 16], F32, kind="ExternalInput").ap(),
        wg=nc.dram_tensor("wg", [8, 16], F32, kind="ExternalInput").ap(),
        biasT=nc.dram_tensor("biasT", [160, 128, 256], BF16, kind="ExternalInput").ap(),
        cm0=nc.dram_tensor("cm0", [64, 1024], F32, kind="ExternalInput").ap(),
        cm1=nc.dram_tensor("cm1", [64, 1024], F32, kind="ExternalInput").ap(),
        w1=nc.dram_tensor("w1", [8, C, 4096], F32, kind="ExternalInput").ap(),
        b1h=nc.dram_tensor("b1h", [128, 8 * 32], F32, kind="ExternalInput").ap(),
        w2=nc.dram_tensor("w2", [8, 4096, C], F32, kind="ExternalInput").ap(),
        b2r=nc.dram_tensor("b2r", [1, 8 * 1024], F32, kind="ExternalInput").ap(),
    )
    out_sl = nc.dram_tensor("out_sl", [B, TC, C], F32, kind="ExternalOutput").ap()

    xnT_sp = nc.dram_tensor("xnT_sp", [128, 16 * 1025], BF16).ap()
    kf_sp = nc.dram_tensor("kf_sp", [128, 16 * 2048], BF16).ap()
    v_sp = nc.dram_tensor("v_sp", [128, 16 * 1024], BF16).ap()
    hx_scr = nc.dram_tensor("hx_scr", [B, 1024], BF16).ap()
    st_in = nc.dram_tensor("st_in", [128, 32 * 65], F32)
    st_out = nc.dram_tensor("st_out", [8 * 128, 32 * 65], F32)
    y_sp = nc.dram_tensor("y_sp", [B, 128, 8 * 1024], BF16).ap()
    x1_sp = nc.dram_tensor("x1_sp", [B, TC, C], F32).ap()
    xn1T_sp = nc.dram_tensor("xn1T_sp", [128, 16 * 1024], BF16).ap()
    gsc_sp = nc.dram_tensor("gsc_sp", [128, 16 * 8], F32).ap()
    k_scr = nc.dram_tensor("k_scr", [256, 8192], BF16)
    ag1_in = nc.dram_tensor("ag1_in", [512, 8192], BF16)
    ag1_out = nc.dram_tensor("ag1_out", [8 * 512, 8192], BF16)
    agq_in = nc.dram_tensor("agq_in", [128, 16384], BF16)
    agq_out = nc.dram_tensor("agq_out", [8 * 128, 16384], BF16)
    agbs_in = nc.dram_tensor("agbs_in", [16, 1024], F32)
    agbs_out = nc.dram_tensor("agbs_out", [128, 1024], F32)
    ag3_in = nc.dram_tensor("ag3_in", [2048, 1024], F32)
    ag3_out = nc.dram_tensor("ag3_out", [8 * 2048, 1024], F32)
    acc = nc.dram_tensor("acc", [B * TC, C], F32).ap()

    RG = [list(range(R))]

    es = ExitStack()
    with es:
        bars = [es.enter_context(nc.semaphore(f"bar{k}")) for k in range(7)]
        ccs = [es.enter_context(nc.semaphore(f"cc{k}")) for k in range(3)]
        prog = Prog(nc, es)

        def head(k):
            def h(ctx):
                for e in ENGINES:
                    ctx.wait_h(e, bars[k], 1)
            return h

        def bar_inc(ctx, k):
            ctx.on("gpsimd", lambda q: q.nop(), inc=[(bars[k], 1)])

        def tail1(ctx):
            ctx.on("gpsimd", lambda q: q.collective_compute(
                "AllGather", ALU.bypass, replica_groups=RG,
                ins=[st_in.ap().opt()], outs=[st_out.ap().opt()]), inc=[(ccs[0], 1)])
            ctx.wait_h("gpsimd", ccs[0], 1)
            bar_inc(ctx, 1)
        prog.prefix = "L1_"
        emit_L1(nc, prog, x_sl=xin["x_sl"], x_halo=xin["x_halo"], w_qkv=xin["w_qkv"],
                wk_map=xin["wk_map"], bk_map=xin["bk_map"], id_bf=xin["id_bf"],
                xnT_sp=xnT_sp, kf_sp=kf_sp, v_sp=v_sp, states=st_in.ap(), hx_scr=hx_scr,
                hooks={"tail": tail1})

        prog.prefix = "L2a_"
        emit_L2a(nc, prog, xnT_in=xnT_sp, kf_in=kf_sp, v_in=v_sp,
                 states_r=lambda r_: st_out.ap()[128 * r_:128 * (r_ + 1), :],
                 prefw=xin["prefw"], w_qkv=xin["w_qkv"], wq_map=xin["wq_map"],
                 bq_map=xin["bq_map"], id_bf=xin["id_bf"], triu_f=xin["triu_f"], y_sp=y_sp,
                 hooks={"head": head(1), "tail": lambda c: bar_inc(c, 2)})

        prog.prefix = "L2b_"
        emit_L2b(nc, prog, xnT_in=xnT_sp, y_in=y_sp, x_sl=xin["x_sl"], W_r=xin["W_r"],
                 W_k=xin["W_k"], W_v=xin["W_v"], w_out=xin["w_out"], mu_r=xin["mu_r"],
                 mu_k=xin["mu_k"], id_bf=xin["id_bf"], x1_out=x1_sp, xn1T_sp=xn1T_sp,
                 hooks={"head": head(2), "tail": lambda c: bar_inc(c, 3)})

        def tail3(ctx):
            ctx.on("gpsimd", lambda q: q.collective_compute(
                "AllGather", ALU.bypass, replica_groups=RG,
                ins=[agq_in.ap().opt()], outs=[agq_out.ap().opt()]), inc=[(ccs[1], 1)])
            ctx.on("gpsimd", lambda q: q.collective_compute(
                "AllGather", ALU.bypass, replica_groups=RG,
                ins=[agbs_in.ap().opt()], outs=[agbs_out.ap().opt()]), inc=[(ccs[1], 1)])
            ctx.wait_h("gpsimd", ccs[1], 2)
            bar_inc(ctx, 4)
        prog.prefix = "L3_"
        emit_L3(nc, prog, xn1T_in=xn1T_sp, Wq=xin["Wq"], Wk=xin["Wk"], Wv=xin["Wv"],
                w_red=xin["w_red"], wg=xin["wg"], id_bf=xin["id_bf"], id_f=xin["id_f"],
                qT_out=lambda b: agq_in.ap().rearrange("p (dt b t) -> p dt b t", b=2, t=1024)[:, :, b, :],
                k_out=lambda b: k_scr.ap()[b * 128:(b + 1) * 128, :],
                v_out=lambda b: ag1_in.ap()[256 + b * 128:256 + (b + 1) * 128, :],
                bs_out=lambda b: agbs_in.ap()[b * 8:(b + 1) * 8, :],
                gsc_sp=gsc_sp,
                hooks={"head": head(3), "tail": tail3})

        def tailT(ctx):
            ctx.on("gpsimd", lambda q: q.collective_compute(
                "AllGather", ALU.bypass, replica_groups=RG,
                ins=[ag1_in.ap().opt()], outs=[ag1_out.ap().opt()]), inc=[(ccs[1], 1)])
            ctx.wait_h("gpsimd", ccs[1], 3)
            bar_inc(ctx, 6)
        prog.prefix = "L3T_"
        emit_L3T(nc, prog, k_in=k_scr.ap(), kT_out=ag1_in.ap()[0:256, :],
                 hooks={"head": head(4), "tail": tailT})

        ag1o = ag1_out.ap()
        agqo = agq_out.ap()
        ag1o_r = ag1o.rearrange("(ci r) j -> ci r j", r=512)

        def kM_src(bb, J):
            return ag1o_r[:, bb * 128:(bb + 1) * 128, J * 128:(J + 1) * 128].transpose([1, 0, 2])

        def v_src(bb, J, dh):
            return ag1o[512 * (J // 8) + 256 + bb * 128:512 * (J // 8) + 256 + (bb + 1) * 128,
                        (J % 8) * 1024 + dh * 512:(J % 8) * 1024 + (dh + 1) * 512]

        def at_dst(o, b, dh):
            return ag3_in.ap()[(o * 2 + b) * 128:(o * 2 + b + 1) * 128, dh * 512:(dh + 1) * 512]

        _ETMAP = {"sync": ET.SP, "scalar": ET.Activation, "gpsimd": ET.Pool}

        def _pid(ctx):
            if not hasattr(ctx, "_pid"):
                p0 = nc.partition_id([_ETMAP[ctx.eng]])
                ctx._pid = ctx.q.snap(p0, min_val=0, max_val=7)
                ctx._offs = {128: ctx._pid * 128, 1024: ctx._pid * 1024}
            return ctx._pid

        def emit_pair_loads(ctx, ld, tp, p, sqt, svd):
            engq = "sync" if p < 2 else "scalar"
            first = [True]
            def dma_one(dst, src_base, moff_elems):
                def f(q, dst=dst, src_base=src_base, moff_elems=moff_elems):
                    sym = ap_replace(src_base, offset=src_base.offset + moff_elems)
                    return q.dma_start(out=dst, in_=sym)
                w = [(tp, tp.n)] if first[0] else []
                first[0] = False
                ctx.do(engq, f, wait=w, inc=[(ld, 16)])
            for blk in range(2):
                o = 2 * p + blk
                if ctx.eng == engq:
                    _pid(ctx)
                    base = agqo[128 * o:128 * (o + 1), :]                         .rearrange("p (db t) -> p db t", t=1024)[:, :, 0:128]
                    moff = ctx._offs[128]
                else:
                    base = None
                    moff = 0
                dst = sqt[:].rearrange("p (db r) -> p db r", r=256)[
                    :, :, blk * 128:(blk + 1) * 128]
                dma_one(dst, base, moff)
            for blk in range(2):
                o = 2 * p + blk
                if ctx.eng == engq:
                    _pid(ctx)
                    base = ag1o[512 * o + 256:512 * o + 512, 0:1024].rearrange("(bb p) d -> p bb d", p=128)
                    moff = ctx._offs[1024]
                else:
                    base = None
                    moff = 0
                dst = svd[:].rearrange("p (k b d) -> p k b d", k=2, b=2)[:, blk, :, :]
                dma_one(dst, base, moff)

        def emit_bs_load(ctx, ld, sbsj):
            for cpr in range(8):
                for b in range(2):
                    ctx.do("sync", lambda q, cpr=cpr, b=b: q.dma_start(
                        out=sbsj[8 * cpr:8 * cpr + 8, b * 1024:(b + 1) * 1024],
                        in_=agbs_out.ap()[16 * cpr + 8 * b:16 * cpr + 8 * b + 8, :]),
                        inc=[(ld, 16)])

        def tail4(ctx):
            ctx.on("gpsimd", lambda q: q.collective_compute(
                "AllGather", ALU.bypass, replica_groups=RG,
                ins=[ag3_in.ap().opt()], outs=[ag3_out.ap().opt()]), inc=[(ccs[2], 1)])
            ctx.wait_h("gpsimd", ccs[2], 1)
            bar_inc(ctx, 5)
        prog.prefix = "L4_"
        emit_L4(nc, prog, kM_src=kM_src, v_src=v_src, emit_pair_loads=emit_pair_loads,
                emit_bs_load=emit_bs_load, at_dst=at_dst, biasT=xin["biasT"],
                cm0=xin["cm0"], cm1=xin["cm1"], triu_b=xin["triu_b"],
                ntriu_b=xin["ntriu_b"], id_bf=xin["id_bf"],
                hooks={"head": head(6), "tail": tail4})

        _l5c = {}

        def at_batch(q, b, half):
            if _l5c.get("q") is not q:
                p0 = nc.partition_id([q.engine])
                _l5c["pid"] = q.snap(p0, min_val=0, max_val=7)
                _l5c["q"] = q
            pid = _l5c["pid"]
            base = ag3_out.ap().rearrange("(l r) d -> l r d", r=2048)[
                2 * half:2 * half + 2, b * 128:(b + 1) * 128, :].transpose([1, 0, 2])
            off = pid * 256 * 1024
            return ap_replace(base, offset=base.offset + off)
        prog.prefix = "L5_"
        emit_L5(nc, prog, xn1T_in=xn1T_sp, gsc_in=gsc_sp, x1_in=x1_sp, at_batch=at_batch,
                w1=xin["w1"], b1h=xin["b1h"], w2=xin["w2"], b2r=xin["b2r"],
                out_sl=out_sl, acc=acc,
                hooks={"head": head(5)})
    return nc


def kernel_fused(inp):
    nc = build_FUSED()
    xs, halos = _split_inputs(inp["L0_attn_x"])
    id_bf = np.eye(128, dtype=BF)
    id_f = np.eye(128, dtype=np.float32)
    triu = np.triu(np.ones((128, 128), np.float32))
    mu_r = np.ascontiguousarray(inp["L0_ffn_mu_r"].reshape(8, 128).T)
    mu_k = np.ascontiguousarray(inp["L0_ffn_mu_k"].reshape(8, 128).T)
    b1h = np.ascontiguousarray(inp["L1_ffn_b1"].reshape(8, 32, 128).transpose(2, 0, 1).reshape(128, 256))
    b2r = np.ascontiguousarray(inp["L1_ffn_b2"].reshape(1, 8 * 1024))
    in_maps = []
    for c in range(R):
        blks = blocks_for_core(c)
        pw_ = np.zeros((128, 8), np.float32)
        pw_[:, :c] = 1.0
        biasT = np.zeros((160, 128, 256), np.float32)
        for p in range(4):
            for J in range(PAIR_STEPS[p]):
                for blk in range(2):
                    g = blks[2 * p + blk]
                    t = biasT[STEP_OFF[p] + J, :, blk * 128:(blk + 1) * 128]
                    if J > g:
                        t[:] = -1e9
                    elif J == g:
                        t[:] = np.where(np.arange(128)[:, None] > np.arange(128)[None, :], -1e9, 0.0)
        cm0_ = np.zeros((8, 64, 128), np.float32)
        cm1_ = np.zeros((8, 64, 128), np.float32)
        for o in range(8):
            g = blks[o]
            cm0_[o, g:, :] = np.float32(1e-9)
            cm1_[o, :g, :] = 1.0
            cm1_[o, g:, :] = np.float32(1e-9)
        in_maps.append(dict(
            x_sl=xs[c], x_halo=halos[c],
            w_qkv=inp["L0_attn_w_qkv"],
            wk_map=inp["L0_attn_wk_map"],
            bk_map=np.ascontiguousarray(inp["L0_attn_bk_map"].reshape(1, 64)),
            wq_map=inp["L0_attn_wq_map"],
            bq_map=np.ascontiguousarray(inp["L0_attn_bq_map"].reshape(1, 64)),
            id_bf=id_bf, id_f=id_f, triu_f=triu,
            triu_b=triu.astype(BF), ntriu_b=(np.float32(-1e-9) * triu).astype(BF),
            prefw=pw_,
            W_r=inp["L0_ffn_W_r"], W_k=inp["L0_ffn_W_k"], W_v=inp["L0_ffn_W_v"],
            w_out=inp["L0_attn_w_out"], mu_r=mu_r, mu_k=mu_k,
            Wq=inp["L1_attn_Wq"], Wk=inp["L1_attn_Wk"], Wv=inp["L1_attn_Wv"],
            w_red=inp["L1_ffn_w_red"], wg=inp["L1_ffn_wg"],
            biasT=np.ascontiguousarray(biasT.astype(BF)),
            cm0=np.ascontiguousarray(cm0_.transpose(1, 0, 2).reshape(64, 1024)),
            cm1=np.ascontiguousarray(cm1_.transpose(1, 0, 2).reshape(64, 1024)),
            w1=inp["L1_ffn_w1"], b1h=b1h, w2=inp["L1_ffn_w2"], b2r=b2r,
        ))
    res = run_bass_kernel_spmd(nc, in_maps, core_ids=list(range(R)))
    out = np.zeros((B, T, C), np.float32)
    for c in range(R):
        out[:, c * TC:(c + 1) * TC, :] = res.results[c]["out_sl"]
    return out


def kernel(**inputs):
    import os as _os
    if _os.environ.get("KERNEL_UNFUSED"):
        return kernel_impl(inputs)
    return kernel_fused(inputs)




# version 03 (final)
# speedup vs baseline: 1.0196x; 1.0196x over previous
import numpy as np
import ml_dtypes
from contextlib import ExitStack
from dataclasses import replace as ap_replace
from types import SimpleNamespace as NS

import concourse.bass as bass
import concourse.mybir as mybir
from concourse.bass_utils import run_bass_kernel_spmd

F32 = mybir.dt.float32
BF16 = mybir.dt.bfloat16
AF = mybir.ActivationFunctionType
ALU = mybir.AluOpType
AX = mybir.AxisListType
BF = ml_dtypes.bfloat16

R = 8
B = 2
T = 8192
TC = T // R
C = 1024
H = 16
HD = 64
NT = TC // 128
EPS = 1e-5

ENGINES = ("sync", "tensor", "vector", "scalar", "gpsimd")


def blocks_for_core(c):
    return [8 * o + c for o in range(8)]


class _Sem:
    __slots__ = ("h", "n")

    def __init__(self, h):
        self.h = h
        self.n = 0


class Ctx:

    def __init__(self, prog, eng, q):
        self.prog = prog
        self.eng = eng
        self.q = q
        self.sems = {}

    def sem(self, name):
        name = self.prog.prefix + name
        if name not in self.prog.handles:
            self.prog.handles[name] = self.prog.es.enter_context(self.prog.nc.semaphore(name))
        s = _Sem(self.prog.handles[name])
        self.sems[name] = s
        return s

    def do(self, eng, fn, wait=(), inc=()):
        if self.eng == eng:
            for s, v in wait:
                self.q.wait_ge(s.h, v)
            ins = fn(self.q)
            for s, k in inc:
                ins.then_inc(s.h, k)
        for s, k in inc:
            s.n += k

    def wait(self, eng, sem, value):
        if self.eng == eng:
            self.q.wait_ge(sem.h, value)

    def wait_h(self, eng, handle, value):
        if self.eng == eng:
            self.q.wait_ge(handle, value)

    def on(self, eng, fn, inc=()):
        if self.eng == eng:
            ins = fn(self.q)
            for h, k in inc:
                ins.then_inc(h, k)


class Prog:
    def __init__(self, nc, es):
        self.nc = nc
        self.es = es
        self.handles = {}
        self.prefix = ""

    def build(self, phase_fn):
        with self.nc.Block() as block:
            for name in ENGINES:
                def body(q, name=name):
                    phase_fn(Ctx(self, name, q))
                getattr(block, name)(body)



def emit_L1(nc, prog, *, x_sl, x_halo, w_qkv, wk_map, bk_map, id_bf,
            xnT_sp, kf_sp, v_sp, states, hx_scr, hooks=None, stage="full"):
    hooks = hooks or {}
    es = ExitStack()
    with es:
        T_ = lambda nm, shp, dt: es.enter_context(nc.sbuf_tensor(prog.prefix + nm, shp, dt))
        P_ = lambda nm, shp, dt: es.enter_context(nc.psum_tensor(prog.prefix + nm, shp, dt))
        sid = T_("sid", [128, 128], BF16)
        wkr = T_("wkr", [128, 8 * 1024], BF16)
        wkm = T_("wkm", [64, 64], BF16)
        wkmT2 = T_("wkmT2", [128, 128], BF16)
        bkb = T_("bkb", [1, 64], BF16)
        bk8 = T_("bk8", [1, 512], BF16)
        ones1 = T_("ones1", [1, 128], BF16)
        onescol = T_("onescol", [128, 1], BF16)
        wcomb = T_("wcomb", [128, 8 * 1024], BF16)
        wvT = T_("wvT", [128, 8 * 1024], BF16)
        sx = [T_(f"sx{i}", [128, 1024], F32) for i in range(2)]
        sxc = T_("sxc", [128, 1024], F32)
        srow = T_("srow", [128, 1], F32)
        svar = T_("svar", [128, 1], F32)
        sln = T_("sln", [128, 1], F32)
        srstd = T_("srstd", [128, 1], F32)
        sxn = T_("sxn", [128, 1024], BF16)
        xnT = T_("xnT", [128, 16 * 1025], BF16)
        sep = T_("sep", [128, 512], F32)
        sen = T_("sen", [128, 512], F32)
        ssum = T_("ssum", [128, 8], F32)
        srec = T_("srec", [128, 8], F32)
        ssum2 = T_("ssum2", [128, 8], F32)
        srec2 = T_("srec2", [128, 8], F32)
        kf = T_("kf", [128, 16 * 2048], BF16)
        vsb = T_("vsb", [128, 16 * 1024], BF16)
        sst = T_("sst", [128, 32 * 65], F32)
        shr = T_("shr", [128, 1], F32)
        shv = T_("shv", [128, 1], F32)
        shl = T_("shl", [128, 1], F32)
        shs = T_("shs", [128, 1], F32)
        shn = T_("shn", [128, 1024], BF16)
        seps = T_("seps", [128, 1], F32)

        ptrT = P_("ptrT", [128, 1024], BF16)
        pwb = P_("pwb", [128, 1024], BF16)
        pp = [P_(f"pp{i}", [128, 512], F32) for i in range(2)]
        pv = [P_(f"pv{i}", [128, 512], F32) for i in range(2)]
        pwk = pwb[0:64, 0:64]
        pw = pp[0]
        pst = pv[0][:, 0:65]

        xnT3 = xnT[:].rearrange("p (g c) -> p g c", c=1025)
        kf3 = kf[:].rearrange("p (g c) -> p g c", c=2048)
        v3 = vsb[:].rearrange("p (g c) -> p g c", c=1024)
        wkr3 = wkr[:].rearrange("p (r c) -> p r c", c=1024)
        wvT3 = wvT[:].rearrange("p (r c) -> p r c", c=1024)

        def phase(ctx: Ctx):
            hooks.get("head", lambda c: None)(ctx)
            ld = ctx.sem("ld")
            tp = ctx.sem("tp")
            vp = ctx.sem("vp")
            sp = ctx.sem("sp")
            stc = ctx.sem("stc")

            ctx.do("sync", lambda q: q.dma_start(out=sid[:], in_=id_bf), inc=[(ld, 16)])
            ctx.do("gpsimd", lambda q: q.dma_start(out=wkr[:].rearrange("p (r c) -> p r c", c=1024),
                                                   in_=w_qkv[1024:2048, :].rearrange("(r p) c -> p r c", p=128)),
                   inc=[(ld, 16)])
            ctx.do("gpsimd", lambda q: q.dma_start(out=wkm[:], in_=wk_map), inc=[(ld, 16)])
            ctx.do("gpsimd", lambda q: q.dma_start(out=bkb[:], in_=bk_map), inc=[(ld, 16)])
            ld_w = ld.n
            ctx.do("vector", lambda q: q.memset(ones1[:], 1.0))
            ctx.do("vector", lambda q: q.memset(onescol[:], 1.0))
            ctx.do("vector", lambda q: q.memset(seps[:], EPS), inc=[(vp, 1)])
            ctx.do("vector", lambda q: q.tensor_copy(bk8[:].rearrange("p (s e) -> p s e", e=64),
                                                     bkb[:].rearrange("p (s e) -> p s e", e=64).broadcast_to([1, 8, 64])),
                   wait=[(ld, ld_w)], inc=[(vp, 1)])
            v_bk8 = vp.n
            ctx.do("vector", lambda q: q.memset(wkmT2[:], 0.0), inc=[(vp, 1)])
            v_z = vp.n
            ctx.do("tensor", lambda q: q.transpose(pwk, wkm[:], sid[0:64, 0:64]),
                   wait=[(ld, ld_w)], inc=[(tp, 1)])
            ctx.do("vector", lambda q: q.tensor_copy(wkmT2[0:64, 0:64], pwk), wait=[(tp, tp.n)], inc=[(vp, 1)])
            ctx.do("gpsimd", lambda q: q.dma_start(out=wkmT2[64:128, 64:128], in_=wkmT2[0:64, 0:64]),
                   wait=[(vp, vp.n)], inc=[(ld, 16)])
            v_wkmT = vp.n
            ld_wkmT = ld.n
            if stage == "prep1":
                vfin = vp.n
                ctx.do("sync", lambda q: q.dma_start(out=xnT_sp, in_=xnT[:]), wait=[(vp, vfin)], inc=[(stc, 16)])
                ctx.do("sync", lambda q: q.dma_start(out=kf_sp, in_=kf[:]), inc=[(stc, 16)])
                ctx.do("sync", lambda q: q.dma_start(out=v_sp, in_=vsb[:]), inc=[(stc, 16)])
                ctx.do("sync", lambda q: q.dma_start(out=states, in_=sst[:]), inc=[(stc, 16)])
                ctx.wait("gpsimd", stc, 16 * 4)
                return
            for ct in range(8):
                for hg in range(2):
                    def mmgrp(q, ct=ct, hg=hg):
                        for j in range(4):
                            hp = hg * 4 + j
                            lt = wkr3[:, hp, ct * 128:(ct + 1) * 128]
                            ins = q.matmul(pw[:, j * 128:(j + 1) * 128], lt, wkmT2[:],
                                           start=True, stop=True)
                        return ins
                    ctx.do("tensor", mmgrp,
                           wait=[(vp, v_wkmT), (ld, ld_wkmT)] if (ct == 0 and hg == 0) else [(vp, vp.n)],
                           inc=[(tp, 1)])
                    ctx.do("vector", lambda q, ct=ct, hg=hg: q.tensor_copy(
                        wcomb[:, ct * 1024 + hg * 512: ct * 1024 + (hg + 1) * 512], pw[:]),
                        wait=[(tp, tp.n)], inc=[(vp, 1)])
            if stage == "prep2":
                vfin = vp.n
                ctx.do("sync", lambda q: q.dma_start(out=xnT_sp, in_=xnT[:]), wait=[(vp, vfin)], inc=[(stc, 16)])
                ctx.do("sync", lambda q: q.dma_start(out=kf_sp, in_=kf[:]), inc=[(stc, 16)])
                ctx.do("sync", lambda q: q.dma_start(out=v_sp, in_=vsb[:]), inc=[(stc, 16)])
                ctx.do("sync", lambda q: q.dma_start(out=states, in_=sst[:]), inc=[(stc, 16)])
                ctx.wait("gpsimd", stc, 16 * 4)
                return
            ctx.do("gpsimd", lambda q: q.dma_start(out=wkr[:].rearrange("p (r c) -> p r c", c=1024),
                                                   in_=w_qkv[2048:3072, :].rearrange("(r p) c -> p r c", p=128)),
                   wait=[(tp, tp.n)], inc=[(ld, 16)])
            ld_vr = ld.n
            for dvt in range(8):
                def trgrp(q, dvt=dvt):
                    for ct in range(8):
                        ins = q.transpose(pwb[:, ct * 128:(ct + 1) * 128], wkr3[:, dvt, ct * 128:(ct + 1) * 128], sid[:])
                    return ins
                ctx.do("tensor", trgrp, wait=[(vp, vp.n), (ld, ld_vr)], inc=[(tp, 1)])
                ctx.do("vector", lambda q, dvt=dvt: q.tensor_copy(
                    wvT3[:, :, dvt * 128:(dvt + 1) * 128],
                    pwb[:].rearrange("p (a c) -> p a c", c=128)),
                    wait=[(tp, tp.n)], inc=[(vp, 1)])

            if stage == "prep":
                vfin = vp.n
                ctx.do("sync", lambda q: q.dma_start(out=xnT_sp, in_=xnT[:]), wait=[(vp, vfin)], inc=[(stc, 16)])
                ctx.do("sync", lambda q: q.dma_start(out=kf_sp, in_=kf[:]), inc=[(stc, 16)])
                ctx.do("sync", lambda q: q.dma_start(out=v_sp, in_=vsb[:]), inc=[(stc, 16)])
                ctx.do("sync", lambda q: q.dma_start(out=states, in_=sst[:]), inc=[(stc, 16)])
                ctx.wait("gpsimd", stc, 16 * 4)
                return
            ctx.do("vector", lambda q: q.memset(sx[0][:], 0.0), inc=[(vp, 1)])
            v_hz = vp.n
            for b in range(B):
                ctx.do("sync", lambda q, b=b: q.dma_start(out=sx[0][64 * b:64 * b + 1, :], in_=x_halo[b:b + 1, :]),
                       wait=[(vp, v_hz)], inc=[(ld, 16)])
            ld_halo = ld.n
            ctx.do("vector", lambda q: q.reduce_sum(shr[:], sx[0][:], axis=AX.X), wait=[(ld, ld_halo)], inc=[(vp, 1)])
            ctx.do("vector", lambda q: q.tensor_scalar(shr[:], shr[:], -1.0 / C, None, op0=ALU.mult),
                   wait=[(vp, vp.n)], inc=[(vp, 1)])
            ctx.do("scalar", lambda q: q.activation(sxc[:], sx[0][:], AF.Identity, bias=shr[:, 0:1]),
                   wait=[(vp, vp.n)], inc=[(sp, 1)])
            ctx.do("scalar", lambda q: q.activation(sx[1][:], sxc[:], AF.Square, accum_out=shv[:, 0:1]),
                   wait=[(sp, sp.n)], inc=[(sp, 1)])
            ctx.do("scalar", lambda q: q.activation(shl[:], shv[:], AF.Ln, scale=1.0 / C, bias=seps[:, 0:1]),
                   wait=[(sp, sp.n)], inc=[(sp, 1)])
            ctx.do("scalar", lambda q: q.activation(shs[:], shl[:], AF.Exp, scale=-0.5),
                   wait=[(sp, sp.n)], inc=[(sp, 1)])
            ctx.do("vector", lambda q: q.tensor_scalar(shn[:], sxc[:], shs[:, 0:1], None, op0=ALU.mult),
                   wait=[(sp, sp.n)], inc=[(vp, 1)])
            v_halo = vp.n
            s_halo = sp.n
            for b in range(B):
                ctx.do("gpsimd", lambda q, b=b: q.dma_start(out=hx_scr[b:b + 1, :], in_=shn[64 * b:64 * b + 1, :]),
                       wait=[(vp, v_halo)], inc=[(ld, 16)])
            ld_hxw = ld.n

            if stage == "halo":
                vfin = vp.n
                ctx.do("sync", lambda q: q.dma_start(out=xnT_sp, in_=xnT[:]), wait=[(vp, vfin)], inc=[(stc, 16)])
                ctx.do("sync", lambda q: q.dma_start(out=kf_sp, in_=kf[:]), inc=[(stc, 16)])
                ctx.do("sync", lambda q: q.dma_start(out=v_sp, in_=vsb[:]), inc=[(stc, 16)])
                ctx.do("sync", lambda q: q.dma_start(out=states, in_=sst[:]), inc=[(stc, 16)])
                ctx.wait("gpsimd", stc, 16 * 4)
                return
            xc_done = {}
            trx_done = {}
            scale_done = [0]
            tr_copy_done = {}
            pp_free = {}
            pv_free = {}
            for i in range(16):
                b, tt = divmod(i, NT)
                g = b * 8 + tt
                w = [(sp, xc_done[i - 2])] if i - 2 in xc_done else [(sp, s_halo)]
                ctx.do("sync", lambda q, b=b, tt=tt, i=i: q.dma_start(
                    out=sx[i % 2][:], in_=x_sl[b, tt * 128:(tt + 1) * 128, :]), wait=w, inc=[(ld, 16)])
                ld_x = ld.n
                ctx.do("vector", lambda q, i=i: q.reduce_sum(srow[:], sx[i % 2][:], axis=AX.X),
                       wait=[(ld, ld_x)] + ([(sp, xc_done[i - 1])] if i - 1 in xc_done else [(sp, s_halo)]),
                       inc=[(vp, 1)])
                ctx.do("vector", lambda q: q.tensor_scalar(srow[:], srow[:], -1.0 / C, None, op0=ALU.mult),
                       wait=[(vp, vp.n)], inc=[(vp, 1)])
                ctx.do("scalar", lambda q, i=i: q.activation(sxc[:], sx[i % 2][:], AF.Identity, bias=srow[:, 0:1]),
                       wait=[(vp, vp.n), (sp, sp.n)], inc=[(sp, 1)])
                ctx.do("scalar", lambda q, i=i: q.activation(sx[i % 2][:], sxc[:], AF.Square, accum_out=svar[:, 0:1]),
                       wait=[(sp, sp.n)] + ([(tp, trx_done[i - 1])] if i - 1 in trx_done else []),
                       inc=[(sp, 1)])
                xc_done[i] = sp.n
                ctx.do("scalar", lambda q: q.activation(sln[:], svar[:], AF.Ln, scale=1.0 / C, bias=seps[:, 0:1]),
                       wait=[(sp, sp.n)], inc=[(sp, 1)])
                ctx.do("scalar", lambda q: q.activation(srstd[:], sln[:], AF.Exp, scale=-0.5),
                       wait=[(sp, sp.n)], inc=[(sp, 1)])
                ctx.do("vector", lambda q: q.tensor_scalar(sxn[:], sxc[:], srstd[:, 0:1], None, op0=ALU.mult),
                       wait=[(sp, sp.n), (vp, vp.n)], inc=[(vp, 1)])
                v_xn = vp.n
                def trx(q):
                    for ct in range(8):
                        ins = q.transpose(ptrT[:, ct * 128:(ct + 1) * 128], sxn[:, ct * 128:(ct + 1) * 128], sid[:])
                    return ins
                wtr = [(vp, v_xn)] + ([(vp, tr_copy_done[i - 1])] if i - 1 in tr_copy_done else [])
                ctx.do("tensor", trx, wait=wtr, inc=[(tp, 1)])
                trx_done[i] = tp.n
                ctx.do("vector", lambda q, b=b, tt=tt: q.tensor_copy(
                    xnT3[:, b * 8:(b + 1) * 8, 1 + tt * 128:1 + (tt + 1) * 128],
                    ptrT[:].rearrange("p (a c) -> p a c", c=128)),
                    wait=[(tp, tp.n)], inc=[(vp, 1)])
                tr_copy_done[i] = vp.n
                for hg in range(2):
                    def pmm2(q, g=g, hg=hg):
                        q.matmul(pp[hg][:], ones1[:], bk8[:], start=True, stop=False)
                        for ct in range(8):
                            lhs = xnT3[:, (g // 8) * 8 + ct, 1 + (g % 8) * 128:1 + (g % 8 + 1) * 128]
                            rhs = wcomb[:, ct * 1024 + hg * 512: ct * 1024 + (hg + 1) * 512]
                            ins = q.matmul(pp[hg][:], lhs, rhs, start=False, stop=(ct == 7))
                        return ins
                    wpp = [(vp, tr_copy_done[i]), (vp, v_bk8)]
                    if (i - 1, hg) in pp_free:
                        wpp.append((sp, pp_free[(i - 1, hg)]))
                    ctx.do("tensor", pmm2, wait=wpp, inc=[(tp, 1)])
                    t_pp = tp.n
                    ctx.do("scalar", lambda q, hg=hg: q.activation(sep[:], pp[hg][:], AF.Exp),
                           wait=[(sp, sp.n), (tp, t_pp), (vp, scale_done[0])], inc=[(sp, 1)])
                    ctx.do("scalar", lambda q, hg=hg: q.activation(sen[:], pp[hg][:], AF.Exp, scale=-1.0),
                           wait=[(sp, sp.n)], inc=[(sp, 1)])
                    pp_free[(i, hg)] = sp.n
                    s_exp = sp.n
                    sep3 = sep[:].rearrange("p (s e) -> p s e", e=64)
                    sen3 = sen[:].rearrange("p (s e) -> p s e", e=64)
                    ssum3 = ssum[:].rearrange("p (s e) -> p s e", e=1)
                    srec3 = srec[:].rearrange("p (s e) -> p s e", e=1)
                    ssum23 = ssum2[:].rearrange("p (s e) -> p s e", e=1)
                    srec23 = srec2[:].rearrange("p (s e) -> p s e", e=1)
                    kfg = kf3[:, g, 0:2048].rearrange("p (h f) -> p h f", f=128)
                    dst_p = kfg[:, hg * 8:(hg + 1) * 8, 0:64]
                    dst_n = kfg[:, hg * 8:(hg + 1) * 8, 64:128]
                    ctx.do("vector", lambda q: q.tensor_reduce(ssum3, sep3, axis=AX.X, op=ALU.add),
                           wait=[(sp, s_exp), (vp, vp.n)], inc=[(vp, 1)])
                    ctx.do("vector", lambda q: q.reciprocal(srec[:], ssum[:]), wait=[(vp, vp.n)], inc=[(vp, 1)])
                    ctx.do("vector", lambda q, dst_p=dst_p: q.tensor_tensor(
                        dst_p, sep3, srec3.broadcast_to([128, 8, 64]), op=ALU.mult),
                        wait=[(vp, vp.n)], inc=[(vp, 1)])
                    ctx.do("vector", lambda q: q.tensor_reduce(ssum23, sen3, axis=AX.X, op=ALU.add),
                           wait=[(vp, vp.n)], inc=[(vp, 1)])
                    ctx.do("vector", lambda q: q.reciprocal(srec2[:], ssum2[:]), wait=[(vp, vp.n)], inc=[(vp, 1)])
                    ctx.do("vector", lambda q, dst_n=dst_n: q.tensor_tensor(
                        dst_n, sen3, srec23.broadcast_to([128, 8, 64]), op=ALU.mult),
                        wait=[(vp, vp.n)], inc=[(vp, 1)])
                    scale_done[0] = vp.n
                for ch in range(2):
                    def vmm(q, g=g, ch=ch):
                        for ct in range(8):
                            lhs = xnT3[:, (g // 8) * 8 + ct, 1 + (g % 8) * 128:1 + (g % 8 + 1) * 128]
                            rhs = wvT[:, ct * 1024 + ch * 512: ct * 1024 + (ch + 1) * 512]
                            ins = q.matmul(pv[ch][:], lhs, rhs, start=(ct == 0), stop=(ct == 7))
                        return ins
                    wv = [(vp, tr_copy_done[i])]
                    if (i - 1, ch) in pv_free:
                        wv.append((vp, pv_free[(i - 1, ch)]))
                    ctx.do("tensor", vmm, wait=wv, inc=[(tp, 1)])
                    ctx.do("vector", lambda q, g=g, ch=ch: q.tensor_copy(
                        v3[:, g, ch * 512:(ch + 1) * 512], pv[ch][:]),
                        wait=[(tp, tp.n), (vp, vp.n)], inc=[(vp, 1)])
                    pv_free[(i, ch)] = vp.n
            v_main = vp.n
            if stage == "main":
                vfin = vp.n
                ctx.do("sync", lambda q: q.dma_start(out=xnT_sp, in_=xnT[:]), wait=[(vp, vfin)], inc=[(stc, 16)])
                ctx.do("sync", lambda q: q.dma_start(out=kf_sp, in_=kf[:]), inc=[(stc, 16)])
                ctx.do("sync", lambda q: q.dma_start(out=v_sp, in_=vsb[:]), inc=[(stc, 16)])
                ctx.do("sync", lambda q: q.dma_start(out=states, in_=sst[:]), inc=[(stc, 16)])
                ctx.wait("gpsimd", stc, 16 * 4)
                return
            pst_free = v_main
            for b in range(B):
                for h in range(H):
                    def stmm(q, b=b, h=h):
                        for ch in range(8):
                            lhs = kf3[:, b * 8 + ch, h * 128:(h + 1) * 128]
                            rhsv = v3[:, b * 8 + ch, h * 64:(h + 1) * 64]
                            q.matmul(pst[:, 0:64], lhs, rhsv, start=(ch == 0), stop=False)
                            ins = q.matmul(pst[:, 64:65], lhs, onescol[:], start=False, stop=(ch == 7))
                        return ins
                    ctx.do("tensor", stmm, wait=[(vp, pst_free)], inc=[(tp, 1)])
                    ctx.do("vector", lambda q, b=b, h=h: q.tensor_copy(
                        sst[:, (b * 16 + h) * 65:(b * 16 + h + 1) * 65], pst),
                        wait=[(tp, tp.n)], inc=[(vp, 1)])
                    pst_free = vp.n
            for b in range(B):
                for ct in range(8):
                    g = b * 8 + ct
                    ctx.do("gpsimd", lambda q, b=b, ct=ct, g=g: q.dma_start(
                        out=xnT3[:, g, 0:1],
                        in_=hx_scr[b, ct * 128:(ct + 1) * 128].rearrange("(a o) -> a o", o=1),
                        allow_slow_non_contiguous=True),
                        wait=[(ld, ld_hxw), (vp, vp.n)], inc=[(ld, 16)])
            ld_halo2 = ld.n
            ctx.wait("sync", ld, ld_halo2)
            vfin = vp.n
            ctx.do("sync", lambda q: q.dma_start(out=xnT_sp, in_=xnT[:]), wait=[(vp, vfin)], inc=[(stc, 16)])
            ctx.do("sync", lambda q: q.dma_start(out=kf_sp, in_=kf[:]), inc=[(stc, 16)])
            ctx.do("sync", lambda q: q.dma_start(out=v_sp, in_=vsb[:]), inc=[(stc, 16)])
            ctx.do("sync", lambda q: q.dma_start(out=states, in_=sst[:]), inc=[(stc, 16)])
            ctx.wait("gpsimd", stc, 16 * 4)
            hooks.get("tail", lambda c: None)(ctx)

        prog.build(phase)


def _l1_io(nc, kind_in="ExternalInput", kind_out="ExternalOutput"):
    return dict(
        x_sl=nc.dram_tensor("x_sl", [B, TC, C], F32, kind=kind_in).ap(),
        x_halo=nc.dram_tensor("x_halo", [B, C], F32, kind=kind_in).ap(),
        w_qkv=nc.dram_tensor("w_qkv", [3 * C, C], F32, kind=kind_in).ap(),
        wk_map=nc.dram_tensor("wk_map", [64, 64], F32, kind=kind_in).ap(),
        bk_map=nc.dram_tensor("bk_map", [1, 64], F32, kind=kind_in).ap(),
        id_bf=nc.dram_tensor("id_bf", [128, 128], BF16, kind=kind_in).ap(),
        xnT_sp=nc.dram_tensor("xnT_sp", [128, 16 * 1025], BF16, kind=kind_out).ap(),
        kf_sp=nc.dram_tensor("kf_sp", [128, 16 * 2048], BF16, kind=kind_out).ap(),
        v_sp=nc.dram_tensor("v_sp", [128, 16 * 1024], BF16, kind=kind_out).ap(),
        states=nc.dram_tensor("states", [128, 32 * 65], F32, kind=kind_out).ap(),
        hx_scr=nc.dram_tensor("hx_scr", [B, 1024], BF16).ap(),
    )


def build_L1(stage="full"):
    nc = bass.Bass("TRN2", target_bir_lowering=False)
    io = _l1_io(nc)
    es = ExitStack()
    with es:
        prog = Prog(nc, es)
        prog.prefix = "L1_"
        emit_L1(nc, prog, stage=stage, **io)
    return nc



def _split_inputs(L0_attn_x):
    x = np.ascontiguousarray(L0_attn_x)
    xs, halos = [], []
    for c in range(R):
        xs.append(np.ascontiguousarray(x[:, c * TC:(c + 1) * TC, :]))
        if c == 0:
            h = np.zeros((B, C), np.float32)
        else:
            h = np.ascontiguousarray(x[:, c * TC - 1, :])
        halos.append(h)
    return xs, halos


def run_L1(inp, stage="full"):
    nc = build_L1(stage)
    xs, halos = _split_inputs(inp["L0_attn_x"])
    id_bf = np.eye(128, dtype=BF)
    in_maps = []
    for c in range(R):
        in_maps.append(dict(
            x_sl=xs[c], x_halo=halos[c],
            w_qkv=inp["L0_attn_w_qkv"],
            wk_map=inp["L0_attn_wk_map"],
            bk_map=np.ascontiguousarray(inp["L0_attn_bk_map"].reshape(1, 64)),
            id_bf=id_bf,
        ))
    return run_bass_kernel_spmd(nc, in_maps, core_ids=list(range(R)))





def emit_L2a(nc, prog, *, xnT_in, kf_in, v_in, states_r, prefw, w_qkv, wq_map, bq_map, id_bf, triu_f, y_sp, hooks=None):
    hooks = hooks or {}
    es = ExitStack()
    with es:
        T_ = lambda nm, shp, dt: es.enter_context(nc.sbuf_tensor(prog.prefix + nm, shp, dt))
        P_ = lambda nm, shp, dt: es.enter_context(nc.psum_tensor(prog.prefix + nm, shp, dt))
        sid = T_("sid", [128, 128], BF16)
        striu = T_("striu", [128, 128], F32)
        wkr = T_("wkr", [128, 8 * 1024], BF16)
        wqm = T_("wqm", [64, 64], BF16)
        wqmT2 = T_("wqmT2", [128, 128], BF16)
        bqb = T_("bqb", [1, 64], BF16)
        bq8 = T_("bq8", [1, 512], BF16)
        ones1 = T_("ones1", [1, 128], BF16)
        onescol = T_("onescol", [128, 1], BF16)
        wcomb = T_("wcomb", [128, 8 * 1024], BF16)
        xnT = T_("xnT", [128, 16 * 1025], BF16)
        kfb = T_("kfb", [128, 8 * 2048], BF16)
        vb = T_("vb", [128, 8 * 1024], BF16)
        qfb = T_("qfb", [128, 8 * 2048], BF16)
        yb = T_("yb", [128, 8 * 1024], BF16)
        acc = T_("acc", [128, 32 * 65], F32)
        stt_ = T_("stt_", [128, 32 * 65], F32)
        spw = T_("spw", [128, 8], F32)
        sep = T_("sep", [128, 512], F32)
        sen = T_("sen", [128, 512], F32)
        ssum = T_("ssum", [128, 8], F32)
        srec = T_("srec", [128, 8], F32)
        ssum2 = T_("ssum2", [128, 8], F32)
        srec2 = T_("srec2", [128, 8], F32)
        kfT = T_("kfT", [128, 128], BF16)
        qfT = T_("qfT", [128, 128], BF16)
        sbf = T_("sbf", [128, 65], BF16)
        pA_sb = T_("pA_sb", [128, 128], BF16)
        sden = T_("sden", [128, 1], F32)
        sdr = T_("sdr", [128, 1], F32)

        pq = [P_(f"pq{i}", [128, 512], F32) for i in range(2)]
        ptrb = P_("ptrb", [128, 256], BF16)
        pat = P_("pat", [128, 128], F32)
        py = P_("py", [128, 65], F32)
        pu = P_("pu", [128, 65], F32)

        xnT3 = xnT[:].rearrange("p (g c) -> p g c", c=1025)
        kfb3 = kfb[:].rearrange("p (g c) -> p g c", c=2048)
        vb3 = vb[:].rearrange("p (g c) -> p g c", c=1024)
        qfb3 = qfb[:].rearrange("p (g c) -> p g c", c=2048)
        yb3 = yb[:].rearrange("p (g c) -> p g c", c=1024)
        wkr3 = wkr[:].rearrange("p (r c) -> p r c", c=1024)

        def phase(ctx: Ctx):
            hooks.get("head", lambda c: None)(ctx)
            ld = ctx.sem("ld")
            tp = ctx.sem("tp")
            vp = ctx.sem("vp")
            sp = ctx.sem("sp")
            stc = ctx.sem("stc")

            ctx.do("sync", lambda q: q.dma_start(out=sid[:], in_=id_bf), inc=[(ld, 16)])
            ctx.do("sync", lambda q: q.dma_start(out=striu[:], in_=triu_f), inc=[(ld, 16)])
            ctx.do("sync", lambda q: q.dma_start(out=xnT[:], in_=xnT_in), inc=[(ld, 16)])
            ctx.do("sync", lambda q: q.dma_start(out=spw[:], in_=prefw), inc=[(ld, 16)])
            ctx.do("gpsimd", lambda q: q.dma_start(out=wkr3, in_=w_qkv[0:1024, :].rearrange("(r p) c -> p r c", p=128)),
                   inc=[(ld, 16)])
            ctx.do("gpsimd", lambda q: q.dma_start(out=wqm[:], in_=wq_map), inc=[(ld, 16)])
            ctx.do("gpsimd", lambda q: q.dma_start(out=bqb[:], in_=bq_map), inc=[(ld, 16)])
            ld_w = ld.n
            ctx.do("vector", lambda q: q.memset(ones1[:], 1.0))
            ctx.do("vector", lambda q: q.memset(onescol[:], 1.0))
            ctx.do("vector", lambda q: q.tensor_copy(bq8[:].rearrange("p (s e) -> p s e", e=64),
                                                     bqb[:].rearrange("p (s e) -> p s e", e=64).broadcast_to([1, 8, 64])),
                   wait=[(ld, ld_w)], inc=[(vp, 1)])
            v_bq8 = vp.n
            ctx.do("vector", lambda q: q.memset(wqmT2[:], 0.0), inc=[(vp, 1)])
            ctx.do("tensor", lambda q: q.transpose(ptrb[0:64, 0:64], wqm[:], sid[0:64, 0:64]),
                   wait=[(ld, ld_w)], inc=[(tp, 1)])
            ctx.do("vector", lambda q: q.tensor_copy(wqmT2[0:64, 0:64], ptrb[0:64, 0:64]),
                   wait=[(tp, tp.n), (vp, vp.n)], inc=[(vp, 1)])
            ctx.do("gpsimd", lambda q: q.dma_start(out=wqmT2[64:128, 64:128], in_=wqmT2[0:64, 0:64]),
                   wait=[(vp, vp.n)], inc=[(ld, 16)])
            ld_wqmT = ld.n
            v_wqmT = vp.n
            for ct in range(8):
                for hg in range(2):
                    def mmgrp(q, ct=ct, hg=hg):
                        for j in range(4):
                            hp = hg * 4 + j
                            lt = wkr3[:, hp, ct * 128:(ct + 1) * 128]
                            ins = q.matmul(pq[0][:, j * 128:(j + 1) * 128], lt, wqmT2[:], start=True, stop=True)
                        return ins
                    ctx.do("tensor", mmgrp,
                           wait=[(vp, v_wqmT), (ld, ld_wqmT)] if (ct == 0 and hg == 0) else [(vp, vp.n)],
                           inc=[(tp, 1)])
                    ctx.do("vector", lambda q, ct=ct, hg=hg: q.tensor_copy(
                        wcomb[:, ct * 1024 + hg * 512: ct * 1024 + (hg + 1) * 512], pq[0][:]),
                        wait=[(tp, tp.n)], inc=[(vp, 1)])
            ctx.do("vector", lambda q: q.memset(acc[:], 0.0), inc=[(vp, 1)])
            for r_ in range(R):
                ctx.do("sync", lambda q, r_=r_: q.dma_start(out=stt_[:], in_=states_r(r_)),
                       wait=[(vp, vp.n)], inc=[(ld, 16)])
                ctx.do("vector", lambda q, r_=r_: q.scalar_tensor_tensor(
                    acc[:], stt_[:], spw[:, r_:r_ + 1], acc[:], op0=ALU.mult, op1=ALU.add),
                    wait=[(ld, ld.n), (vp, vp.n)], inc=[(vp, 1)])
            for b in range(B):
                ctx.do("sync", lambda q, b=b: q.dma_start(out=kfb[:], in_=kf_in[:, b * 8 * 2048:(b + 1) * 8 * 2048]),
                       wait=[(vp, vp.n), (stc, stc.n)], inc=[(ld, 16)])
                ctx.do("sync", lambda q, b=b: q.dma_start(out=vb[:], in_=v_in[:, b * 8 * 1024:(b + 1) * 8 * 1024]),
                       inc=[(ld, 16)])
                ld_b = ld.n
                pp_free = {}
                scale_done = [0]
                for tt in range(NT):
                    for hg in range(2):
                        def pmm2(q, b=b, tt=tt, hg=hg):
                            q.matmul(pq[hg][:], ones1[:], bq8[:], start=True, stop=False)
                            for ct in range(8):
                                lhs = xnT3[:, b * 8 + ct, 1 + tt * 128:1 + (tt + 1) * 128]
                                rhs = wcomb[:, ct * 1024 + hg * 512: ct * 1024 + (hg + 1) * 512]
                                ins = q.matmul(pq[hg][:], lhs, rhs, start=False, stop=(ct == 7))
                            return ins
                        wpp = [(vp, v_bq8), (ld, ld_b)]
                        if (tt - 1, hg) in pp_free:
                            wpp.append((sp, pp_free[(tt - 1, hg)]))
                        elif b == 1:
                            wpp.append((sp, sp.n))
                        ctx.do("tensor", pmm2, wait=wpp, inc=[(tp, 1)])
                        t_pp = tp.n
                        ctx.do("scalar", lambda q, hg=hg: q.activation(sep[:], pq[hg][:], AF.Exp),
                               wait=[(sp, sp.n), (tp, t_pp), (vp, scale_done[0])], inc=[(sp, 1)])
                        ctx.do("scalar", lambda q, hg=hg: q.activation(sen[:], pq[hg][:], AF.Exp, scale=-1.0),
                               wait=[(sp, sp.n)], inc=[(sp, 1)])
                        pp_free[(tt, hg)] = sp.n
                        s_exp = sp.n
                        sep3 = sep[:].rearrange("p (s e) -> p s e", e=64)
                        sen3 = sen[:].rearrange("p (s e) -> p s e", e=64)
                        ssum3 = ssum[:].rearrange("p (s e) -> p s e", e=1)
                        srec3 = srec[:].rearrange("p (s e) -> p s e", e=1)
                        ssum23 = ssum2[:].rearrange("p (s e) -> p s e", e=1)
                        srec23 = srec2[:].rearrange("p (s e) -> p s e", e=1)
                        qfg = qfb3[:, tt, 0:2048].rearrange("p (h f) -> p h f", f=128)
                        dst_p = qfg[:, hg * 8:(hg + 1) * 8, 0:64]
                        dst_n = qfg[:, hg * 8:(hg + 1) * 8, 64:128]
                        ctx.do("vector", lambda q: q.tensor_reduce(ssum3, sep3, axis=AX.X, op=ALU.add),
                               wait=[(sp, s_exp), (vp, vp.n)], inc=[(vp, 1)])
                        ctx.do("vector", lambda q: q.reciprocal(srec[:], ssum[:]), wait=[(vp, vp.n)], inc=[(vp, 1)])
                        ctx.do("vector", lambda q, dst_p=dst_p: q.tensor_tensor(
                            dst_p, sep3, srec3.broadcast_to([128, 8, 64]), op=ALU.mult),
                            wait=[(vp, vp.n)], inc=[(vp, 1)])
                        ctx.do("vector", lambda q: q.tensor_reduce(ssum23, sen3, axis=AX.X, op=ALU.add),
                               wait=[(vp, vp.n)], inc=[(vp, 1)])
                        ctx.do("vector", lambda q: q.reciprocal(srec2[:], ssum2[:]), wait=[(vp, vp.n)], inc=[(vp, 1)])
                        ctx.do("vector", lambda q, dst_n=dst_n: q.tensor_tensor(
                            dst_n, sen3, srec23.broadcast_to([128, 8, 64]), op=ALU.mult),
                            wait=[(vp, vp.n)], inc=[(vp, 1)])
                        scale_done[0] = vp.n
                v_qf = vp.n
                for h in range(H):
                    for chk in range(8):
                        i65 = (b * 16 + h) * 65
                        def trq(q, h=h, chk=chk):
                            q.transpose(ptrb[:, 0:128], kfb3[:, chk, h * 128:(h + 1) * 128], sid[:])
                            return q.transpose(ptrb[:, 128:256], qfb3[:, chk, h * 128:(h + 1) * 128], sid[:])
                        ctx.do("tensor", trq, wait=[(vp, max(v_qf, vp.n))], inc=[(tp, 1)])
                        ctx.do("vector", lambda q: q.tensor_copy(kfT[:], ptrb[:, 0:128]),
                               wait=[(tp, tp.n)], inc=[(vp, 1)])
                        ctx.do("vector", lambda q: q.tensor_copy(qfT[:], ptrb[:, 128:256]),
                               wait=[(vp, vp.n)], inc=[(vp, 1)])
                        ctx.do("vector", lambda q, i65=i65: q.tensor_copy(sbf[:], acc[:, i65:i65 + 65]),
                               wait=[(vp, vp.n)], inc=[(vp, 1)])
                        v_pre = vp.n
                        ctx.do("tensor", lambda q: q.matmul(pat[:], kfT[:], qfT[:], start=True, stop=True),
                               wait=[(vp, v_pre)], inc=[(tp, 1)])
                        ctx.do("vector", lambda q: q.tensor_tensor(pA_sb[:], pat[:], striu[:], op=ALU.mult),
                               wait=[(tp, tp.n)], inc=[(vp, 1)])
                        v_pa = vp.n
                        def ymm(q, b=b, h=h, chk=chk):
                            q.matmul(py[:, 0:64], pA_sb[:], vb3[:, chk, h * 64:(h + 1) * 64], start=True, stop=False)
                            q.matmul(py[:, 64:65], pA_sb[:], onescol[:], start=False, stop=False)
                            q.matmul(py[:, 0:64], qfT[:], sbf[:, 0:64], start=False, stop=False)
                            return q.matmul(py[:, 64:65], qfT[:], sbf[:, 64:65], start=False, stop=True)
                        ctx.do("tensor", ymm, wait=[(vp, v_pa)], inc=[(tp, 1)])
                        t_y = tp.n
                        def umm(q, b=b, h=h, chk=chk):
                            q.matmul(pu[:, 0:64], kfb3[:, chk, h * 128:(h + 1) * 128],
                                     vb3[:, chk, h * 64:(h + 1) * 64], start=True, stop=False)
                            return q.matmul(pu[:, 64:65], kfb3[:, chk, h * 128:(h + 1) * 128], onescol[:],
                                            start=False, stop=True)
                        ctx.do("tensor", umm, inc=[(tp, 1)])
                        t_u = tp.n
                        ctx.do("vector", lambda q, i65=i65: q.tensor_tensor(
                            acc[:, i65:i65 + 65], pu[:], acc[:, i65:i65 + 65], op=ALU.add),
                            wait=[(tp, t_u)], inc=[(vp, 1)])
                        ctx.do("vector", lambda q: q.tensor_scalar(sden[:], py[:, 64:65], 1e-6, None, op0=ALU.max),
                               wait=[(tp, t_y), (vp, vp.n)], inc=[(vp, 1)])
                        ctx.do("vector", lambda q: q.reciprocal(sdr[:], sden[:]), wait=[(vp, vp.n)], inc=[(vp, 1)])
                        ctx.do("vector", lambda q, h=h, chk=chk: q.tensor_scalar(
                            yb3[:, chk, h * 64:(h + 1) * 64], py[:, 0:64], sdr[:, 0:1], None, op0=ALU.mult),
                            wait=[(vp, vp.n)], inc=[(vp, 1)])
                ctx.do("sync", lambda q, b=b: q.dma_start(out=y_sp[b], in_=yb[:]),
                       wait=[(vp, vp.n)], inc=[(stc, 16)])
            ctx.wait("gpsimd", stc, 16 * 2)
            hooks.get("tail", lambda c: None)(ctx)

        prog.build(phase)


def _l2a_io(nc, ki="ExternalInput", ko="ExternalOutput"):
    return dict(
        xnT_in=nc.dram_tensor("xnT_in", [128, 16 * 1025], BF16, kind=ki).ap(),
        kf_in=nc.dram_tensor("kf_in", [128, 16 * 2048], BF16, kind=ki).ap(),
        v_in=nc.dram_tensor("v_in", [128, 16 * 1024], BF16, kind=ki).ap(),
        prefw=nc.dram_tensor("prefw", [128, 8], F32, kind=ki).ap(),
        w_qkv=nc.dram_tensor("w_qkv", [3 * C, C], F32, kind=ki).ap(),
        wq_map=nc.dram_tensor("wq_map", [64, 64], F32, kind=ki).ap(),
        bq_map=nc.dram_tensor("bq_map", [1, 64], F32, kind=ki).ap(),
        id_bf=nc.dram_tensor("id_bf", [128, 128], BF16, kind=ki).ap(),
        triu_f=nc.dram_tensor("triu_f", [128, 128], F32, kind=ki).ap(),
        y_sp=nc.dram_tensor("y_sp", [B, 128, 8 * 1024], BF16, kind=ko).ap(),
    )


def build_L2a():
    nc = bass.Bass("TRN2", target_bir_lowering=False)
    io = _l2a_io(nc)
    st_all = nc.dram_tensor("states_all", [8, 128, 32 * 65], F32, kind="ExternalInput").ap()
    io["states_r"] = lambda r_: st_all[r_]
    es = ExitStack()
    with es:
        prog = Prog(nc, es)
        prog.prefix = "L2a_"
        emit_L2a(nc, prog, **io)
    return nc


def run_L2a(inp, r1):
    nc = build_L2a()
    id_bf = np.eye(128, dtype=BF)
    triu_f = np.triu(np.ones((128, 128), np.float32))
    states_all = np.stack([r1.results[c]["states"] for c in range(R)])
    in_maps = []
    for c in range(R):
        pw_ = np.zeros((128, 8), np.float32)
        pw_[:, :c] = 1.0
        in_maps.append(dict(
            xnT_in=r1.results[c]["xnT_sp"],
            kf_in=r1.results[c]["kf_sp"],
            v_in=r1.results[c]["v_sp"],
            states_all=states_all,
            prefw=pw_,
            w_qkv=inp["L0_attn_w_qkv"],
            wq_map=inp["L0_attn_wq_map"],
            bq_map=np.ascontiguousarray(inp["L0_attn_bq_map"].reshape(1, 64)),
            id_bf=id_bf, triu_f=triu_f,
        ))
    return run_bass_kernel_spmd(nc, in_maps, core_ids=list(range(R)))





def emit_L2b(nc, prog, *, xnT_in, y_in, x_sl, W_r, W_k, W_v, w_out, mu_r, mu_k,
             id_bf, x1_out, xn1T_sp, hooks=None):
    hooks = hooks or {}
    es = ExitStack()
    with es:
        T_ = lambda nm, shp, dt: es.enter_context(nc.sbuf_tensor(prog.prefix + nm, shp, dt))
        P_ = lambda nm, shp, dt: es.enter_context(nc.psum_tensor(prog.prefix + nm, shp, dt))
        sid = T_("sid", [128, 128], BF16)
        stg = T_("stg", [128, 8 * 1024], BF16)
        WrT = T_("WrT", [128, 8 * 1024], BF16)
        WkT = T_("WkT", [128, 8 * 1024], BF16)
        WvT = T_("WvT", [128, 8 * 1024], BF16)
        woT = T_("woT", [128, 8 * 1024], BF16)
        smur = T_("smur", [128, 8], F32)
        smuk = T_("smuk", [128, 8], F32)
        somur = T_("somur", [128, 8], F32)
        somuk = T_("somuk", [128, 8], F32)
        xnTb = T_("xnTb", [128, 8 * 1025], BF16)
        ybuf = T_("ybuf", [128, 8 * 1024], BF16)
        yT = T_("yT", [128, 8 * 1024], BF16)
        xrT = T_("xrT", [128, 8 * 1024], BF16)
        xkT = T_("xkT", [128, 8 * 1024], BF16)
        kact = T_("kact", [128, 8 * 1024], BF16)
        stmp = T_("stmp", [128, 1024], F32)
        sx = T_("sx", [128, 1024], F32)
        sx1 = T_("sx1", [128, 1024], F32)
        sr = T_("sr", [128, 1024], F32)
        srel = T_("srel", [128, 512], F32)
        srow = T_("srow", [128, 1], F32)
        svar = T_("svar", [128, 1], F32)
        sln = T_("sln", [128, 1], F32)
        srstd = T_("srstd", [128, 1], F32)
        seps = T_("seps", [128, 1], F32)
        sxn1 = T_("sxn1", [128, 1024], BF16)

        pA = [P_(f"pA{i}", [128, 512], F32) for i in range(2)]
        pB = [P_(f"pB{i}", [128, 512], F32) for i in range(2)]
        ptrb = P_("ptrb", [128, 1024], BF16)

        xn1Tb = stg
        xnTb3 = xnTb[:].rearrange("p (g c) -> p g c", c=1025)
        stg3 = stg[:].rearrange("p (r c) -> p r c", c=1024)
        yb3 = ybuf[:].rearrange("p (g c) -> p g c", c=1024)
        yT3 = yT[:].rearrange("p (g c) -> p g c", c=1024)
        xrT3 = xrT[:].rearrange("p (g c) -> p g c", c=1024)
        xkT3 = xkT[:].rearrange("p (g c) -> p g c", c=1024)
        kact3 = kact[:].rearrange("p (g c) -> p g c", c=1024)
        WkT3 = WkT[:].rearrange("p (g c) -> p g c", c=1024)
        xn1Tb3 = xn1Tb[:].rearrange("p (g c) -> p g c", c=1024)

        def phase(ctx: Ctx):
            hooks.get("head", lambda c: None)(ctx)
            ld = ctx.sem("ld")
            tp = ctx.sem("tp")
            vp = ctx.sem("vp")
            sp = ctx.sem("sp")
            stc = ctx.sem("stc")

            ctx.do("sync", lambda q: q.dma_start(out=sid[:], in_=id_bf), inc=[(ld, 16)])
            ctx.do("sync", lambda q: q.dma_start(out=smur[:], in_=mu_r), inc=[(ld, 16)])
            ctx.do("sync", lambda q: q.dma_start(out=smuk[:], in_=mu_k), inc=[(ld, 16)])
            ld_mu = ld.n
            ctx.do("vector", lambda q: q.memset(seps[:], EPS))
            ctx.do("vector", lambda q: q.tensor_scalar(somur[:], smur[:], -1.0, 1.0, op0=ALU.mult, op1=ALU.add),
                   wait=[(ld, ld_mu)], inc=[(vp, 1)])
            ctx.do("vector", lambda q: q.tensor_scalar(somuk[:], smuk[:], -1.0, 1.0, op0=ALU.mult, op1=ALU.add),
                   wait=[(vp, vp.n)], inc=[(vp, 1)])
            for (wsrc, wdst) in ((W_r, WrT), (W_k, WkT), (W_v, WvT), (w_out, woT)):
                ctx.do("gpsimd", lambda q, wsrc=wsrc: q.dma_start(
                    out=stg3, in_=wsrc.rearrange("(r p) c -> p r c", p=128)),
                    wait=[(tp, tp.n)], inc=[(ld, 16)])
                ld_st = ld.n
                for dt in range(8):
                    def trg(q, dt=dt):
                        for ct in range(8):
                            ins = q.transpose(ptrb[:, ct * 128:(ct + 1) * 128],
                                              stg3[:, dt, ct * 128:(ct + 1) * 128], sid[:])
                        return ins
                    ctx.do("tensor", trg, wait=[(ld, ld_st), (vp, vp.n)], inc=[(tp, 1)])
                    ctx.do("vector", lambda q, wdst=wdst, dt=dt: q.tensor_copy(
                        wdst[:].rearrange("p (g c) -> p g c", c=1024)[:, :, dt * 128:(dt + 1) * 128],
                        ptrb[:].rearrange("p (a c) -> p a c", c=128)),
                        wait=[(tp, tp.n)], inc=[(vp, 1)])
            for b in range(B):
                v_start = vp.n
                ctx.do("sync", lambda q, b=b: q.dma_start(
                    out=xnTb[:], in_=xnT_in[:, b * 8 * 1025:(b + 1) * 8 * 1025]),
                    wait=[(vp, v_start), (stc, stc.n)], inc=[(ld, 16)])
                ctx.do("sync", lambda q, b=b: q.dma_start(out=ybuf[:], in_=y_in[b]), inc=[(ld, 16)])
                ld_b = ld.n
                for ct in range(8):
                    for (dst3, mu_s, omu_s, tag) in ((xrT3, smur, somur, 0), (xkT3, smuk, somuk, 1)):
                        ctx.do("vector", lambda q, ct=ct, omu_s=omu_s: q.tensor_scalar(
                            stmp[:], xnTb3[:, ct, 0:1024], omu_s[:, ct:ct + 1], None, op0=ALU.mult),
                            wait=[(ld, ld_b), (vp, vp.n)], inc=[(vp, 1)])
                        ctx.do("vector", lambda q, ct=ct, dst3=dst3, mu_s=mu_s: q.scalar_tensor_tensor(
                            dst3[:, ct, :], xnTb3[:, ct, 1:1025], mu_s[:, ct:ct + 1], stmp[:],
                            op0=ALU.mult, op1=ALU.add),
                            wait=[(vp, vp.n)], inc=[(vp, 1)])
                v_lerp = vp.n
                for tt in range(8):
                    def tryy(q, tt=tt):
                        for dt in range(8):
                            ins = q.transpose(ptrb[:, dt * 128:(dt + 1) * 128],
                                              yb3[:, tt, dt * 128:(dt + 1) * 128], sid[:])
                        return ins
                    ctx.do("tensor", tryy, wait=[(ld, ld_b), (vp, vp.n)], inc=[(tp, 1)])
                    ctx.do("vector", lambda q, tt=tt: q.tensor_copy(
                        yT3[:, :, tt * 128:(tt + 1) * 128],
                        ptrb[:].rearrange("p (a c) -> p a c", c=128)),
                        wait=[(tp, tp.n)], inc=[(vp, 1)])
                for dt in range(8):
                    for ch in range(2):
                        def kmm(q, dt=dt, ch=ch):
                            for ct in range(8):
                                ins = q.matmul(pB[ch][:], WkT3[:, ct, dt * 128:(dt + 1) * 128],
                                               xkT3[:, ct, ch * 512:(ch + 1) * 512],
                                               start=(ct == 0), stop=(ct == 7))
                            return ins
                        ctx.do("tensor", kmm, wait=[(vp, v_lerp), (sp, sp.n)], inc=[(tp, 1)])
                        t_k = tp.n
                        ctx.do("scalar", lambda q, ch=ch: q.activation(srel[:], pB[ch][:], AF.Relu),
                               wait=[(tp, t_k), (sp, sp.n)], inc=[(sp, 1)])
                        ctx.do("scalar", lambda q, dt=dt, ch=ch: q.activation(
                            kact3[:, dt, ch * 512:(ch + 1) * 512], srel[:], AF.Square),
                            wait=[(sp, sp.n)], inc=[(sp, 1)])
                s_kact = sp.n
                pa_free = sp.n
                for tt in range(8):
                    for ch in range(2):
                        def amm(q, tt=tt, ch=ch):
                            for dt in range(8):
                                ins = q.matmul(pA[ch][:], yT3[:, dt, tt * 128:(tt + 1) * 128],
                                               woT[:, dt * 1024 + ch * 512: dt * 1024 + (ch + 1) * 512],
                                               start=(dt == 0), stop=(dt == 7))
                            return ins
                        ctx.do("tensor", amm, wait=[(vp, vp.n)], inc=[(tp, 1)])
                    t_a = tp.n
                    for ch in range(2):
                        def rmm(q, tt=tt, ch=ch):
                            for ct in range(8):
                                ins = q.matmul(pB[ch][:], xrT3[:, ct, tt * 128:(tt + 1) * 128],
                                               WrT[:, ct * 1024 + ch * 512: ct * 1024 + (ch + 1) * 512],
                                               start=(ct == 0), stop=(ct == 7))
                            return ins
                        ctx.do("tensor", rmm, wait=[(sp, max(s_kact, sp.n)), (vp, vp.n)], inc=[(tp, 1)])
                        t_r = tp.n
                        ctx.do("scalar", lambda q, tt=tt, ch=ch: q.activation(
                            sr[:, ch * 512:(ch + 1) * 512], pB[ch][:], AF.Sigmoid),
                            wait=[(tp, t_r), (sp, sp.n)], inc=[(sp, 1)])
                    s_r = sp.n
                    ctx.do("sync", lambda q, b=b, tt=tt: q.dma_start(
                        out=sx[:], in_=x_sl[b, tt * 128:(tt + 1) * 128, :]),
                        wait=[(vp, vp.n)], inc=[(ld, 16)])
                    ld_x = ld.n
                    for ch in range(2):
                        def fmm(q, tt=tt, ch=ch):
                            for dt in range(8):
                                ins = q.matmul(pB[ch][:], kact3[:, dt, tt * 128:(tt + 1) * 128],
                                               WvT[:, dt * 1024 + ch * 512: dt * 1024 + (ch + 1) * 512],
                                               start=(dt == 0), stop=(dt == 7))
                            return ins
                        ctx.do("tensor", fmm, wait=[(sp, s_r)], inc=[(tp, 1)])
                    t_f = tp.n
                    for ch in range(2):
                        ctx.do("vector", lambda q, ch=ch: q.tensor_tensor(
                            sx1[:, ch * 512:(ch + 1) * 512], pA[ch][:], sx[:, ch * 512:(ch + 1) * 512],
                            op=ALU.add),
                            wait=[(tp, t_f), (ld, ld_x), (vp, vp.n)], inc=[(vp, 1)])
                        ctx.do("vector", lambda q, ch=ch: q.tensor_tensor(
                            stmp[:, 0:512], pB[ch][:], sr[:, ch * 512:(ch + 1) * 512], op=ALU.mult),
                            wait=[(vp, vp.n)], inc=[(vp, 1)])
                        ctx.do("vector", lambda q, ch=ch: q.tensor_tensor(
                            sx1[:, ch * 512:(ch + 1) * 512], sx1[:, ch * 512:(ch + 1) * 512],
                            stmp[:, 0:512], op=ALU.add),
                            wait=[(vp, vp.n)], inc=[(vp, 1)])
                    v_x1 = vp.n
                    ctx.do("sync", lambda q, b=b, tt=tt: q.dma_start(
                        out=x1_out[b, tt * 128:(tt + 1) * 128, :], in_=sx1[:]),
                        wait=[(vp, v_x1)], inc=[(stc, 16)])
                    ctx.do("vector", lambda q: q.reduce_sum(srow[:], sx1[:], axis=AX.X),
                           wait=[(vp, v_x1)], inc=[(vp, 1)])
                    ctx.do("vector", lambda q: q.tensor_scalar(srow[:], srow[:], -1.0 / C, None, op0=ALU.mult),
                           wait=[(vp, vp.n)], inc=[(vp, 1)])
                    ctx.do("scalar", lambda q: q.activation(sx[:], sx1[:], AF.Identity, bias=srow[:, 0:1]),
                           wait=[(vp, vp.n), (sp, sp.n), (stc, stc.n)], inc=[(sp, 1)])
                    ctx.do("scalar", lambda q: q.activation(sx1[:], sx[:], AF.Square, accum_out=svar[:, 0:1]),
                           wait=[(sp, sp.n)], inc=[(sp, 1)])
                    ctx.do("scalar", lambda q: q.activation(sln[:], svar[:], AF.Ln, scale=1.0 / C, bias=seps[:, 0:1]),
                           wait=[(sp, sp.n)], inc=[(sp, 1)])
                    ctx.do("scalar", lambda q: q.activation(srstd[:], sln[:], AF.Exp, scale=-0.5),
                           wait=[(sp, sp.n)], inc=[(sp, 1)])
                    ctx.do("vector", lambda q: q.tensor_scalar(sxn1[:], sx[:], srstd[:, 0:1], None, op0=ALU.mult),
                           wait=[(sp, sp.n), (vp, vp.n)], inc=[(vp, 1)])
                    v_xn1 = vp.n
                    def trx1(q):
                        for ct in range(8):
                            ins = q.transpose(ptrb[:, ct * 128:(ct + 1) * 128],
                                              sxn1[:, ct * 128:(ct + 1) * 128], sid[:])
                        return ins
                    ctx.do("tensor", trx1, wait=[(vp, v_xn1)], inc=[(tp, 1)])
                    ctx.do("vector", lambda q, tt=tt: q.tensor_copy(
                        xn1Tb3[:, :, tt * 128:(tt + 1) * 128],
                        ptrb[:].rearrange("p (a c) -> p a c", c=128)),
                        wait=[(tp, tp.n)], inc=[(vp, 1)])
                ctx.do("sync", lambda q, b=b: q.dma_start(
                    out=xn1T_sp[:, b * 8 * 1024:(b + 1) * 8 * 1024], in_=xn1Tb[:]),
                    wait=[(vp, vp.n)], inc=[(stc, 16)])
            ctx.wait("gpsimd", stc, stc.n)
            hooks.get("tail", lambda c: None)(ctx)

        prog.build(phase)


def _l2b_io(nc, ki="ExternalInput", ko="ExternalOutput"):
    return dict(
        xnT_in=nc.dram_tensor("xnT_in", [128, 16 * 1025], BF16, kind=ki).ap(),
        y_in=nc.dram_tensor("y_in", [B, 128, 8 * 1024], BF16, kind=ki).ap(),
        x_sl=nc.dram_tensor("x_sl", [B, TC, C], F32, kind=ki).ap(),
        W_r=nc.dram_tensor("W_r", [C, C], F32, kind=ki).ap(),
        W_k=nc.dram_tensor("W_k", [C, C], F32, kind=ki).ap(),
        W_v=nc.dram_tensor("W_v", [C, C], F32, kind=ki).ap(),
        w_out=nc.dram_tensor("w_out", [C, C], F32, kind=ki).ap(),
        mu_r=nc.dram_tensor("mu_r", [128, 8], F32, kind=ki).ap(),
        mu_k=nc.dram_tensor("mu_k", [128, 8], F32, kind=ki).ap(),
        id_bf=nc.dram_tensor("id_bf", [128, 128], BF16, kind=ki).ap(),
        x1_out=nc.dram_tensor("x1_out", [B, TC, C], F32, kind=ko).ap(),
        xn1T_sp=nc.dram_tensor("xn1T_sp", [128, 16 * 1024], BF16, kind=ko).ap(),
    )


def build_L2b():
    nc = bass.Bass("TRN2", target_bir_lowering=False)
    io = _l2b_io(nc)
    es = ExitStack()
    with es:
        prog = Prog(nc, es)
        prog.prefix = "L2b_"
        emit_L2b(nc, prog, **io)
    return nc


def run_L2b(inp, r1, r2a):
    nc = build_L2b()
    id_bf = np.eye(128, dtype=BF)
    xs, _ = _split_inputs(inp["L0_attn_x"])
    mu_r = np.ascontiguousarray(inp["L0_ffn_mu_r"].reshape(8, 128).T)
    mu_k = np.ascontiguousarray(inp["L0_ffn_mu_k"].reshape(8, 128).T)
    in_maps = []
    for c in range(R):
        in_maps.append(dict(
            xnT_in=r1.results[c]["xnT_sp"],
            y_in=r2a.results[c]["y_sp"],
            x_sl=xs[c],
            W_r=inp["L0_ffn_W_r"], W_k=inp["L0_ffn_W_k"], W_v=inp["L0_ffn_W_v"],
            w_out=inp["L0_attn_w_out"],
            mu_r=mu_r, mu_k=mu_k, id_bf=id_bf,
        ))
    return run_bass_kernel_spmd(nc, in_maps, core_ids=list(range(R)))





def emit_L3(nc, prog, *, xn1T_in, Wq, Wk, Wv, w_red, wg, id_bf, id_f,
            qT_out, k_out, v_out, bs_out, gsc_sp, hooks=None):
    hooks = hooks or {}
    es = ExitStack()
    with es:
        T_ = lambda nm, shp, dt: es.enter_context(nc.sbuf_tensor(prog.prefix + nm, shp, dt))
        P_ = lambda nm, shp, dt: es.enter_context(nc.psum_tensor(prog.prefix + nm, shp, dt))
        sid = T_("sid", [128, 128], BF16)
        sidf = T_("sidf", [128, 128], F32)
        stg = T_("stg", [128, 8 * 1024], BF16)
        WqT = T_("WqT", [128, 8 * 1024], BF16)
        WkT = T_("WkT", [128, 8 * 1024], BF16)
        WvT = T_("WvT", [128, 8 * 1024], BF16)
        xn1T = T_("xn1T", [128, 16 * 1024], BF16)
        qTb = T_("qTb", [128, 8 * 1024], BF16)
        ksb = T_("ksb", [128, 8 * 1024], BF16)
        vsb2 = T_("vsb2", [128, 8 * 1024], BF16)
        swred = T_("swred", [128, 8 * 16], BF16)
        swg = T_("swg", [8, 16], F32)
        swgb = T_("swgb", [8, 16], BF16)
        ssq8 = T_("ssq8", [8, 16], F32)
        snrm = T_("snrm", [8, 1], F32)
        srn = T_("srn", [8, 1], F32)
        sdiag = T_("sdiag", [8, 8], BF16)
        swgnT = T_("swgnT", [16, 8], BF16)
        sbsd = T_("sbsd", [128, 64], F32)
        sbsj = T_("sbsj", [8, 1024], F32)
        redT = T_("redT", [16, 1024], BF16)
        slog = T_("slog", [128, 8], F32)
        sm = T_("sm", [128, 1], F32)
        snm = T_("snm", [128, 1], F32)
        sexp = T_("sexp", [128, 8], F32)
        ssm = T_("ssm", [128, 1], F32)
        ssc = T_("ssc", [128, 1], F32)
        sind = T_("sind", [128, 8], F32)
        gsc = T_("gsc", [128, 16 * 8], F32)

        pq = [P_(f"pq{i}", [128, 512], F32) for i in range(2)]
        ptrb = P_("ptrb", [128, 1024], BF16)
        pbsj = P_("pbsj", [8, 128], F32)
        pl = P_("pl", [128, 8], F32)
        psm = P_("psm", [16, 8], F32)

        xn1T3 = xn1T[:].rearrange("p (g c) -> p g c", c=1024)
        stg3 = stg[:].rearrange("p (r c) -> p r c", c=1024)
        WqT3 = WqT[:].rearrange("p (g c) -> p g c", c=1024)
        WkT3 = WkT[:].rearrange("p (g c) -> p g c", c=1024)
        WvT3 = WvT[:].rearrange("p (g c) -> p g c", c=1024)
        qTb3 = qTb[:].rearrange("p (g c) -> p g c", c=1024)
        ksb3 = ksb[:].rearrange("p (g c) -> p g c", c=1024)
        vsb3 = vsb2[:].rearrange("p (g c) -> p g c", c=1024)
        swred3 = swred[:].rearrange("p (g c) -> p g c", c=16)

        def phase(ctx: Ctx):
            hooks.get("head", lambda c: None)(ctx)
            ld = ctx.sem("ld")
            tp = ctx.sem("tp")
            vp = ctx.sem("vp")
            sp = ctx.sem("sp")
            stc = ctx.sem("stc")

            ctx.do("sync", lambda q: q.dma_start(out=sid[:], in_=id_bf), inc=[(ld, 16)])
            ctx.do("sync", lambda q: q.dma_start(out=sidf[:], in_=id_f), inc=[(ld, 16)])
            ctx.do("sync", lambda q: q.dma_start(out=xn1T[:], in_=xn1T_in), inc=[(ld, 16)])
            ctx.do("sync", lambda q: q.dma_start(out=swg[:], in_=wg), inc=[(ld, 16)])
            ctx.do("gpsimd", lambda q: q.dma_start(out=swgb[:], in_=wg), inc=[(ld, 16)])
            ctx.do("gpsimd", lambda q: q.dma_start(out=swred3, in_=w_red.rearrange("(r p) g -> p r g", p=128)),
                   inc=[(ld, 16)])
            ld_w = ld.n
            for (wsrc, wdst) in ((Wq, WqT), (Wk, WkT), (Wv, WvT)):
                ctx.do("gpsimd", lambda q, wsrc=wsrc: q.dma_start(
                    out=stg3, in_=wsrc.rearrange("(r p) c -> p r c", p=128)),
                    wait=[(tp, tp.n)], inc=[(ld, 16)])
                ld_st = ld.n
                for dt in range(8):
                    def trg(q, dt=dt):
                        for ct in range(8):
                            ins = q.transpose(ptrb[:, ct * 128:(ct + 1) * 128],
                                              stg3[:, dt, ct * 128:(ct + 1) * 128], sid[:])
                        return ins
                    ctx.do("tensor", trg, wait=[(ld, ld_st), (vp, vp.n)], inc=[(tp, 1)])
                    ctx.do("vector", lambda q, wdst=wdst, dt=dt: q.tensor_copy(
                        wdst[:].rearrange("p (g c) -> p g c", c=1024)[:, :, dt * 128:(dt + 1) * 128],
                        ptrb[:].rearrange("p (a c) -> p a c", c=128)),
                        wait=[(tp, tp.n)], inc=[(vp, 1)])
            ctx.do("vector", lambda q: q.tensor_tensor(ssq8[:], swg[:], swg[:], op=ALU.mult),
                   wait=[(ld, ld_w)], inc=[(vp, 1)])
            ctx.do("vector", lambda q: q.reduce_sum(snrm[:], ssq8[:], axis=AX.X),
                   wait=[(vp, vp.n)], inc=[(vp, 1)])
            ctx.do("scalar", lambda q: q.activation(srn[:], snrm[:], AF.Ln), wait=[(vp, vp.n)], inc=[(sp, 1)])
            ctx.do("scalar", lambda q: q.activation(snrm[:], srn[:], AF.Exp, scale=0.5),
                   wait=[(sp, sp.n)], inc=[(sp, 1)])
            ctx.do("vector", lambda q: q.tensor_scalar(snrm[:], snrm[:], 1e-4, None, op0=ALU.max),
                   wait=[(sp, sp.n)], inc=[(vp, 1)])
            ctx.do("vector", lambda q: q.reciprocal(srn[:], snrm[:]), wait=[(vp, vp.n)], inc=[(vp, 1)])
            ctx.do("vector", lambda q: q.tensor_scalar(sdiag[:], sidf[0:8, 0:8], srn[:, 0:1], None, op0=ALU.mult),
                   wait=[(vp, vp.n)], inc=[(vp, 1)])
            v_diag = vp.n
            ctx.do("tensor", lambda q: q.matmul(psm[:], swgb[:], sdiag[:], start=True, stop=True),
                   wait=[(vp, v_diag), (ld, ld_w)], inc=[(tp, 1)])
            ctx.do("vector", lambda q: q.tensor_copy(swgnT[:], psm[:]), wait=[(tp, tp.n)], inc=[(vp, 1)])
            v_wgnT = vp.n
            for b in range(B):
                for dt in range(8):
                    for ch in range(2):
                        def qmm(q, b=b, dt=dt, ch=ch):
                            for ht in range(8):
                                ins = q.matmul(pq[ch][:], WqT3[:, ht, dt * 128:(dt + 1) * 128],
                                               xn1T3[:, b * 8 + ht, ch * 512:(ch + 1) * 512],
                                               start=(ht == 0), stop=(ht == 7))
                            return ins
                        ctx.do("tensor", qmm, wait=[(vp, vp.n), (sp, sp.n)], inc=[(tp, 1)])
                        t_q = tp.n
                        ctx.do("scalar", lambda q, b=b, dt=dt, ch=ch: q.activation(
                            qTb3[:, dt, ch * 512:(ch + 1) * 512], pq[ch][:], AF.Identity,
                            scale=(1.0 if b == 0 else -1.0)),
                            wait=[(tp, t_q), (sp, sp.n)], inc=[(sp, 1)])
                ctx.do("sync", lambda q, b=b: q.dma_start(
                    out=qT_out(b), in_=qTb[:].rearrange("p (dt t) -> p dt t", t=1024)),
                    wait=[(sp, sp.n)], inc=[(stc, 16)])
                for (wT3, dsb, dsb3, out_fn) in ((WkT3, ksb, ksb3, k_out), (WvT3, vsb2, vsb3, v_out)):
                    for tt in range(8):
                        for ch in range(2):
                            def kmm(q, b=b, tt=tt, ch=ch, wT3=wT3):
                                for ct in range(8):
                                    ins = q.matmul(pq[ch][:], xn1T3[:, b * 8 + ct, tt * 128:(tt + 1) * 128],
                                                   wT3[:, ct, ch * 512:(ch + 1) * 512],
                                                   start=(ct == 0), stop=(ct == 7))
                                return ins
                            ctx.do("tensor", kmm, wait=[(vp, vp.n), (sp, sp.n), (stc, stc.n)], inc=[(tp, 1)])
                            ctx.do("vector", lambda q, tt=tt, ch=ch, dsb3=dsb3: q.tensor_copy(
                                dsb3[:, tt, ch * 512:(ch + 1) * 512], pq[ch][:]),
                                wait=[(tp, tp.n)], inc=[(vp, 1)])
                    ctx.do("sync", lambda q, dsb=dsb, out_fn=out_fn, b=b: q.dma_start(out=out_fn(b), in_=dsb[:]),
                           wait=[(vp, vp.n)], inc=[(stc, 16)])
                for tt in range(8):
                    def trv(q, tt=tt):
                        for dt in range(8):
                            ins = q.transpose(ptrb[:, dt * 128:(dt + 1) * 128],
                                              vsb3[:, tt, dt * 128:(dt + 1) * 128], sid[:])
                        return ins
                    ctx.do("tensor", trv, wait=[(vp, vp.n)], inc=[(tp, 1)])
                    ctx.do("vector", lambda q, tt=tt: q.tensor_reduce(
                        sbsd[:].rearrange("p (dt j) -> p dt j", j=8)[:, :, tt:tt + 1],
                        ptrb[:].rearrange("p (dt c) -> p dt c", c=128), axis=AX.X, op=ALU.add),
                        wait=[(tp, tp.n)], inc=[(vp, 1)])
                v_bs = vp.n
                for dt in range(8):
                    ctx.do("tensor", lambda q, dt=dt: q.transpose(
                        pbsj[:], sbsd[:, dt * 8:(dt + 1) * 8], sidf[:]),
                        wait=[(vp, v_bs)] if dt == 0 else [(vp, vp.n)], inc=[(tp, 1)])
                    ctx.do("vector", lambda q, dt=dt: q.tensor_copy(
                        sbsj[:, dt * 128:(dt + 1) * 128], pbsj[:]),
                        wait=[(tp, tp.n)], inc=[(vp, 1)])
                ctx.do("sync", lambda q, b=b: q.dma_start(out=bs_out(b), in_=sbsj[:]),
                       wait=[(vp, vp.n)], inc=[(stc, 16)])
                for ch in range(2):
                    def rmm(q, b=b, ch=ch):
                        for ct in range(8):
                            ins = q.matmul(pq[ch][0:16, :], swred3[:, ct, :],
                                           xn1T3[:, b * 8 + ct, ch * 512:(ch + 1) * 512],
                                           start=(ct == 0), stop=(ct == 7))
                        return ins
                    ctx.do("tensor", rmm, wait=[(vp, vp.n)], inc=[(tp, 1)])
                    ctx.do("vector", lambda q, ch=ch: q.tensor_copy(
                        redT[:, ch * 512:(ch + 1) * 512], pq[ch][0:16, :]),
                        wait=[(tp, tp.n)], inc=[(vp, 1)])
                v_red = vp.n
                for tt in range(8):
                    ctx.do("tensor", lambda q, tt=tt: q.matmul(
                        pl[:], redT[:, tt * 128:(tt + 1) * 128], swgnT[:], start=True, stop=True),
                        wait=[(vp, max(v_red, v_wgnT)), (vp, vp.n)], inc=[(tp, 1)])
                    t_l = tp.n
                    ctx.do("vector", lambda q: q.tensor_copy(slog[:], pl[:]), wait=[(tp, t_l), (vp, vp.n)],
                           inc=[(vp, 1)])
                    ctx.do("vector", lambda q: q.reduce_max(sm[:], slog[:], axis=AX.X),
                           wait=[(vp, vp.n)], inc=[(vp, 1)])
                    ctx.do("vector", lambda q: q.tensor_scalar(snm[:], sm[:], -1.0, None, op0=ALU.mult),
                           wait=[(vp, vp.n)], inc=[(vp, 1)])
                    v_nm = vp.n
                    ctx.do("scalar", lambda q: q.activation(sexp[:], slog[:], AF.Exp, bias=snm[:, 0:1]),
                           wait=[(vp, v_nm), (sp, sp.n)], inc=[(sp, 1)])
                    s_e = sp.n
                    ctx.do("vector", lambda q: q.reduce_sum(ssm[:], sexp[:], axis=AX.X),
                           wait=[(sp, s_e), (vp, vp.n)], inc=[(vp, 1)])
                    ctx.do("vector", lambda q: q.reciprocal(ssc[:], ssm[:]), wait=[(vp, vp.n)], inc=[(vp, 1)])
                    ctx.do("vector", lambda q: q.tensor_scalar(sind[:], slog[:], sm[:, 0:1], None, op0=ALU.is_equal),
                           wait=[(vp, vp.n)], inc=[(vp, 1)])
                    ctx.do("vector", lambda q, b=b, tt=tt: q.tensor_scalar(
                        gsc[:, (b * 8 + tt) * 8:(b * 8 + tt + 1) * 8], sind[:], ssc[:, 0:1], None, op0=ALU.mult),
                        wait=[(vp, vp.n)], inc=[(vp, 1)])
            ctx.do("sync", lambda q: q.dma_start(out=gsc_sp, in_=gsc[:]), wait=[(vp, vp.n)], inc=[(stc, 16)])
            ctx.wait("gpsimd", stc, stc.n)
            hooks.get("tail", lambda c: None)(ctx)

        prog.build(phase)


def build_L3():
    nc = bass.Bass("TRN2", target_bir_lowering=False)
    qTn_sp = nc.dram_tensor("qTn_sp", [B, 128, 8 * 1024], BF16, kind="ExternalOutput").ap()
    k_sp = nc.dram_tensor("k_sp", [B, 128, 8 * 1024], BF16, kind="ExternalOutput").ap()
    v_sp = nc.dram_tensor("v_sp", [B, 128, 8 * 1024], BF16, kind="ExternalOutput").ap()
    bs_sp = nc.dram_tensor("bs_sp", [B, 8, 1024], F32, kind="ExternalOutput").ap()
    io = dict(
        xn1T_in=nc.dram_tensor("xn1T_in", [128, 16 * 1024], BF16, kind="ExternalInput").ap(),
        Wq=nc.dram_tensor("Wq", [C, C], F32, kind="ExternalInput").ap(),
        Wk=nc.dram_tensor("Wk", [C, C], F32, kind="ExternalInput").ap(),
        Wv=nc.dram_tensor("Wv", [C, C], F32, kind="ExternalInput").ap(),
        w_red=nc.dram_tensor("w_red", [C, 16], F32, kind="ExternalInput").ap(),
        wg=nc.dram_tensor("wg", [8, 16], F32, kind="ExternalInput").ap(),
        id_bf=nc.dram_tensor("id_bf", [128, 128], BF16, kind="ExternalInput").ap(),
        id_f=nc.dram_tensor("id_f", [128, 128], F32, kind="ExternalInput").ap(),
        gsc_sp=nc.dram_tensor("gsc_sp", [128, 16 * 8], F32, kind="ExternalOutput").ap(),
        qT_out=lambda b: qTn_sp[b].rearrange("p (dt t) -> p dt t", t=1024),
        k_out=lambda b: k_sp[b],
        v_out=lambda b: v_sp[b],
        bs_out=lambda b: bs_sp[b],
    )
    es = ExitStack()
    with es:
        prog = Prog(nc, es)
        prog.prefix = "L3_"
        emit_L3(nc, prog, **io)
    return nc


def run_L3(inp, r2b):
    nc = build_L3()
    id_bf = np.eye(128, dtype=BF)
    id_f = np.eye(128, dtype=np.float32)
    in_maps = []
    for c in range(R):
        in_maps.append(dict(
            xn1T_in=r2b.results[c]["xn1T_sp"],
            Wq=inp["L1_attn_Wq"], Wk=inp["L1_attn_Wk"], Wv=inp["L1_attn_Wv"],
            w_red=inp["L1_ffn_w_red"], wg=inp["L1_ffn_wg"],
            id_bf=id_bf, id_f=id_f,
        ))
    return run_bass_kernel_spmd(nc, in_maps, core_ids=list(range(R)))





PAIR_STEPS = [16, 32, 48, 64]
STEP_OFF = [0, 16, 48, 96]


def emit_L4(nc, prog, *, kM_src, v_src, emit_pair_loads, emit_bs_load, at_dst,
            biasT, cm0, cm1, triu_b, ntriu_b, id_bf, hooks=None):
    hooks = hooks or {}
    es = ExitStack()
    with es:
        T_ = lambda nm, shp, dt: es.enter_context(nc.sbuf_tensor(prog.prefix + nm, shp, dt))
        P_ = lambda nm, shp, dt: es.enter_context(nc.psum_tensor(prog.prefix + nm, shp, dt))
        sid = T_("sid", [128, 128], BF16)
        striu = T_("striu", [128, 128], BF16)
        sntriu = T_("sntriu", [128, 128], BF16)
        sqt = T_("sqt", [128, 8 * 2 * 2 * 128], BF16)
        svd = T_("svd", [128, 2 * 2 * 1024], BF16)
        sbsj = T_("sbsj", [64, 2 * 1024], F32)
        scm0 = T_("scm0", [64, 8 * 128], F32)
        scm1 = T_("scm1", [64, 8 * 128], F32)
        smt = [T_(f"smt{i}", [128, 8 * 2 * 128], BF16) for i in range(2)]
        sbias = [T_(f"sbias{i}", [128, 256], BF16) for i in range(2)]
        svt = [T_(f"svt{i}", [128, 2 * 512], BF16) for i in range(2)]
        pcache = T_("pcache", [128, 72 * 256], BF16)
        sx1e = T_("sx1e", [128, 512], F32)
        so = T_("so", [128, 512], F32)

        pd = [P_(f"pd{i}", [128, 256], F32) for i in range(2)]
        pav = [P_(f"pav{i}", [128, 512], F32) for i in range(4)]
        px = [P_(f"px{i}", [128, 512], F32) for i in range(2)]

        sqt5 = sqt[:].rearrange("p (dt b k i) -> p dt b k i", b=2, k=2, i=128)
        svd4 = svd[:].rearrange("p (k b d) -> p k b d", k=2, b=2)
        sbsj3 = sbsj[:].rearrange("j (b d) -> j b d", b=2)
        scm03 = scm0[:].rearrange("j (o i) -> j o i", i=128)
        scm13 = scm1[:].rearrange("j (o i) -> j o i", i=128)
        pc3 = pcache[:].rearrange("p (s i) -> p s i", i=256)

        def phase(ctx: Ctx):
            hooks.get("head", lambda c: None)(ctx)
            ld = ctx.sem("ld")
            lm = ctx.sem("lm")
            lv = ctx.sem("lv")
            tp = ctx.sem("tp")
            vp = ctx.sem("vp")
            sp = ctx.sem("sp")
            stc = ctx.sem("stc")

            ctx.do("sync", lambda q: q.dma_start(out=sid[:], in_=id_bf), inc=[(ld, 16)])
            ctx.do("sync", lambda q: q.dma_start(out=striu[:], in_=triu_b), inc=[(ld, 16)])
            ctx.do("sync", lambda q: q.dma_start(out=sntriu[:], in_=ntriu_b), inc=[(ld, 16)])
            emit_bs_load(ctx, ld, sbsj)
            ctx.do("sync", lambda q: q.dma_start(out=scm0[:], in_=cm0), inc=[(ld, 16)])
            ctx.do("sync", lambda q: q.dma_start(out=scm1[:], in_=cm1), inc=[(ld, 16)])
            ld_const = ld.n

            for p in range(4):
                nsteps = PAIR_STEPS[p]
                soff = STEP_OFF[p]
                emit_pair_loads(ctx, ld, tp, p, sqt, svd)
                ld_pair = ld.n
                mm_done = {}
                for J in range(nsteps):
                    wbuf = [(tp, mm_done[J - 2])] if J - 2 in mm_done else [(tp, tp.n)]
                    for bb_ in range(2):
                        ctx.do("sync", lambda q, J=J, bb_=bb_: q.dma_start(
                            out=smt[J % 2][:].rearrange("p (dt b j) -> p dt b j", b=2, j=128)[:, :, bb_, :],
                            in_=kM_src(bb_, J)),
                            wait=wbuf if bb_ == 0 else [], inc=[(lm, 16)])
                    ctx.do("sync", lambda q, J=J, soff=soff: q.dma_start(
                        out=sbias[J % 2][:], in_=biasT[soff + J]), inc=[(lm, 16)])
                    ld_m = lm.n
                    def dmm(q, J=J):
                        sm3 = smt[J % 2][:].rearrange("p (dt b j) -> p dt b j", b=2, j=128)
                        q.matmul(pd[J % 2][:], sid[:], sbias[J % 2][:], start=True, stop=False)
                        for b in range(2):
                            for dt in range(8):
                                ins = q.matmul(pd[J % 2][:], sm3[:, dt, b, :],
                                               sqt[:, dt * 512 + b * 256: dt * 512 + (b + 1) * 256],
                                               start=False, stop=(b == 1 and dt == 7))
                        return ins
                    wd = [(lm, ld_m), (ld, ld_pair)]
                    if J < 2:
                        wd.append((sp, sp.n))
                    ctx.do("tensor", dmm, wait=wd, inc=[(tp, 1)])
                    mm_done[J] = tp.n
                    t_d = tp.n
                    ctx.do("scalar", lambda q, J=J, soff=soff: q.activation(
                        pc3[:, J, :], pd[J % 2][:], AF.Sigmoid, scale=1.0 / 32.0),
                        wait=[(tp, t_d), (sp, sp.n)], inc=[(sp, 1)])
                s_pc = sp.n
                for dh in range(2):
                    av_done = {}
                    for J in range(nsteps):
                        wbuf = [(tp, av_done[J - 2])] if J - 2 in av_done else [(tp, tp.n)]
                        for bb_ in range(2):
                            ctx.do("sync", lambda q, J=J, dh=dh, bb_=bb_: q.dma_start(
                                out=svt[J % 2][:, bb_ * 512:(bb_ + 1) * 512],
                                in_=v_src(bb_, J, dh)),
                                wait=wbuf if bb_ == 0 else [], inc=[(lv, 16)])
                        ld_v = lv.n
                        def avmm(q, J=J, dh=dh, nsteps=nsteps):
                            sv3 = svt[J % 2][:].rearrange("p (b d) -> p b d", b=2)
                            pa_ = pc3[:, J, 0:128]
                            pb_ = pc3[:, J, 128:256]
                            st = (J == 0)
                            last = (J == nsteps - 1)
                            q.matmul(pav[0][:], pa_, sv3[:, 0, :], start=st, stop=False)
                            q.matmul(pav[1][:], pa_, sv3[:, 1, :], start=st, stop=last)
                            q.matmul(pav[2][:], pb_, sv3[:, 0, :], start=st, stop=False)
                            return q.matmul(pav[3][:], pb_, sv3[:, 1, :], start=st, stop=last)
                        ctx.do("tensor", avmm,
                               wait=[(lv, ld_v)] + ([(sp, s_pc), (vp, vp.n)] if J == 0 else []),
                               inc=[(tp, 1)])
                        av_done[J] = tp.n
                    def sufmm(q, p=p, dh=dh):
                        for ki, blk in enumerate(range(2)):
                            o = 2 * p + blk
                            q.matmul(pav[2 * blk][:], scm03[:, o, :], sbsj3[:, 0, dh * 512:(dh + 1) * 512],
                                     start=False, stop=False)
                            q.matmul(pav[2 * blk][:], sntriu[:], svd4[:, blk, 0, dh * 512:(dh + 1) * 512],
                                     start=False, stop=True)
                            q.matmul(px[blk][:], scm13[:, o, :], sbsj3[:, 1, dh * 512:(dh + 1) * 512],
                                     start=True, stop=False)
                            q.matmul(px[blk][:], striu[:], svd4[:, blk, 1, dh * 512:(dh + 1) * 512],
                                     start=False, stop=False)
                            ins = q.matmul(px[blk][:], sntriu[:], svd4[:, blk, 1, dh * 512:(dh + 1) * 512],
                                           start=False, stop=False)
                        return ins
                    ctx.do("tensor", sufmm, wait=[(ld, ld_const), (sp, sp.n), (vp, vp.n)], inc=[(tp, 1)])
                    t_suf = tp.n
                    for blk in range(2):
                        ctx.do("vector", lambda q, blk=blk: q.tensor_copy(so[:], pav[2 * blk][:]),
                               wait=[(tp, t_suf), (vp, vp.n), (stc, stc.n)], inc=[(vp, 1)])
                        v_o0 = vp.n
                        ctx.do("sync", lambda q, p=p, blk=blk, dh=dh: q.dma_start(
                            out=at_dst(2 * p + blk, 0, dh), in_=so[:]),
                            wait=[(vp, v_o0)], inc=[(stc, 16)])
                        ctx.do("scalar", lambda q, blk=blk: q.activation(sx1e[:], px[blk][:], AF.Identity),
                               wait=[(tp, t_suf), (sp, sp.n), (stc, stc.n)], inc=[(sp, 1)])
                        s_x1 = sp.n
                        ctx.do("vector", lambda q, blk=blk: q.tensor_tensor(
                            so[:], sx1e[:], pav[2 * blk + 1][:], op=ALU.subtract),
                            wait=[(sp, s_x1), (vp, vp.n), (stc, stc.n)], inc=[(vp, 1)])
                        v_o1 = vp.n
                        ctx.do("sync", lambda q, p=p, blk=blk, dh=dh: q.dma_start(
                            out=at_dst(2 * p + blk, 1, dh), in_=so[:]),
                            wait=[(vp, v_o1)], inc=[(stc, 16)])
            ctx.wait("gpsimd", stc, stc.n)
            hooks.get("tail", lambda c: None)(ctx)

        prog.build(phase)


def build_L4():
    nc = bass.Bass("TRN2", target_bir_lowering=False)
    kM = nc.dram_tensor("kM", [B, 1024, 8192], BF16, kind="ExternalInput").ap()
    vf = nc.dram_tensor("vf", [B, 8192, 1024], BF16, kind="ExternalInput").ap()
    qTb_in = nc.dram_tensor("qTb_in", [4, 128, 4096], BF16, kind="ExternalInput").ap()
    vdiag_in = nc.dram_tensor("vdiag_in", [4, 128, 4096], BF16, kind="ExternalInput").ap()
    bsJ = nc.dram_tensor("bsJ", [64, 2048], F32, kind="ExternalInput").ap()
    attn1 = nc.dram_tensor("attn1", [8, B, 128, 1024], F32, kind="ExternalOutput").ap()
    io = dict(
        biasT=nc.dram_tensor("biasT", [160, 128, 256], BF16, kind="ExternalInput").ap(),
        cm0=nc.dram_tensor("cm0", [64, 1024], F32, kind="ExternalInput").ap(),
        cm1=nc.dram_tensor("cm1", [64, 1024], F32, kind="ExternalInput").ap(),
        triu_b=nc.dram_tensor("triu_b", [128, 128], BF16, kind="ExternalInput").ap(),
        ntriu_b=nc.dram_tensor("ntriu_b", [128, 128], BF16, kind="ExternalInput").ap(),
        id_bf=nc.dram_tensor("id_bf", [128, 128], BF16, kind="ExternalInput").ap(),
        kM_src=lambda bb, J: kM[bb, :, J * 128:(J + 1) * 128].rearrange("(dt p) j -> p dt j", p=128),
        v_src=lambda bb, J, dh: vf[bb, J * 128:(J + 1) * 128, dh * 512:(dh + 1) * 512],
        at_dst=lambda o, b, dh: attn1[o, b, :, dh * 512:(dh + 1) * 512],
    )

    def emit_pair_loads(ctx, ld, tp, p, sqt, svd):
        ctx.do("sync", lambda q, p=p: q.dma_start(out=sqt[:], in_=qTb_in[p]),
               wait=[(tp, tp.n)], inc=[(ld, 16)])
        ctx.do("sync", lambda q, p=p: q.dma_start(out=svd[:], in_=vdiag_in[p]), inc=[(ld, 16)])

    def emit_bs_load(ctx, ld, sbsj):
        ctx.do("sync", lambda q: q.dma_start(out=sbsj[:], in_=bsJ), inc=[(ld, 16)])

    io["emit_pair_loads"] = emit_pair_loads
    io["emit_bs_load"] = emit_bs_load
    es = ExitStack()
    with es:
        prog = Prog(nc, es)
        prog.prefix = "L4_"
        emit_L4(nc, prog, **io)
    return nc


def run_L4(inp, r3):
    nc = build_L4()
    id_bf = np.eye(128, dtype=BF)
    triu = np.triu(np.ones((128, 128), np.float32))
    triu_b = triu.astype(BF)
    ntriu_b = (np.float32(-1e-9) * triu).astype(BF)
    kM = np.stack([np.concatenate(
        [r3.results[c]["k_sp"][b].reshape(128, 8, 1024).transpose(1, 0, 2).reshape(1024, 1024)
         for c in range(R)], axis=0).reshape(1024, 8192) for b in range(B)])
    vfull = np.stack([np.concatenate(
        [r3.results[c]["v_sp"][b].reshape(128, 8, 1024).transpose(1, 0, 2).reshape(1024, 1024)
         for c in range(R)], axis=0) for b in range(B)])
    qT = np.stack([np.concatenate(
        [r3.results[c]["qTn_sp"][b].reshape(128, 8, 1024) for c in range(R)], axis=2)
        for b in range(B)])
    bsJ_ = np.stack([np.concatenate([r3.results[c]["bs_sp"][b] for c in range(R)], axis=0)
                     for b in range(B)])
    bsJ = np.ascontiguousarray(bsJ_.transpose(1, 0, 2).reshape(64, 2048))
    in_maps = []
    for c in range(R):
        blks = blocks_for_core(c)
        qTb6 = np.zeros((4, 128, 8, 2, 2, 128), BF)
        vdg4 = np.zeros((4, 128, 2, 2, 1024), BF)
        for p in range(4):
            for blk in range(2):
                g = blks[2 * p + blk]
                for b in range(B):
                    qTb6[p, :, :, b, blk, :] = qT[b][:, :, g * 128:(g + 1) * 128]
                    vdg4[p, :, blk, b, :] = vfull[b][g * 128:(g + 1) * 128, :]
        qTb = qTb6.reshape(4, 128, 4096)
        vdg = vdg4.reshape(4, 128, 4096)
        biasT = np.zeros((160, 128, 256), np.float32)
        for p in range(4):
            for J in range(PAIR_STEPS[p]):
                for blk in range(2):
                    g = blks[2 * p + blk]
                    t = biasT[STEP_OFF[p] + J, :, blk * 128:(blk + 1) * 128]
                    if J > g:
                        t[:] = -1e9
                    elif J == g:
                        t[:] = np.where(np.arange(128)[:, None] > np.arange(128)[None, :], -1e9, 0.0)
        cm0_ = np.zeros((8, 64, 128), np.float32)
        cm1_ = np.zeros((8, 64, 128), np.float32)
        for o in range(8):
            g = blks[o]
            cm0_[o, g:, :] = np.float32(1e-9)
            cm1_[o, :g, :] = 1.0
            cm1_[o, g:, :] = np.float32(1e-9)
        cm0 = np.ascontiguousarray(cm0_.transpose(1, 0, 2).reshape(64, 1024))
        cm1 = np.ascontiguousarray(cm1_.transpose(1, 0, 2).reshape(64, 1024))
        in_maps.append(dict(
            kM=kM, vf=vfull,
            qTb_in=qTb, vdiag_in=vdg, bsJ=bsJ,
            biasT=biasT.astype(BF), cm0=cm0, cm1=cm1,
            triu_b=triu_b, ntriu_b=ntriu_b, id_bf=id_bf,
        ))
    return run_bass_kernel_spmd(nc, in_maps, core_ids=list(range(R)))





def emit_L5(nc, prog, *, xn1T_in, gsc_in, x1_in, at_batch, w1, b1h, w2, b2r,
            out_sl, acc, hooks=None):
    hooks = hooks or {}
    es = ExitStack()
    with es:
        T_ = lambda nm, shp, dt: es.enter_context(nc.sbuf_tensor(prog.prefix + nm, shp, dt))
        P_ = lambda nm, shp, dt: es.enter_context(nc.psum_tensor(prog.prefix + nm, shp, dt))
        xn1T = T_("xn1T", [128, 16 * 1024], BF16)
        gsc = T_("gsc", [128, 16 * 8], F32)
        w1sb = T_("w1sb", [128, 8 * 4096], BF16)
        w2sb = T_("w2sb", [128, 32 * 1024], BF16)
        sb1 = T_("sb1", [128, 8 * 32], F32)
        sb2 = T_("sb2", [1, 8 * 1024], BF16)
        ones1 = T_("ones1", [1, 128], BF16)
        hT = T_("hT", [128, 32 * 256], BF16)
        sa = T_("sa", [128, 1024], F32)
        sstg = T_("sstg", [128, 2048], F32)
        gy = [T_("gy0", [128, 512], F32)]

        ph = [P_(f"ph{i}", [128, 256], F32) for i in range(2)]
        py = [P_(f"py{i}", [128, 512], F32) for i in range(4)]

        xn1T3 = xn1T[:].rearrange("p (g c) -> p g c", c=1024)
        w1sb3 = w1sb[:].rearrange("p (dt f) -> p dt f", f=4096)
        w2sb3 = w2sb[:].rearrange("p (ft d) -> p ft d", d=1024)
        hT3 = hT[:].rearrange("p (ft t) -> p ft t", t=256)

        def phase(ctx: Ctx):
            hooks.get("head", lambda c: None)(ctx)
            ld = ctx.sem("ld")
            lw = ctx.sem("lw")
            tp = ctx.sem("tp")
            vp = ctx.sem("vp")
            sp = ctx.sem("sp")
            stc = ctx.sem("stc")

            ctx.do("sync", lambda q: q.dma_start(out=xn1T[:], in_=xn1T_in), inc=[(ld, 16)])
            ctx.do("sync", lambda q: q.dma_start(out=gsc[:], in_=gsc_in), inc=[(ld, 16)])
            ctx.do("sync", lambda q: q.dma_start(out=sb1[:], in_=b1h), inc=[(ld, 16)])
            ctx.do("gpsimd", lambda q: q.dma_start(out=sb2[:], in_=b2r), inc=[(ld, 16)])
            ctx.do("vector", lambda q: q.memset(ones1[:], 1.0), inc=[(vp, 1)])
            ld_c = ld.n
            v_ones = vp.n
            sstg3 = sstg[:].rearrange("p (l c) -> p l c", c=1024)
            for b in range(B):
                for half in range(4):
                    _eng = "sync" if (b * 4 + half) % 2 == 0 else "scalar"
                    ctx.do(_eng, lambda q, b=b, half=half: q.dma_start(
                        out=sstg3, in_=at_batch(q, b, half)),
                        wait=[(stc, stc.n)], inc=[(ld, 16)])
                    ld_h = ld.n
                    for li in range(2):
                        tt = half * 2 + li
                        ctx.do("sync", lambda q, b=b, tt=tt: q.dma_start(
                            out=sa[:], in_=x1_in[b, tt * 128:(tt + 1) * 128, :]),
                            wait=[(stc, stc.n)], inc=[(ld, 16)])
                        ld_i = ld.n
                        ctx.do("vector", lambda q, li=li: q.tensor_tensor(
                            sa[:], sa[:], sstg3[:, li, :], op=ALU.add),
                            wait=[(ld, max(ld_i, ld_h)), (vp, vp.n)], inc=[(vp, 1)])
                        ctx.do("sync", lambda q, b=b, tt=tt: q.dma_start(
                            out=acc[(b * 8 + tt) * 128:(b * 8 + tt + 1) * 128, :], in_=sa[:]),
                            wait=[(vp, vp.n)], inc=[(stc, 16)])
            reg_last = {i: stc.n for i in range(16)}
            gy_free = {}
            for e in range(8):
                ctx.do("gpsimd", lambda q, e=e: q.dma_start(
                    out=w1sb3, in_=w1[e].rearrange("(dt p) f -> p dt f", p=128)),
                    wait=[(tp, tp.n), (ld, ld_c)], inc=[(lw, 16)])
                lw_w1 = lw.n
                ctx.do("gpsimd", lambda q, e=e: q.dma_start(
                    out=w2sb3, in_=w2[e].rearrange("(ft p) d -> p ft d", p=128)),
                    inc=[(lw, 16)])
                lw_e = lw.n
                for blk in range(8):
                    b, w_ = divmod(blk, 4)
                    t0 = w_ * 256
                    gel_done = {}
                    for ft in range(32):
                        def hmm(q, e=e, b=b, t0=t0, ft=ft):
                            for ct in range(8):
                                ins = q.matmul(ph[ft % 2][:], w1sb3[:, ct, ft * 128:(ft + 1) * 128],
                                               xn1T3[:, b * 8 + ct, t0:t0 + 256],
                                               start=(ct == 0), stop=(ct == 7))
                            return ins
                        wh = [(lw, lw_w1)]
                        if ft - 2 in gel_done:
                            wh.append((sp, gel_done[ft - 2]))
                        elif ft < 2:
                            wh.append((sp, sp.n))
                        ctx.do("tensor", hmm, wait=wh, inc=[(tp, 1)])
                        t_h = tp.n
                        ctx.do("scalar", lambda q, e=e, ft=ft: q.activation(
                            hT3[:, ft, :], ph[ft % 2][:], AF.Gelu, bias=sb1[:, e * 32 + ft:e * 32 + ft + 1]),
                            wait=[(tp, t_h), (sp, sp.n)], inc=[(sp, 1)])
                        gel_done[ft] = sp.n
                    s_h = sp.n
                    for ts_ in range(2):
                        for dc in range(2):
                            pidx = ts_ * 2 + dc
                            def ymm(q, e=e, ts_=ts_, dc=dc, pidx=pidx):
                                q.matmul(py[pidx][:], ones1[:], sb2[0:1, e * 1024 + dc * 512:e * 1024 + (dc + 1) * 512],
                                         start=True, stop=False)
                                for ft in range(32):
                                    ins = q.matmul(py[pidx][:], hT3[:, ft, ts_ * 128:(ts_ + 1) * 128],
                                                   w2sb3[:, ft, dc * 512:(dc + 1) * 512],
                                                   start=False, stop=(ft == 31))
                                return ins
                            wy = [(sp, s_h), (vp, vp.n), (lw, lw_e)]
                            ctx.do("tensor", ymm, wait=wy, inc=[(tp, 1)])
                            t_y = tp.n
                            tt = w_ * 2 + ts_
                            gcol = (b * 8 + tt) * 8 + e
                            gidx = 0
                            wv_ = [(tp, t_y), (vp, vp.n)]
                            if gidx in gy_free:
                                wv_.append((stc, gy_free[gidx]))
                            ctx.do("vector", lambda q, gidx=gidx, pidx=pidx, gcol=gcol: q.tensor_scalar(
                                gy[gidx][:], py[pidx][:], gsc[:, gcol:gcol + 1], None, op0=ALU.mult),
                                wait=wv_, inc=[(vp, 1)])
                            v_gy = vp.n
                            reg = b * 8 + tt
                            ctx.do("gpsimd", lambda q, gidx=gidx, reg=reg, dc=dc: q.dma_start(
                                out=acc[reg * 128:(reg + 1) * 128, dc * 512:(dc + 1) * 512],
                                in_=gy[gidx][:], accum_op=ALU.add),
                                wait=[(vp, v_gy), (stc, reg_last[reg])], inc=[(stc, 16)])
                            reg_last[reg] = stc.n
                            gy_free[gidx] = stc.n
            fin = max(reg_last.values())
            ctx.do("sync", lambda q: q.dma_start(
                out=out_sl.rearrange("b t c -> (b t) c"), in_=acc), wait=[(stc, fin)], inc=[(stc, 16)])
            ctx.wait("gpsimd", stc, stc.n)
            hooks.get("tail", lambda c: None)(ctx)

        prog.build(phase)


def build_L5():
    nc = bass.Bass("TRN2", target_bir_lowering=False)
    at_in = nc.dram_tensor("at_in", [B, TC, C], F32, kind="ExternalInput").ap()
    io = dict(
        xn1T_in=nc.dram_tensor("xn1T_in", [128, 16 * 1024], BF16, kind="ExternalInput").ap(),
        gsc_in=nc.dram_tensor("gsc_in", [128, 16 * 8], F32, kind="ExternalInput").ap(),
        x1_in=nc.dram_tensor("x1_in", [B, TC, C], F32, kind="ExternalInput").ap(),
        w1=nc.dram_tensor("w1", [8, C, 4096], F32, kind="ExternalInput").ap(),
        b1h=nc.dram_tensor("b1h", [128, 8 * 32], F32, kind="ExternalInput").ap(),
        w2=nc.dram_tensor("w2", [8, 4096, C], F32, kind="ExternalInput").ap(),
        b2r=nc.dram_tensor("b2r", [1, 8 * 1024], F32, kind="ExternalInput").ap(),
        out_sl=nc.dram_tensor("out_sl", [B, TC, C], F32, kind="ExternalOutput").ap(),
        acc=nc.dram_tensor("acc", [B * TC, C], F32).ap(),
        at_batch=lambda q, b, half: at_in[b, half * 256:(half + 1) * 256, :]
            .rearrange("(l p) c -> p l c", p=128),
    )
    es = ExitStack()
    with es:
        prog = Prog(nc, es)
        prog.prefix = "L5_"
        emit_L5(nc, prog, **io)
    return nc


def run_L5(inp, r2b, r3, r4):
    nc = build_L5()
    at_full = np.zeros((B, T, C), np.float32)
    for cp in range(R):
        blks = blocks_for_core(cp)
        for o, g in enumerate(blks):
            for b in range(B):
                at_full[b, g * 128:(g + 1) * 128] = r4.results[cp]["attn1"][o, b]
    b1h = np.ascontiguousarray(inp["L1_ffn_b1"].reshape(8, 32, 128).transpose(2, 0, 1).reshape(128, 256))
    b2r = np.ascontiguousarray(inp["L1_ffn_b2"].reshape(1, 8 * 1024))
    in_maps = []
    for c in range(R):
        in_maps.append(dict(
            xn1T_in=r2b.results[c]["xn1T_sp"],
            gsc_in=r3.results[c]["gsc_sp"],
            x1_in=r2b.results[c]["x1_out"],
            at_in=np.ascontiguousarray(at_full[:, c * TC:(c + 1) * TC, :]),
            w1=inp["L1_ffn_w1"], b1h=b1h, w2=inp["L1_ffn_w2"], b2r=b2r,
        ))
    return run_bass_kernel_spmd(nc, in_maps, core_ids=list(range(R)))


def kernel_impl(inputs):
    r1 = run_L1(inputs)
    r2a = run_L2a(inputs, r1)
    r2b = run_L2b(inputs, r1, r2a)
    r3 = run_L3(inputs, r2b)
    r4 = run_L4(inputs, r3)
    r5 = run_L5(inputs, r2b, r3, r4)
    out = np.zeros((B, T, C), np.float32)
    for c in range(R):
        out[:, c * TC:(c + 1) * TC, :] = r5.results[c]["out_sl"]
    return out





def emit_L3T(nc, prog, *, k_in, kT_out, hooks=None):
    hooks = hooks or {}

    def phase(ctx):
        hooks.get("head", lambda c: None)(ctx)
        stc = ctx.sem("stc")
        src5 = k_in.rearrange("(b2 u s) (tt d) -> b2 s tt u d", b2=2, s=8, d=1024)
        dst5 = kT_out.rearrange("(b2 tt u) (s d) -> b2 s tt u d", b2=2, u=16, d=1024)
        for b in range(2):
            for s in range(8):
                ctx.do("sync", lambda q, b=b, s=s: q.dma_start(
                    out=dst5[b, s], in_=src5[b, s]), inc=[(stc, 16)])
        ctx.wait("gpsimd", stc, stc.n)
        hooks.get("tail", lambda c: None)(ctx)

    prog.build(phase)


def build_FUSED():
    nc = bass.Bass("TRN2", target_bir_lowering=False)
    ET = mybir.EngineType

    xin = dict(
        x_sl=nc.dram_tensor("x_sl", [B, TC, C], F32, kind="ExternalInput").ap(),
        x_halo=nc.dram_tensor("x_halo", [B, C], F32, kind="ExternalInput").ap(),
        w_qkv=nc.dram_tensor("w_qkv", [3 * C, C], F32, kind="ExternalInput").ap(),
        wk_map=nc.dram_tensor("wk_map", [64, 64], F32, kind="ExternalInput").ap(),
        bk_map=nc.dram_tensor("bk_map", [1, 64], F32, kind="ExternalInput").ap(),
        wq_map=nc.dram_tensor("wq_map", [64, 64], F32, kind="ExternalInput").ap(),
        bq_map=nc.dram_tensor("bq_map", [1, 64], F32, kind="ExternalInput").ap(),
        id_bf=nc.dram_tensor("id_bf", [128, 128], BF16, kind="ExternalInput").ap(),
        id_f=nc.dram_tensor("id_f", [128, 128], F32, kind="ExternalInput").ap(),
        triu_f=nc.dram_tensor("triu_f", [128, 128], F32, kind="ExternalInput").ap(),
        triu_b=nc.dram_tensor("triu_b", [128, 128], BF16, kind="ExternalInput").ap(),
        ntriu_b=nc.dram_tensor("ntriu_b", [128, 128], BF16, kind="ExternalInput").ap(),
        prefw=nc.dram_tensor("prefw", [128, 8], F32, kind="ExternalInput").ap(),
        W_r=nc.dram_tensor("W_r", [C, C], F32, kind="ExternalInput").ap(),
        W_k=nc.dram_tensor("W_k", [C, C], F32, kind="ExternalInput").ap(),
        W_v=nc.dram_tensor("W_v", [C, C], F32, kind="ExternalInput").ap(),
        w_out=nc.dram_tensor("w_out", [C, C], F32, kind="ExternalInput").ap(),
        mu_r=nc.dram_tensor("mu_r", [128, 8], F32, kind="ExternalInput").ap(),
        mu_k=nc.dram_tensor("mu_k", [128, 8], F32, kind="ExternalInput").ap(),
        Wq=nc.dram_tensor("Wq", [C, C], F32, kind="ExternalInput").ap(),
        Wk=nc.dram_tensor("Wk", [C, C], F32, kind="ExternalInput").ap(),
        Wv=nc.dram_tensor("Wv", [C, C], F32, kind="ExternalInput").ap(),
        w_red=nc.dram_tensor("w_red", [C, 16], F32, kind="ExternalInput").ap(),
        wg=nc.dram_tensor("wg", [8, 16], F32, kind="ExternalInput").ap(),
        biasT=nc.dram_tensor("biasT", [160, 128, 256], BF16, kind="ExternalInput").ap(),
        cm0=nc.dram_tensor("cm0", [64, 1024], F32, kind="ExternalInput").ap(),
        cm1=nc.dram_tensor("cm1", [64, 1024], F32, kind="ExternalInput").ap(),
        w1=nc.dram_tensor("w1", [8, C, 4096], F32, kind="ExternalInput").ap(),
        b1h=nc.dram_tensor("b1h", [128, 8 * 32], F32, kind="ExternalInput").ap(),
        w2=nc.dram_tensor("w2", [8, 4096, C], F32, kind="ExternalInput").ap(),
        b2r=nc.dram_tensor("b2r", [1, 8 * 1024], F32, kind="ExternalInput").ap(),
    )
    out_sl = nc.dram_tensor("out_sl", [B, TC, C], F32, kind="ExternalOutput").ap()

    xnT_sp = nc.dram_tensor("xnT_sp", [128, 16 * 1025], BF16).ap()
    kf_sp = nc.dram_tensor("kf_sp", [128, 16 * 2048], BF16).ap()
    v_sp = nc.dram_tensor("v_sp", [128, 16 * 1024], BF16).ap()
    hx_scr = nc.dram_tensor("hx_scr", [B, 1024], BF16).ap()
    st_in = nc.dram_tensor("st_in", [128, 32 * 65], F32)
    st_out = nc.dram_tensor("st_out", [8 * 128, 32 * 65], F32)
    y_sp = nc.dram_tensor("y_sp", [B, 128, 8 * 1024], BF16).ap()
    x1_sp = nc.dram_tensor("x1_sp", [B, TC, C], F32).ap()
    xn1T_sp = nc.dram_tensor("xn1T_sp", [128, 16 * 1024], BF16).ap()
    gsc_sp = nc.dram_tensor("gsc_sp", [128, 16 * 8], F32).ap()
    k_scr = nc.dram_tensor("k_scr", [256, 8192], BF16)
    ag1_in = nc.dram_tensor("ag1_in", [512, 8192], BF16)
    ag1_out = nc.dram_tensor("ag1_out", [8 * 512, 8192], BF16)
    agq_in = nc.dram_tensor("agq_in", [128, 16384], BF16)
    agq_out = nc.dram_tensor("agq_out", [8 * 128, 16384], BF16)
    agbs_in = nc.dram_tensor("agbs_in", [16, 1024], F32)
    agbs_out = nc.dram_tensor("agbs_out", [128, 1024], F32)
    ag3_in = nc.dram_tensor("ag3_in", [2048, 1024], F32)
    ag3_out = nc.dram_tensor("ag3_out", [8 * 2048, 1024], F32)
    acc = nc.dram_tensor("acc", [B * TC, C], F32).ap()

    RG = [list(range(R))]

    es = ExitStack()
    with es:
        bars = [es.enter_context(nc.semaphore(f"bar{k}")) for k in range(7)]
        ccs = [es.enter_context(nc.semaphore(f"cc{k}")) for k in range(3)]
        prog = Prog(nc, es)

        def head(k):
            def h(ctx):
                for e in ENGINES:
                    ctx.wait_h(e, bars[k], 1)
            return h

        def bar_inc(ctx, k):
            ctx.on("gpsimd", lambda q: q.nop(), inc=[(bars[k], 1)])

        def tail1(ctx):
            ctx.on("gpsimd", lambda q: q.collective_compute(
                "AllGather", ALU.bypass, replica_groups=RG,
                ins=[st_in.ap().opt()], outs=[st_out.ap().opt()]), inc=[(ccs[0], 1)])
            ctx.wait_h("gpsimd", ccs[0], 1)
            bar_inc(ctx, 1)
        prog.prefix = "L1_"
        emit_L1(nc, prog, x_sl=xin["x_sl"], x_halo=xin["x_halo"], w_qkv=xin["w_qkv"],
                wk_map=xin["wk_map"], bk_map=xin["bk_map"], id_bf=xin["id_bf"],
                xnT_sp=xnT_sp, kf_sp=kf_sp, v_sp=v_sp, states=st_in.ap(), hx_scr=hx_scr,
                hooks={"tail": tail1})

        prog.prefix = "L2a_"
        emit_L2a(nc, prog, xnT_in=xnT_sp, kf_in=kf_sp, v_in=v_sp,
                 states_r=lambda r_: st_out.ap()[128 * r_:128 * (r_ + 1), :],
                 prefw=xin["prefw"], w_qkv=xin["w_qkv"], wq_map=xin["wq_map"],
                 bq_map=xin["bq_map"], id_bf=xin["id_bf"], triu_f=xin["triu_f"], y_sp=y_sp,
                 hooks={"head": head(1), "tail": lambda c: bar_inc(c, 2)})

        prog.prefix = "L2b_"
        emit_L2b(nc, prog, xnT_in=xnT_sp, y_in=y_sp, x_sl=xin["x_sl"], W_r=xin["W_r"],
                 W_k=xin["W_k"], W_v=xin["W_v"], w_out=xin["w_out"], mu_r=xin["mu_r"],
                 mu_k=xin["mu_k"], id_bf=xin["id_bf"], x1_out=x1_sp, xn1T_sp=xn1T_sp,
                 hooks={"head": head(2), "tail": lambda c: bar_inc(c, 3)})

        def tail3(ctx):
            ctx.on("gpsimd", lambda q: q.collective_compute(
                "AllGather", ALU.bypass, replica_groups=RG,
                ins=[agq_in.ap().opt()], outs=[agq_out.ap().opt()]), inc=[(ccs[1], 1)])
            ctx.on("gpsimd", lambda q: q.collective_compute(
                "AllGather", ALU.bypass, replica_groups=RG,
                ins=[agbs_in.ap().opt()], outs=[agbs_out.ap().opt()]), inc=[(ccs[1], 1)])
            ctx.wait_h("gpsimd", ccs[1], 2)
            bar_inc(ctx, 4)
        prog.prefix = "L3_"
        emit_L3(nc, prog, xn1T_in=xn1T_sp, Wq=xin["Wq"], Wk=xin["Wk"], Wv=xin["Wv"],
                w_red=xin["w_red"], wg=xin["wg"], id_bf=xin["id_bf"], id_f=xin["id_f"],
                qT_out=lambda b: agq_in.ap().rearrange("p (dt b t) -> p dt b t", b=2, t=1024)[:, :, b, :],
                k_out=lambda b: k_scr.ap()[b * 128:(b + 1) * 128, :],
                v_out=lambda b: ag1_in.ap()[256 + b * 128:256 + (b + 1) * 128, :],
                bs_out=lambda b: agbs_in.ap()[b * 8:(b + 1) * 8, :],
                gsc_sp=gsc_sp,
                hooks={"head": head(3), "tail": tail3})

        def tailT(ctx):
            ctx.on("gpsimd", lambda q: q.collective_compute(
                "AllGather", ALU.bypass, replica_groups=RG,
                ins=[ag1_in.ap().opt()], outs=[ag1_out.ap().opt()]), inc=[(ccs[1], 1)])
            ctx.wait_h("gpsimd", ccs[1], 3)
            bar_inc(ctx, 6)
        prog.prefix = "L3T_"
        emit_L3T(nc, prog, k_in=k_scr.ap(), kT_out=ag1_in.ap()[0:256, :],
                 hooks={"head": head(4), "tail": tailT})

        ag1o = ag1_out.ap()
        agqo = agq_out.ap()
        ag1o_r = ag1o.rearrange("(ci r) j -> ci r j", r=512)

        def kM_src(bb, J):
            return ag1o_r[:, bb * 128:(bb + 1) * 128, J * 128:(J + 1) * 128].transpose([1, 0, 2])

        def v_src(bb, J, dh):
            return ag1o[512 * (J // 8) + 256 + bb * 128:512 * (J // 8) + 256 + (bb + 1) * 128,
                        (J % 8) * 1024 + dh * 512:(J % 8) * 1024 + (dh + 1) * 512]

        def at_dst(o, b, dh):
            return ag3_in.ap()[(o * 2 + b) * 128:(o * 2 + b + 1) * 128, dh * 512:(dh + 1) * 512]

        _ETMAP = {"sync": ET.SP, "scalar": ET.Activation, "gpsimd": ET.Pool}

        def _pid(ctx):
            if not hasattr(ctx, "_pid"):
                p0 = nc.partition_id([_ETMAP[ctx.eng]])
                ctx._pid = ctx.q.snap(p0, min_val=0, max_val=7)
                ctx._offs = {128: ctx._pid * 128, 1024: ctx._pid * 1024}
            return ctx._pid

        def emit_pair_loads(ctx, ld, tp, p, sqt, svd):
            engq = "sync" if p < 2 else "scalar"
            first = [True]
            def dma_one(dst, src_base, moff_elems):
                def f(q, dst=dst, src_base=src_base, moff_elems=moff_elems):
                    sym = ap_replace(src_base, offset=src_base.offset + moff_elems)
                    return q.dma_start(out=dst, in_=sym)
                w = [(tp, tp.n)] if first[0] else []
                first[0] = False
                ctx.do(engq, f, wait=w, inc=[(ld, 16)])
            for blk in range(2):
                o = 2 * p + blk
                if ctx.eng == engq:
                    _pid(ctx)
                    base = agqo[128 * o:128 * (o + 1), :]                         .rearrange("p (db t) -> p db t", t=1024)[:, :, 0:128]
                    moff = ctx._offs[128]
                else:
                    base = None
                    moff = 0
                dst = sqt[:].rearrange("p (db r) -> p db r", r=256)[
                    :, :, blk * 128:(blk + 1) * 128]
                dma_one(dst, base, moff)
            for blk in range(2):
                o = 2 * p + blk
                if ctx.eng == engq:
                    _pid(ctx)
                    base = ag1o[512 * o + 256:512 * o + 512, 0:1024].rearrange("(bb p) d -> p bb d", p=128)
                    moff = ctx._offs[1024]
                else:
                    base = None
                    moff = 0
                dst = svd[:].rearrange("p (k b d) -> p k b d", k=2, b=2)[:, blk, :, :]
                dma_one(dst, base, moff)

        def emit_bs_load(ctx, ld, sbsj):
            for cpr in range(8):
                for b in range(2):
                    ctx.do("sync", lambda q, cpr=cpr, b=b: q.dma_start(
                        out=sbsj[8 * cpr:8 * cpr + 8, b * 1024:(b + 1) * 1024],
                        in_=agbs_out.ap()[16 * cpr + 8 * b:16 * cpr + 8 * b + 8, :]),
                        inc=[(ld, 16)])

        def tail4(ctx):
            ctx.on("gpsimd", lambda q: q.collective_compute(
                "AllGather", ALU.bypass, replica_groups=RG,
                ins=[ag3_in.ap().opt()], outs=[ag3_out.ap().opt()]), inc=[(ccs[2], 1)])
            ctx.wait_h("gpsimd", ccs[2], 1)
            bar_inc(ctx, 5)
        prog.prefix = "L4_"
        emit_L4(nc, prog, kM_src=kM_src, v_src=v_src, emit_pair_loads=emit_pair_loads,
                emit_bs_load=emit_bs_load, at_dst=at_dst, biasT=xin["biasT"],
                cm0=xin["cm0"], cm1=xin["cm1"], triu_b=xin["triu_b"],
                ntriu_b=xin["ntriu_b"], id_bf=xin["id_bf"],
                hooks={"head": head(6), "tail": tail4})

        _l5c = {}

        def at_batch(q, b, half):
            if _l5c.get("q") is not q:
                p0 = nc.partition_id([q.engine])
                _l5c["pid"] = q.snap(p0, min_val=0, max_val=7)
                _l5c["q"] = q
            pid = _l5c["pid"]
            base = ag3_out.ap().rearrange("(l r) d -> l r d", r=2048)[
                2 * half:2 * half + 2, b * 128:(b + 1) * 128, :].transpose([1, 0, 2])
            off = pid * 256 * 1024
            return ap_replace(base, offset=base.offset + off)
        prog.prefix = "L5_"
        emit_L5(nc, prog, xn1T_in=xn1T_sp, gsc_in=gsc_sp, x1_in=x1_sp, at_batch=at_batch,
                w1=xin["w1"], b1h=xin["b1h"], w2=xin["w2"], b2r=xin["b2r"],
                out_sl=out_sl, acc=acc,
                hooks={"head": head(5)})
    return nc


def kernel_fused(inp):
    nc = build_FUSED()
    xs, halos = _split_inputs(inp["L0_attn_x"])
    id_bf = np.eye(128, dtype=BF)
    id_f = np.eye(128, dtype=np.float32)
    triu = np.triu(np.ones((128, 128), np.float32))
    mu_r = np.ascontiguousarray(inp["L0_ffn_mu_r"].reshape(8, 128).T)
    mu_k = np.ascontiguousarray(inp["L0_ffn_mu_k"].reshape(8, 128).T)
    b1h = np.ascontiguousarray(inp["L1_ffn_b1"].reshape(8, 32, 128).transpose(2, 0, 1).reshape(128, 256))
    b2r = np.ascontiguousarray(inp["L1_ffn_b2"].reshape(1, 8 * 1024))
    in_maps = []
    for c in range(R):
        blks = blocks_for_core(c)
        pw_ = np.zeros((128, 8), np.float32)
        pw_[:, :c] = 1.0
        biasT = np.zeros((160, 128, 256), np.float32)
        for p in range(4):
            for J in range(PAIR_STEPS[p]):
                for blk in range(2):
                    g = blks[2 * p + blk]
                    t = biasT[STEP_OFF[p] + J, :, blk * 128:(blk + 1) * 128]
                    if J > g:
                        t[:] = -1e9
                    elif J == g:
                        t[:] = np.where(np.arange(128)[:, None] > np.arange(128)[None, :], -1e9, 0.0)
        cm0_ = np.zeros((8, 64, 128), np.float32)
        cm1_ = np.zeros((8, 64, 128), np.float32)
        for o in range(8):
            g = blks[o]
            cm0_[o, g:, :] = np.float32(1e-9)
            cm1_[o, :g, :] = 1.0
            cm1_[o, g:, :] = np.float32(1e-9)
        in_maps.append(dict(
            x_sl=xs[c], x_halo=halos[c],
            w_qkv=inp["L0_attn_w_qkv"],
            wk_map=inp["L0_attn_wk_map"],
            bk_map=np.ascontiguousarray(inp["L0_attn_bk_map"].reshape(1, 64)),
            wq_map=inp["L0_attn_wq_map"],
            bq_map=np.ascontiguousarray(inp["L0_attn_bq_map"].reshape(1, 64)),
            id_bf=id_bf, id_f=id_f, triu_f=triu,
            triu_b=triu.astype(BF), ntriu_b=(np.float32(-1e-9) * triu).astype(BF),
            prefw=pw_,
            W_r=inp["L0_ffn_W_r"], W_k=inp["L0_ffn_W_k"], W_v=inp["L0_ffn_W_v"],
            w_out=inp["L0_attn_w_out"], mu_r=mu_r, mu_k=mu_k,
            Wq=inp["L1_attn_Wq"], Wk=inp["L1_attn_Wk"], Wv=inp["L1_attn_Wv"],
            w_red=inp["L1_ffn_w_red"], wg=inp["L1_ffn_wg"],
            biasT=np.ascontiguousarray(biasT.astype(BF)),
            cm0=np.ascontiguousarray(cm0_.transpose(1, 0, 2).reshape(64, 1024)),
            cm1=np.ascontiguousarray(cm1_.transpose(1, 0, 2).reshape(64, 1024)),
            w1=inp["L1_ffn_w1"], b1h=b1h, w2=inp["L1_ffn_w2"], b2r=b2r,
        ))
    res = run_bass_kernel_spmd(nc, in_maps, core_ids=list(range(R)))
    out = np.zeros((B, T, C), np.float32)
    for c in range(R):
        out[:, c * TC:(c + 1) * TC, :] = res.results[c]["out_sl"]
    return out


def kernel(**inputs):
    import os as _os
    if _os.environ.get("KERNEL_UNFUSED"):
        return kernel_impl(inputs)
    return kernel_fused(inputs)


